# version 06
# baseline (speedup 1.0000x reference)
import numpy as np
import ml_dtypes
from contextlib import ExitStack

import concourse.bass as bass
import concourse.mybir as mybir
from concourse.bass_utils import run_bass_kernel_spmd

F32 = mybir.dt.float32
BF16 = mybir.dt.bfloat16
AF = mybir.ActivationFunctionType
ALU = mybir.AluOpType
AX = mybir.AxisListType

B, L, D, DI, NST, RNK, HF, E = 4, 4096, 1024, 2048, 16, 64, 4096, 8
TOK = L // 2
NCH = 8
TC = L // NCH
EPS = 1e-5
BF16NP = ml_dtypes.bfloat16


class Sem:
    def __init__(self, nc, ctx, name):
        self.h = ctx.enter_context(nc.semaphore(name))
        self.n = 0

    def inc(self, instr, k=1):
        self.n += k
        return instr.then_inc(self.h, k)

    def wait(self, eng, at=None):
        eng.wait_ge(self.h, self.n if at is None else at)


class MK:
    def __init__(self, upto=99):
        self.upto = upto
        nc = bass.Bass("TRN2", target_bir_lowering=False)
        self.nc = nc
        dt = nc.dram_tensor
        ei, eo, it = "ExternalInput", "ExternalOutput", "Internal"
        self.x_full = dt("x_full", [L, D], F32, kind=ei).ap()
        self.x_own = dt("x_own", [TOK, D], F32, kind=ei).ap()
        self.ln_g = dt("ln_g", [128, 8], F32, kind=ei).ap()
        self.ln_b = dt("ln_b", [128, 8], F32, kind=ei).ap()
        self.w_in = dt("w_in", [D, 2 * DI], F32, kind=ei).ap()
        self.conv_w = dt("conv_w", [128, 64], F32, kind=ei).ap()
        self.conv_b = dt("conv_b", [128, 16], F32, kind=ei).ap()
        self.w_xp = dt("w_xp", [DI, 96], F32, kind=ei).ap()
        self.w_dt = dt("w_dt", [RNK, DI], F32, kind=ei).ap()
        self.dt_bias = dt("dt_bias", [128, 16], F32, kind=ei).ap()
        self.a_log = dt("a_log", [128, 256], F32, kind=ei).ap()
        self.d_skip = dt("d_skip", [128, 16], F32, kind=ei).ap()
        self.w_out = dt("w_out", [DI, D], F32, kind=ei).ap()
        self.wg = dt("wg", [D, E], F32, kind=ei).ap()
        self.bg_bc = dt("bg_bc", [128, E], F32, kind=ei).ap()
        self.w1 = dt("w1", [E, D, HF], F32, kind=ei).ap()
        self.w2 = dt("w2", [E, HF, D], F32, kind=ei).ap()
        self.w3 = dt("w3", [E, D, HF], F32, kind=ei).ap()
        self.ident = dt("ident", [128, 128], F32, kind=ei).ap()
        self.ones_f = dt("ones_f", [1, 128], F32, kind=ei).ap()
        self.out_c0 = dt("out_c0", [TOK, D], F32, kind=eo).ap()
        self.out_c1 = dt("out_c1", [TOK, D], F32, kind=eo).ap()
        self.xnT_s = dt("xnT_s", [D, L], F32, kind=it).ap()
        self.xnTo_s = dt("xnTo_s", [D, TOK], BF16, kind=it).ap()
        self.xiT_s = dt("xiT_s", [DI, L], F32, kind=it).ap()
        self.szT_s = dt("szT_s", [DI, L], BF16, kind=it).ap()
        self.xcT_s = dt("xcT_s", [DI, L], F32, kind=it).ap()
        self.xcdT_s = dt("xcdT_s", [DI, L], BF16, kind=it).ap()
        self.dltT_s = dt("dltT_s", [DI, L], BF16, kind=it).ap()
        self.bmcm_s = dt("bmcm_s", [2 * NST, L], F32, kind=it).ap()
        self.sT_s = dt("sT_s", [D, L], F32, kind=it).ap()
        self.g_s = dt("g_s", [D, TOK], F32, kind=it).ap()
        self.xse_s = dt("xse_s", [128, 8192], BF16, kind=it).ap()
        self.h1_s = dt("h1_s", [128, 32768], BF16, kind=it).ap()
        self.w1q_s = dt("w1q_s", [128, 4096], BF16, kind=it).ap()
        self.gsnap_s = dt("gsnap_s", [8 * D, TOK], F32, kind=it).ap()
        self.y2snap_s = dt("y2snap_s", [8 * D, TOK], F32, kind=it).ap()
        self.wt_s = dt("wt_s", [E, TOK], F32, kind=it).ap()
        self.dbg = None
        self.dbg_src = None

    def add_debug(self, *specs):
        self.dbg = []
        for i, (name, rows, cols) in enumerate(specs):
            ap = getattr(self, name)
            self.dbg.append((self.nc.dram_tensor(
                f"dbg{i}", [rows, cols], ap.dtype, kind="ExternalOutput").ap(),
                name, rows, cols))

    def build(self):
        nc = self.nc
        with ExitStack() as ctx:
            self.ld = Sem(nc, ctx, "ld")
            self.gd = Sem(nc, ctx, "gd")
            self.mm = Sem(nc, ctx, "mm")
            self.vv = Sem(nc, ctx, "vv")
            self.aa = Sem(nc, ctx, "aa")
            self.so = Sem(nc, ctx, "so")
            self.wos = [Sem(nc, ctx, f"wos{i}") for i in range(2)]
            self.wqs = [Sem(nc, ctx, f"wqs{i}") for i in range(2)]
            self.w2s = [Sem(nc, ctx, f"w2s{i}") for i in range(4)]
            self.gis = [Sem(nc, ctx, f"gis{i}") for i in range(2)]
            self.brs = [Sem(nc, ctx, f"brs{i}") for i in range(2)]
            self.pp = Sem(nc, ctx, "pp")
            sb = lambda name, shape, dtp=F32: ctx.enter_context(
                nc.sbuf_tensor(name, shape, dtp))
            self.ident_sb = sb("ident_sb", [128, 128])
            self.ones_sb = sb("ones_sb", [1, 128])
            self.stats = sb("stats", [128, 32 * 4])
            self.stats_o = sb("stats_o", [128, 16 * 2])
            self.lng_sb = sb("lng_sb", [128, 8])
            self.lnb_sb = sb("lnb_sb", [128, 8])
            self.cw_sb = sb("cw_sb", [128, 16 * 4])
            self.cb_sb = sb("cb_sb", [128, 16])
            self.dtb_sb = sb("dtb_sb", [128, 16])
            self.dsk_sb = sb("dsk_sb", [128, 16])
            self.a_sb = sb("a_sb", [128, 16 * NST])
            self.cst = sb("cst", [128, 8])

            self.ph_consts()
            if self.upto >= 1:
                self.ph_a0()
            if self.upto >= 2:
                self.ph_a()
            if self.upto >= 3:
                self.ph_a_own()
            if self.upto >= 4:
                self.ph_b()
            if self.upto >= 5:
                self.ph_s()
            if self.upto >= 6:
                self.ph_moe()
            if self.upto >= 7:
                self.ph_f()
            self.ph_fini()
        return nc

    def _phase_fence(self):
        nc, so = self.nc, self.so
        with nc.Block() as blk:
            @blk.sync
            def _(e):
                so.wait(e)

    def ph_consts(self):
        nc, ld, aa, vv = self.nc, self.ld, self.aa, self.vv
        with nc.Block() as blk:
            @blk.sync
            def _(e):
                ld.inc(e.dma_start(out=self.ident_sb[:], in_=self.ident), 16)
                ld.inc(e.dma_start(out=self.ones_sb[:], in_=self.ones_f), 16)
                ld.inc(e.dma_start(out=self.lng_sb[:], in_=self.ln_g), 16)
                ld.inc(e.dma_start(out=self.lnb_sb[:], in_=self.ln_b), 16)
                ld.inc(e.dma_start(out=self.cw_sb[:], in_=self.conv_w), 16)
                ld.inc(e.dma_start(out=self.cb_sb[:], in_=self.conv_b), 16)
                ld.inc(e.dma_start(out=self.dtb_sb[:], in_=self.dt_bias), 16)
                ld.inc(e.dma_start(out=self.dsk_sb[:], in_=self.d_skip), 16)
                ld.inc(e.dma_start(out=self.a_sb[:], in_=self.a_log), 16)

            @blk.scalar
            def _(e):
                ld.wait(e)
                aa.inc(e.activation(self.a_sb[:], self.a_sb[:], AF.Exp))

            @blk.vector
            def _(e):
                for i, v in enumerate([1.0 / D, -0.5, 1.0 + EPS, -EPS,
                                       -1.0, 1.0, 0.5]):
                    e.memset(self.cst[:, i:i + 1], v)
                aa.wait(e)
                vv.inc(e.tensor_scalar_mul(self.a_sb[:], self.a_sb[:], -1.0))

    def ph_a0(self):
        nc, ld, vv, aa = self.nc, self.ld, self.vv, self.aa
        with ExitStack() as ctx:
            xt = [ctx.enter_context(nc.sbuf_tensor(f"a0x{i}", [128, D], F32))
                  for i in range(8)]
            q = [ctx.enter_context(nc.sbuf_tensor(f"a0q{i}", [128, D], F32))
                 for i in range(4)]
            tmp = ctx.enter_context(nc.sbuf_tensor("a0t", [128, 16], F32))
            groups = [("full", g) for g in range(8)] + [("own", g) for g in range(4)]
            with nc.Block() as blk:
                @blk.sync
                def _(e):
                    for i in range(4):
                        ld.inc(e.dma_start(
                            out=xt[i][:],
                            in_=self.x_full[i * 128:(i + 1) * 128, :]), 16)
            for gi, (kind, g) in enumerate(groups):
                bufs = [xt[(gi % 2) * 4 + i] for i in range(4)]
                nbufs = [xt[((gi + 1) % 2) * 4 + i] for i in range(4)]
                ld_t = self.ld.n

                with nc.Block() as blk:
                    @blk.sync
                    def _(e, gi=gi, nbufs=nbufs):
                        if gi + 1 < len(groups):
                            nkind, ng = groups[gi + 1]
                            src = self.x_full if nkind == "full" else self.x_own
                            for i in range(4):
                                t = ng * 4 + i
                                ld.inc(e.dma_start(
                                    out=nbufs[i][:],
                                    in_=src[t * 128:(t + 1) * 128, :]), 16)

                    @blk.vector
                    def _(e, bufs=bufs, ld_t=ld_t):
                        e.wait_ge(self.ld.h, ld_t)
                        for i in range(4):
                            vv.inc(e.reduce_sum(tmp[:, i:i + 1], bufs[i][:],
                                                axis=AX.X))
                        e.drain()
                        for i in range(4):
                            vv.inc(e.tensor_scalar_mul(tmp[:, 4 + i:5 + i],
                                                       tmp[:, i:i + 1], 1.0 / D))
                        e.drain()
                        for i in range(4):
                            vv.inc(e.tensor_mul(tmp[:, 8 + i:9 + i],
                                                tmp[:, 4 + i:5 + i],
                                                tmp[:, 4 + i:5 + i]))
                        e.drain()
                        for i in range(4):
                            vv.inc(e.tensor_scalar(tmp[:, 8 + i:9 + i],
                                                   tmp[:, 8 + i:9 + i],
                                                   -1.0, EPS, ALU.mult, ALU.add))

                    @blk.scalar
                    def _(e, bufs=bufs, ld_t=ld_t):
                        e.wait_ge(self.ld.h, ld_t)
                        for i in range(4):
                            aa.inc(e.activation(q[i][:], bufs[i][:], AF.Square))

                with nc.Block() as blk:
                    @blk.vector
                    def _(e):
                        for i in range(4):
                            vv.inc(e.reduce_sum(tmp[:, 12 + i:13 + i], q[i][:],
                                                axis=AX.X))

                with nc.Block() as blk:
                    @blk.scalar
                    def _(e, kind=kind, g=g):
                        stats = self.stats if kind == "full" else self.stats_o
                        ncol = 4 if kind == "full" else 2
                        cc = [(g * 4 + i) * ncol for i in range(4)]
                        for i in range(4):
                            aa.inc(e.copy(stats[:, cc[i]:cc[i] + 1],
                                          tmp[:, 4 + i:5 + i]))
                            aa.inc(e.activation(tmp[:, 12 + i:13 + i],
                                                tmp[:, 12 + i:13 + i], AF.Ln,
                                                bias=tmp[:, 8 + i:9 + i],
                                                scale=self.cst[:, 0:1]))
                        e.drain()
                        for i in range(4):
                            aa.inc(e.activation(stats[:, cc[i] + 1:cc[i] + 2],
                                                tmp[:, 12 + i:13 + i], AF.Exp,
                                                scale=self.cst[:, 1:2]))
                        if kind == "full":
                            e.drain()
                            for i in range(4):
                                aa.inc(e.activation(tmp[:, 8 + i:9 + i],
                                                    stats[:, cc[i] + 1:cc[i] + 2],
                                                    AF.Square))
                            e.drain()
                            for i in range(4):
                                aa.inc(e.activation(tmp[:, 8 + i:9 + i],
                                                    tmp[:, 8 + i:9 + i], AF.Ln,
                                                    bias=self.cst[:, 2:3],
                                                    scale=self.cst[:, 3:4]))
                            e.drain()
                            for i in range(4):
                                aa.inc(e.activation(tmp[:, 8 + i:9 + i],
                                                    tmp[:, 8 + i:9 + i], AF.Exp,
                                                    scale=self.cst[:, 1:2]))

                with nc.Block() as blk:
                    @blk.vector
                    def _(e, kind=kind, g=g):
                        if kind != "full":
                            return
                        for i in range(4):
                            c0 = (g * 4 + i) * 4
                            vv.inc(e.tensor_mul(self.stats[:, c0 + 2:c0 + 3],
                                                self.stats[:, c0 + 1:c0 + 2],
                                                tmp[:, 8 + i:9 + i]))

    def ph_a(self):
        nc, ld, gd, mm, vv, aa, so = (self.nc, self.ld, self.gd, self.mm,
                                      self.vv, self.aa, self.so)
        with ExitStack() as ctx:
            sb = lambda n, s, d=F32: ctx.enter_context(nc.sbuf_tensor(n, s, d))
            w_in_bf = sb("w_in_bf", [128, 8 * 4096], BF16)
            xt = [sb(f"ax{i}", [128, D]) for i in range(8)]
            xn = [sb(f"axn{i}", [128, D]) for i in range(4)]
            z2 = [sb(f"az2{i}", [128, D]) for i in range(4)]
            xn2t = sb("axn2t", [128, 8 * TC], BF16)
            xnt = sb("axnt", [128, 8 * TC])
            ev = sb("aev", [128, 8 * TC])
            evz = sb("aevz", [128, 8 * TC], BF16)
            pA = ctx.enter_context(nc.psum_tensor("pA", [128, 8 * TC], F32))

            with nc.Block() as blk:
                @blk.gpsimd
                def _(e):
                    for k in range(8):
                        gd.inc(e.dma_start(
                            out=w_in_bf[:, k * 4096:(k + 1) * 4096],
                            in_=self.w_in[k * 128:(k + 1) * 128, :]), 16)

                @blk.sync
                def _(e):
                    for i in range(4):
                        ld.inc(e.dma_start(
                            out=xt[i][:],
                            in_=self.x_full[i * 128:(i + 1) * 128, :]), 16)

            for c in range(NCH):
                bufs = [xt[(c % 2) * 4 + i] for i in range(4)]
                nbufs = [xt[((c + 1) % 2) * 4 + i] for i in range(4)]
                ld_t = self.ld.n
                gd_t = self.gd.n

                with nc.Block() as blk:
                    @blk.sync
                    def _(e, c=c, nbufs=nbufs):
                        if c + 1 < NCH:
                            for i in range(4):
                                t = (c + 1) * 4 + i
                                ld.inc(e.dma_start(
                                    out=nbufs[i][:],
                                    in_=self.x_full[t * 128:(t + 1) * 128, :]), 16)

                    @blk.vector
                    def _(e, c=c, bufs=bufs, ld_t=ld_t):
                        e.wait_ge(self.ld.h, ld_t)
                        st = self.stats
                        for i in range(4):
                            t = c * 4 + i
                            vv.inc(e.tensor_scalar(
                                xn[i][:], bufs[i][:], st[:, 4 * t:4 * t + 1],
                                st[:, 4 * t + 1:4 * t + 2],
                                ALU.subtract, ALU.mult))
                            vv.inc(e.tensor_scalar(
                                z2[i][:], bufs[i][:], st[:, 4 * t:4 * t + 1],
                                st[:, 4 * t + 2:4 * t + 3],
                                ALU.subtract, ALU.mult))

                with nc.Block() as blk:
                    @blk.tensor
                    def _(e):
                        for dcol in range(8):
                            for tt in range(4):
                                ins = e.transpose(
                                    pA[:, dcol * TC + tt * 128:
                                       dcol * TC + (tt + 1) * 128],
                                    z2[tt][:, dcol * 128:(dcol + 1) * 128],
                                    self.ident_sb[:])
                        mm.inc(ins)

                with nc.Block() as blk:
                    @blk.vector
                    def _(e):
                        for dcol in range(8):
                            vv.inc(e.tensor_scalar(
                                xn2t[:, dcol * TC:(dcol + 1) * TC],
                                pA[:, dcol * TC:(dcol + 1) * TC],
                                self.lng_sb[:, dcol:dcol + 1],
                                self.lnb_sb[:, dcol:dcol + 1],
                                ALU.mult, ALU.add))

                with nc.Block() as blk:
                    @blk.tensor
                    def _(e):
                        for dcol in range(8):
                            for tt in range(4):
                                ins = e.transpose(
                                    pA[:, dcol * TC + tt * 128:
                                       dcol * TC + (tt + 1) * 128],
                                    xn[tt][:, dcol * 128:(dcol + 1) * 128],
                                    self.ident_sb[:])
                        mm.inc(ins)

                with nc.Block() as blk:
                    so_t = self.so.n

                    @blk.vector
                    def _(e, so_t=so_t):
                        e.wait_ge(self.so.h, so_t)
                        for dcol in range(8):
                            vv.inc(e.tensor_copy(
                                xnt[:, dcol * TC:(dcol + 1) * TC],
                                pA[:, dcol * TC:(dcol + 1) * TC]))

                    @blk.sync
                    def _(e, c=c):
                        self.vv.wait(e)
                        for dcol in range(8):
                            so.inc(e.dma_start(
                                out=self.xnT_s[dcol * 128:(dcol + 1) * 128,
                                               c * TC:(c + 1) * TC],
                                in_=xnt[:, dcol * TC:(dcol + 1) * TC]), 16)

                for grp in range(4):
                    o0 = grp * 8
                    with nc.Block() as blk:
                        @blk.tensor
                        def _(e, o0=o0, gd_t=gd_t, c=c, grp=grp):
                            if c == 0 and grp == 0:
                                e.wait_ge(self.gd.h, gd_t)
                            for oi in range(8):
                                o = o0 + oi
                                for k in range(8):
                                    ins = e.matmul(
                                        pA[:, oi * TC:(oi + 1) * TC],
                                        w_in_bf[:, k * 4096 + o * 128:
                                                k * 4096 + (o + 1) * 128],
                                        xn2t[:, k * TC:(k + 1) * TC],
                                        start=(k == 0), stop=(k == 7))
                            mm.inc(ins)

                    with nc.Block() as blk:
                        so_t = self.so.n
                        if o0 < 16:
                            @blk.vector
                            def _(e, so_t=so_t):
                                e.wait_ge(self.so.h, so_t)
                                for oi in range(8):
                                    vv.inc(e.tensor_copy(
                                        ev[:, oi * TC:(oi + 1) * TC],
                                        pA[:, oi * TC:(oi + 1) * TC]))

                            @blk.sync
                            def _(e, o0=o0, c=c):
                                self.vv.wait(e)
                                for oi in range(8):
                                    o = o0 + oi
                                    so.inc(e.dma_start(
                                        out=self.xiT_s[o * 128:(o + 1) * 128,
                                                       c * TC:(c + 1) * TC],
                                        in_=ev[:, oi * TC:(oi + 1) * TC]), 16)
                        else:
                            @blk.scalar
                            def _(e, so_t=so_t):
                                e.wait_ge(self.so.h, so_t)
                                for oi in range(8):
                                    aa.inc(e.activation(
                                        evz[:, oi * TC:(oi + 1) * TC],
                                        pA[:, oi * TC:(oi + 1) * TC], AF.Silu))

                            @blk.sync
                            def _(e, o0=o0, c=c):
                                self.aa.wait(e)
                                for oi in range(8):
                                    o = o0 + oi - 16
                                    so.inc(e.dma_start(
                                        out=self.szT_s[o * 128:(o + 1) * 128,
                                                       c * TC:(c + 1) * TC],
                                        in_=evz[:, oi * TC:(oi + 1) * TC]), 16)

    def ph_a_own(self):
        nc, ld, mm, vv, so = self.nc, self.ld, self.mm, self.vv, self.so
        with ExitStack() as ctx:
            sb = lambda n, s, d=F32: ctx.enter_context(nc.sbuf_tensor(n, s, d))
            xt = [sb(f"ox{i}", [128, D]) for i in range(8)]
            xn = [sb(f"oxn{i}", [128, D]) for i in range(4)]
            xnt_bf = sb("oxnt_bf", [128, 8 * TC], BF16)
            xnt_f = sb("oxnt_f", [128, 8 * TC])
            pA = ctx.enter_context(nc.psum_tensor("poA", [128, 8 * TC], F32))

            with nc.Block() as blk:
                @blk.sync
                def _(e):
                    for i in range(4):
                        ld.inc(e.dma_start(
                            out=xt[i][:],
                            in_=self.x_own[i * 128:(i + 1) * 128, :]), 16)

            for c in range(4):
                bufs = [xt[(c % 2) * 4 + i] for i in range(4)]
                nbufs = [xt[((c + 1) % 2) * 4 + i] for i in range(4)]
                ld_t = self.ld.n
                with nc.Block() as blk:
                    @blk.sync
                    def _(e, c=c, nbufs=nbufs):
                        if c + 1 < 4:
                            for i in range(4):
                                t = (c + 1) * 4 + i
                                ld.inc(e.dma_start(
                                    out=nbufs[i][:],
                                    in_=self.x_own[t * 128:(t + 1) * 128, :]), 16)

                    @blk.vector
                    def _(e, c=c, bufs=bufs, ld_t=ld_t):
                        e.wait_ge(self.ld.h, ld_t)
                        st = self.stats_o
                        for i in range(4):
                            t = c * 4 + i
                            vv.inc(e.tensor_scalar(
                                xn[i][:], bufs[i][:], st[:, 2 * t:2 * t + 1],
                                st[:, 2 * t + 1:2 * t + 2],
                                ALU.subtract, ALU.mult))

                with nc.Block() as blk:
                    @blk.tensor
                    def _(e):
                        for dcol in range(8):
                            for tt in range(4):
                                ins = e.transpose(
                                    pA[:, dcol * TC + tt * 128:
                                       dcol * TC + (tt + 1) * 128],
                                    xn[tt][:, dcol * 128:(dcol + 1) * 128],
                                    self.ident_sb[:])
                        mm.inc(ins)

                with nc.Block() as blk:
                    so_t = self.so.n

                    @blk.vector
                    def _(e, so_t=so_t):
                        e.wait_ge(self.so.h, so_t)
                        for dcol in range(8):
                            vv.inc(e.tensor_copy(
                                xnt_bf[:, dcol * TC:(dcol + 1) * TC],
                                pA[:, dcol * TC:(dcol + 1) * TC]))
                            vv.inc(e.tensor_copy(
                                xnt_f[:, dcol * TC:(dcol + 1) * TC],
                                pA[:, dcol * TC:(dcol + 1) * TC]))

                    @blk.sync
                    def _(e, c=c):
                        self.vv.wait(e)
                        for dcol in range(8):
                            so.inc(e.dma_start(
                                out=self.xnTo_s[dcol * 128:(dcol + 1) * 128,
                                                c * TC:(c + 1) * TC],
                                in_=xnt_bf[:, dcol * TC:(dcol + 1) * TC]), 16)
                            so.inc(e.dma_start(
                                out=self.g_s[dcol * 128:(dcol + 1) * 128,
                                             c * TC:(c + 1) * TC],
                                in_=xnt_f[:, dcol * TC:(dcol + 1) * TC]), 16)

    def ph_b(self):
        bparts = getattr(self, "bparts", {"conv", "xdbl", "delta", "delta_sp"})
        nc, ld, gd, mm, vv, aa, so = (self.nc, self.ld, self.gd, self.mm,
                                      self.vv, self.aa, self.so)
        self._phase_fence()
        with ExitStack() as ctx:
            sb = lambda n, s, d=F32: ctx.enter_context(nc.sbuf_tensor(n, s, d))
            xib = [sb(f"bxi{i}", [128, L + 4]) for i in range(2)]
            acc = [sb(f"bacc{i}", [128, L]) for i in range(2)]
            xc = sb("bxc", [128, L])
            xc_bf = sb("bxcbf", [128, L], BF16)
            xcd_bf = sb("bxcdbf", [128, L], BF16)
            w_xp_bf = sb("bwxp", [128, 16 * 96], BF16)
            dlo_bf = sb("bdlo", [64, L], BF16)
            bmcm = sb("bbmcm", [32, L])

            with ExitStack() as pctx:
                pX = pctx.enter_context(nc.psum_tensor("pX", [96, L], F32))

                with nc.Block() as blk:
                    @blk.gpsimd
                    def _(e):
                        gd.inc(e.dma_start(
                            out=w_xp_bf[:].rearrange("p (a c) -> p a c", c=96),
                            in_=self.w_xp.rearrange("(a p) c -> p a c", p=128)), 16)

                    @blk.sync
                    def _(e):
                        ld.inc(e.dma_start(out=xib[0][:, 4:4 + L],
                                           in_=self.xiT_s[0:128, :]), 16)

                for t in range(16):
                    buf = xib[t % 2]
                    nbuf = xib[(t + 1) % 2]
                    ld_t = self.ld.n
                    gd_t = self.gd.n
                    so_t = self.so.n
                    with nc.Block() as blk:
                        @blk.sync
                        def _(e, t=t, nbuf=nbuf):
                            if t + 1 < 16:
                                ld.inc(e.dma_start(
                                    out=nbuf[:, 4:4 + L],
                                    in_=self.xiT_s[(t + 1) * 128:(t + 2) * 128, :]),
                                    16)

                        @blk.vector
                        def _(e, t=t, buf=buf, ld_t=ld_t, so_t=so_t):
                            e.wait_ge(self.ld.h, ld_t)
                            e.memset(buf[:, 1:4], 0.0)
                            e.drain()
                            vv.inc(e.tensor_scalar_mul(
                                acc[0][:], buf[:, 1:1 + L],
                                self.cw_sb[:, 4 * t:4 * t + 1]))
                            e.drain()
                            vv.inc(e.scalar_tensor_tensor(
                                acc[1][:], buf[:, 2:2 + L],
                                self.cw_sb[:, 4 * t + 1:4 * t + 2], acc[0][:],
                                ALU.mult, ALU.add))
                            e.drain()
                            vv.inc(e.scalar_tensor_tensor(
                                acc[0][:], buf[:, 3:3 + L],
                                self.cw_sb[:, 4 * t + 2:4 * t + 3], acc[1][:],
                                ALU.mult, ALU.add))
                            e.drain()
                            vv.inc(e.scalar_tensor_tensor(
                                acc[1][:], buf[:, 4:4 + L],
                                self.cw_sb[:, 4 * t + 3:4 * t + 4], acc[0][:],
                                ALU.mult, ALU.add))
                            self._bconv_mark = self.vv.n
                            e.wait_ge(self.aa.h, self.aa.n + 1)
                            e.wait_ge(self.so.h, so_t)
                            vv.inc(e.tensor_copy(xc_bf[:], xc[:]))
                            vv.inc(e.tensor_scalar_mul(
                                xcd_bf[:], xc[:], self.dsk_sb[:, t:t + 1]))

                        @blk.scalar
                        def _(e, t=t, so_t=so_t):
                            e.wait_ge(self.vv.h, self._bconv_mark)
                            e.wait_ge(self.so.h, so_t)
                            aa.inc(e.activation(xc[:], acc[1][:], AF.Silu,
                                                bias=self.cb_sb[:, t:t + 1]))

                        @blk.sync
                        def _(e, t=t):
                            self.vv.wait(e)
                            so.inc(e.dma_start(
                                out=self.xcT_s[t * 128:(t + 1) * 128, :],
                                in_=xc[:]), 16)
                            so.inc(e.dma_start(
                                out=self.xcdT_s[t * 128:(t + 1) * 128, :],
                                in_=xcd_bf[:]), 16)

                        if "xdbl" in bparts:
                            @blk.tensor
                            def _(e, t=t, gd_t=gd_t):
                                if t == 0:
                                    e.wait_ge(self.gd.h, gd_t)
                                self.vv.wait(e)
                                for cc in range(8):
                                    ins = e.matmul(
                                        pX[:, cc * TC:(cc + 1) * TC],
                                        w_xp_bf[:, t * 96:(t + 1) * 96],
                                        xc_bf[:, cc * TC:(cc + 1) * TC],
                                        start=(t == 0), stop=(t == 15))
                                mm.inc(ins)

                if "xdbl" not in bparts:
                    return
                with nc.Block() as blk:
                    @blk.vector
                    def _(e):
                        vv.inc(e.tensor_copy(dlo_bf[:], pX[0:64, :]))
                        vv.inc(e.tensor_copy(bmcm[:], pX[64:96, :]))

                    @blk.sync
                    def _(e):
                        self.vv.wait(e)
                        so.inc(e.dma_start(out=self.bmcm_s, in_=bmcm[:]), 16)

            if "delta" not in bparts:
                return
            with ExitStack() as pctx:
                pD = pctx.enter_context(nc.psum_tensor("pD", [128, L], F32))
                w_dt_bf = ctx.enter_context(
                    nc.sbuf_tensor("bwdt", [64, DI], BF16))
                dl1 = ctx.enter_context(nc.sbuf_tensor("bdl1", [128, L], F32))
                dl2 = ctx.enter_context(nc.sbuf_tensor("bdl2", [128, L], F32))
                dl3 = ctx.enter_context(nc.sbuf_tensor("bdl3", [128, L], BF16))

                with nc.Block() as blk:
                    @blk.gpsimd
                    def _(e):
                        gd.inc(e.dma_start(out=w_dt_bf[:], in_=self.w_dt), 16)

                for t in range(16):
                    gd_t = self.gd.n
                    with nc.Block() as blk:
                        @blk.tensor
                        def _(e, t=t, gd_t=gd_t):
                            if t == 0:
                                e.wait_ge(self.gd.h, gd_t)
                            for cc in range(8):
                                ins = e.matmul(
                                    pD[:, cc * TC:(cc + 1) * TC],
                                    w_dt_bf[:, t * 128:(t + 1) * 128],
                                    dlo_bf[:, cc * TC:(cc + 1) * TC],
                                    start=True, stop=True)
                            mm.inc(ins)

                    if "delta_sp" not in bparts:
                        continue
                    with nc.Block() as blk:
                        so_t = self.so.n

                        @blk.scalar
                        def _(e, t=t):
                            for cc in range(8):
                                aa.inc(e.activation(
                                    dl1[:, cc * TC:(cc + 1) * TC],
                                    pD[:, cc * TC:(cc + 1) * TC], AF.Abs,
                                    bias=self.dtb_sb[:, t:t + 1]))
                            e.drain()
                            aa.inc(e.activation(dl1[:], dl1[:], AF.Exp,
                                                scale=self.cst[:, 4:5]))
                            e.drain()
                            aa.inc(e.activation(dl1[:], dl1[:], AF.Ln, bias=self.cst[:, 5:6]))

                        @blk.vector
                        def _(e, t=t, so_t=so_t):
                            for cc in range(8):
                                vv.inc(e.tensor_scalar(
                                    dl2[:, cc * TC:(cc + 1) * TC],
                                    pD[:, cc * TC:(cc + 1) * TC],
                                    self.dtb_sb[:, t:t + 1], 0.0,
                                    ALU.add, ALU.max))
                            self.aa.wait(e)
                            e.wait_ge(self.so.h, so_t)
                            e.drain()
                            vv.inc(e.tensor_add(dl3[:], dl2[:], dl1[:]))

                        @blk.sync
                        def _(e, t=t):
                            self.vv.wait(e)
                            so.inc(e.dma_start(
                                out=self.dltT_s[t * 128:(t + 1) * 128, :],
                                in_=dl3[:]), 16)

    def ph_s(self):
        nc, ld, gd, mm, vv, aa, so = (self.nc, self.ld, self.gd, self.mm,
                                      self.vv, self.aa, self.so)
        self._phase_fence()
        with ExitStack() as ctx:
            sb = lambda n, s, d=F32: ctx.enter_context(nc.sbuf_tensor(n, s, d))
            dlt = sb("sdlt", [128, 16 * TC], BF16)
            xch = sb("sxch", [128, 8 * TC])
            du = sb("sdu", [128, 16 * TC], BF16)
            Y = sb("sY", [128, 16 * TC])
            da = sb("sda", [128, 8 * TC])
            bca = sb("sbca", [128, 32 * TC], BF16)
            brow = sb("sbrow", [1, 2 * TC])
            ymbuf = sb("symbuf", [128, 4 * TC])
            stbf = sb("sstbf", [128, 8 * TC], BF16)
            hstate = sb("shstate", [128, 256])
            yg = sb("syg", [128, 16 * TC], BF16)
            wo = [sb(f"swo{i}", [128, 1024], BF16) for i in range(2)]
            xntb = sb("sxntb", [128, 8 * TC])
            sout = sb("ssout", [128, 8 * TC])
            pA = ctx.enter_context(nc.psum_tensor("pSA", [128, 8 * TC], F32))

            with nc.Block() as blk:
                @blk.vector
                def _(e):
                    vv.inc(e.memset(hstate[:], 0.0))

            for c in range(NCH):
                ld_t = self.ld.n
                with nc.Block() as blk:
                    @blk.sync
                    def _(e, c=c):
                        for t in range(16):
                            ld.inc(e.dma_start(
                                out=dlt[:, t * TC:(t + 1) * TC],
                                in_=self.dltT_s[t * 128:(t + 1) * 128,
                                                c * TC:(c + 1) * TC]), 16)
                        for t in range(8):
                            ld.inc(e.dma_start(
                                out=xch[:, t * TC:(t + 1) * TC],
                                in_=self.xcT_s[t * 128:(t + 1) * 128,
                                               c * TC:(c + 1) * TC]), 16)
                        for dm in range(8):
                            ld.inc(e.dma_start(
                                out=xntb[:, dm * TC:(dm + 1) * TC],
                                in_=self.xnT_s[dm * 128:(dm + 1) * 128,
                                               c * TC:(c + 1) * TC]), 16)

                    @blk.vector
                    def _(e, c=c):
                        self.ld.wait(e)
                        vv.inc(e.memset(Y[:], 0.0))
                        vv.inc(e.tensor_mul(du[:, 0:8 * TC], dlt[:, 0:8 * TC],
                                            xch[:]))

                with nc.Block() as blk:
                    @blk.sync
                    def _(e, c=c):
                        for t in range(8):
                            ld.inc(e.dma_start(
                                out=xch[:, t * TC:(t + 1) * TC],
                                in_=self.xcT_s[(8 + t) * 128:(9 + t) * 128,
                                               c * TC:(c + 1) * TC]), 16)

                    @blk.vector
                    def _(e):
                        self.ld.wait(e)
                        vv.inc(e.tensor_mul(du[:, 8 * TC:16 * TC],
                                            dlt[:, 8 * TC:16 * TC], xch[:]))

                with nc.Block() as blk:
                    mm0, vv0 = self.mm.n, self.vv.n
                    br_tgt = {}

                    @blk.sync
                    def _(e, c=c, mm0=mm0):
                        for g in range(16):
                            if g >= 1:
                                e.wait_ge(self.mm.h, mm0 + g)
                            s = self.brs[g % 2]
                            for i in range(2):
                                r = (g % 8) * 2 + i + (16 if g >= 8 else 0)
                                s.inc(e.dma_start(
                                    out=brow[0:1, i * TC:(i + 1) * TC],
                                    in_=self.bmcm_s[r:r + 1,
                                                    c * TC:(c + 1) * TC]), 16)
                            br_tgt[g] = s.n

                    @blk.tensor
                    def _(e, mm0=mm0, vv0=vv0):
                        for g in range(16):
                            e.wait_ge(self.brs[g % 2].h, br_tgt[g])
                            if g >= 2:
                                e.wait_ge(self.vv.h, vv0 + g - 1)
                            half = (g % 2) * 2
                            for i in range(2):
                                ins = e.matmul(
                                    pA[:, (half + i) * TC:(half + i + 1) * TC],
                                    self.ones_sb[0:1, :],
                                    brow[0:1, i * TC:(i + 1) * TC],
                                    start=True, stop=True)
                            mm.inc(ins)

                    @blk.vector
                    def _(e, mm0=mm0):
                        for g in range(16):
                            e.wait_ge(self.mm.h, mm0 + g + 1)
                            half = (g % 2) * 2
                            for i in range(2):
                                slot = (g % 8) * 2 + i + (16 if g >= 8 else 0)
                                ins = e.tensor_copy(
                                    bca[:, slot * TC:(slot + 1) * TC],
                                    pA[:, (half + i) * TC:(half + i + 1) * TC])
                            vv.inc(ins)

                with nc.Block() as blk:
                    aa0, vv0 = self.aa.n, self.vv.n
                    nbatch = NST * 4

                    @blk.scalar
                    def _(e, aa0=aa0, vv0=vv0):
                        for b in range(nbatch):
                            n, q = b // 4, b % 4
                            if b >= 2:
                                e.wait_ge(self.vv.h, vv0 + b - 1)
                            sl = (b % 2) * 4
                            for i in range(4):
                                t = q * 4 + i
                                ins = e.activation(
                                    da[:, (sl + i) * TC:(sl + i + 1) * TC],
                                    dlt[:, t * TC:(t + 1) * TC], AF.Exp,
                                    scale=self.a_sb[:, t * NST + n:
                                                    t * NST + n + 1])
                            aa.inc(ins)

                    pp0 = self.pp.n

                    @blk.vector
                    def _(e, aa0=aa0, pp0=pp0):
                        for db in range(nbatch // 2):
                            bs = (2 * db, 2 * db + 1)
                            e.wait_ge(self.aa.h, aa0 + 2 * db + 2)
                            for h, b in enumerate(bs):
                                n, q = b // 4, b % 4
                                for i in range(4):
                                    t = q * 4 + i
                                    e.tensor_mul(
                                        stbf[:, (h * 4 + i) * TC:
                                             (h * 4 + i + 1) * TC],
                                        du[:, t * TC:(t + 1) * TC],
                                        bca[:, n * TC:(n + 1) * TC])
                            e.drain()
                            for h, b in enumerate(bs):
                                n, q = b // 4, b % 4
                                sl = (b % 2) * 4
                                for i in range(4):
                                    idx = n * 16 + q * 4 + i
                                    e.tensor_tensor_scan(
                                        xch[:, (h * 4 + i) * TC:
                                            (h * 4 + i + 1) * TC],
                                        da[:, (sl + i) * TC:(sl + i + 1) * TC],
                                        stbf[:, (h * 4 + i) * TC:
                                             (h * 4 + i + 1) * TC],
                                        hstate[:, idx:idx + 1],
                                        ALU.mult, ALU.add)
                            e.drain()
                            for h, b in enumerate(bs):
                                n, q = b // 4, b % 4
                                t0 = q * 4
                                e.tensor_copy(
                                    hstate[:, n * 16 + t0:n * 16 + t0 + 4],
                                    xch[:, (h * 4 + 1) * TC - 1:
                                        (h * 4 + 4) * TC:TC])
                                if b >= 1:
                                    e.wait_ge(self.pp.h, pp0 + b)
                                for i in range(4):
                                    ins = e.tensor_mul(
                                        ymbuf[:, i * TC:(i + 1) * TC],
                                        xch[:, (h * 4 + i) * TC:
                                            (h * 4 + i + 1) * TC],
                                        bca[:, (16 + n) * TC:(17 + n) * TC])
                                vv.inc(ins)

                    @blk.gpsimd
                    def _(e, pp0=pp0):
                        vvb = self.vv.n - nbatch
                        for b in range(nbatch):
                            n, q = b // 4, b % 4
                            t0 = q * 4
                            e.wait_ge(self.vv.h, vvb + b + 1)
                            for i in range(4):
                                t = t0 + i
                                ins = e.tensor_add(
                                    Y[:, t * TC:(t + 1) * TC],
                                    Y[:, t * TC:(t + 1) * TC],
                                    ymbuf[:, i * TC:(i + 1) * TC])
                            self.pp.inc(ins)

                with nc.Block() as blk:
                    @blk.sync
                    def _(e, c=c):
                        for t in range(16):
                            ld.inc(e.dma_start(
                                out=bca[:, t * TC:(t + 1) * TC],
                                in_=self.xcdT_s[t * 128:(t + 1) * 128,
                                                c * TC:(c + 1) * TC]), 16)
                            ld.inc(e.dma_start(
                                out=bca[:, (16 + t) * TC:(17 + t) * TC],
                                in_=self.szT_s[t * 128:(t + 1) * 128,
                                               c * TC:(c + 1) * TC]), 16)

                with nc.Block() as blk:
                    @blk.vector
                    def _(e, c=c):
                        self.ld.wait(e)
                        self.pp.wait(e)
                        for grp in range(4):
                            if grp:
                                e.drain()
                            for i in range(4):
                                t = grp * 4 + i
                                e.tensor_add(ymbuf[:, i * TC:(i + 1) * TC],
                                             bca[:, t * TC:(t + 1) * TC],
                                             Y[:, t * TC:(t + 1) * TC])
                            e.drain()
                            for i in range(4):
                                t = grp * 4 + i
                                ins = e.tensor_mul(
                                    yg[:, t * TC:(t + 1) * TC],
                                    ymbuf[:, i * TC:(i + 1) * TC],
                                    bca[:, (16 + t) * TC:(17 + t) * TC])
                        vv.inc(ins)

                with nc.Block() as blk:
                    mm0 = self.mm.n
                    wo_tgt = {}

                    @blk.gpsimd
                    def _(e, mm0=mm0):
                        for t in range(16):
                            if t >= 2:
                                e.wait_ge(self.mm.h, mm0 + t - 1)
                            self.wos[t % 2].inc(e.dma_start(
                                out=wo[t % 2][:],
                                in_=self.w_out[t * 128:(t + 1) * 128, :]), 16)
                            wo_tgt[t] = self.wos[t % 2].n

                    @blk.tensor
                    def _(e):
                        for t in range(16):
                            e.wait_ge(self.wos[t % 2].h, wo_tgt[t])
                            for dm in range(8):
                                ins = e.matmul(
                                    pA[:, dm * TC:(dm + 1) * TC],
                                    wo[t % 2][:, dm * 128:(dm + 1) * 128],
                                    yg[:, t * TC:(t + 1) * TC],
                                    start=(t == 0), stop=(t == 15))
                            mm.inc(ins)

                with nc.Block() as blk:
                    so_t = self.so.n

                    @blk.vector
                    def _(e, so_t=so_t):
                        e.wait_ge(self.so.h, so_t)
                        for dm in range(8):
                            vv.inc(e.tensor_add(
                                sout[:, dm * TC:(dm + 1) * TC],
                                pA[:, dm * TC:(dm + 1) * TC],
                                xntb[:, dm * TC:(dm + 1) * TC]))

                    @blk.sync
                    def _(e, c=c):
                        self.vv.wait(e)
                        for dm in range(8):
                            so.inc(e.dma_start(
                                out=self.sT_s[dm * 128:(dm + 1) * 128,
                                              c * TC:(c + 1) * TC],
                                in_=sout[:, dm * TC:(dm + 1) * TC]), 16)

    def ph_moe(self):
        nc, ld, gd, mm, vv, aa, so = (self.nc, self.ld, self.gd, self.mm,
                                      self.vv, self.aa, self.so)
        self._phase_fence()
        with ExitStack() as ctx:
            sb = lambda n, s, d=F32: ctx.enter_context(nc.sbuf_tensor(n, s, d))
            xnt_own = sb("mxnt", [128, 8 * TOK], BF16)
            wgsb = sb("mwg", [128, 64], BF16)
            bgsb = sb("mbg", [128, 8])
            lg = sb("mlg", [128, 16 * 8])
            lm = sb("mlm", [128, 16 * 8])
            msk = sb("mmsk", [128, 16 * 8])
            wgt = sb("mwgt", [128, 16 * 8])
            gm = sb("mgm", [128, 16])
            gm2 = sb("mgm2", [128, 16])
            gw1 = sb("mgw1", [128, 16])
            gw2 = sb("mgw2", [128, 16])
            gtd = sb("mgtd", [128, 16])
            wtT = sb("mwtT", [8, TOK])
            werow = sb("mwerow", [1, TOK])
            wcast = sb("mwcast", [128, TOK], BF16)
            xse = sb("mxse", [128, 8 * 1024], BF16)
            w1q = [sb(f"mw1q{i}", [128, 8 * 512], BF16) for i in range(2)]
            w3q = [sb(f"mw3q{i}", [128, 8 * 512], BF16) for i in range(2)]
            h1 = sb("mh1", [128, 64 * 512], BF16)
            w2kt = [sb(f"mw2kt{i}", [128, 2 * 1024], BF16) for i in range(3)]
            silb = sb("msilb", [128, 4 * 512], BF16)
            gio = sb("mgio", [128, 2 * 512])
            gout = sb("mgout", [128, 8 * 512])
            pM = ctx.enter_context(nc.psum_tensor("pM", [128, 8 * TC], F32))

            with nc.Block() as blk:
                @blk.sync
                def _(e):
                    for dm in range(8):
                        ld.inc(e.dma_start(
                            out=xnt_own[:, dm * TOK:(dm + 1) * TOK],
                            in_=self.xnTo_s[dm * 128:(dm + 1) * 128, :]), 16)
                    ld.inc(e.dma_start(out=bgsb[:], in_=self.bg_bc), 16)

                @blk.gpsimd
                def _(e):
                    gd.inc(e.dma_start(
                        out=wgsb[:].rearrange("p (a c) -> p a c", c=8),
                        in_=self.wg.rearrange("(a p) c -> p a c", p=128)), 16)

            with nc.Block() as blk:
                @blk.tensor
                def _(e):
                    self.ld.wait(e)
                    self.gd.wait(e)
                    for tt in range(16):
                        for k in range(8):
                            ins = e.matmul(
                                pM[:, tt * 8:(tt + 1) * 8],
                                xnt_own[:, k * TOK + tt * 128:
                                        k * TOK + (tt + 1) * 128],
                                wgsb[:, k * 8:(k + 1) * 8],
                                start=(k == 0), stop=(k == 7))
                        mm.inc(ins)

            with nc.Block() as blk:
                @blk.vector
                def _(e):
                    for tt in range(16):
                        vv.inc(e.tensor_add(lg[:, tt * 8:(tt + 1) * 8],
                                            pM[:, tt * 8:(tt + 1) * 8], bgsb[:]))
                    e.drain()
                    for tt in range(16):
                        vv.inc(e.reduce_max(gm[:, tt:tt + 1],
                                            lg[:, tt * 8:(tt + 1) * 8], axis=AX.X))
                    e.drain()
                    for tt in range(16):
                        vv.inc(e.tensor_scalar(msk[:, tt * 8:(tt + 1) * 8],
                                               lg[:, tt * 8:(tt + 1) * 8],
                                               gm[:, tt:tt + 1], None,
                                               ALU.is_equal))
                    e.drain()
                    for tt in range(16):
                        vv.inc(e.scalar_tensor_tensor(
                            lm[:, tt * 8:(tt + 1) * 8],
                            msk[:, tt * 8:(tt + 1) * 8], -1e30,
                            lg[:, tt * 8:(tt + 1) * 8], ALU.mult, ALU.add))
                    e.drain()
                    for tt in range(16):
                        vv.inc(e.reduce_max(gm2[:, tt:tt + 1],
                                            lm[:, tt * 8:(tt + 1) * 8], axis=AX.X))
                    e.drain()
                    vv.inc(e.tensor_sub(gtd[:], gm[:], gm2[:]))

            with nc.Block() as blk:
                @blk.scalar
                def _(e):
                    aa.inc(e.activation(gtd[:], gtd[:], AF.Tanh, scale=self.cst[:, 6:7]))

            with nc.Block() as blk:
                @blk.vector
                def _(e):
                    vv.inc(e.tensor_scalar(gw1[:], gtd[:], 0.5, 0.5,
                                           ALU.mult, ALU.add))
                    for tt in range(16):
                        vv.inc(e.tensor_scalar(wgt[:, tt * 8:(tt + 1) * 8],
                                               lm[:, tt * 8:(tt + 1) * 8],
                                               gm2[:, tt:tt + 1], None,
                                               ALU.is_equal))
                    e.drain()
                    vv.inc(e.tensor_scalar(gw2[:], gw1[:], -1.0, 1.0,
                                           ALU.mult, ALU.add))
                    for tt in range(16):
                        vv.inc(e.tensor_scalar_mul(msk[:, tt * 8:(tt + 1) * 8],
                                                   msk[:, tt * 8:(tt + 1) * 8],
                                                   gw1[:, tt:tt + 1]))
                    e.drain()
                    for tt in range(16):
                        vv.inc(e.scalar_tensor_tensor(
                            wgt[:, tt * 8:(tt + 1) * 8],
                            wgt[:, tt * 8:(tt + 1) * 8],
                            gw2[:, tt:tt + 1],
                            msk[:, tt * 8:(tt + 1) * 8],
                            ALU.mult, ALU.add))

            with nc.Block() as blk:
                @blk.tensor
                def _(e):
                    for tt in range(16):
                        ins = e.transpose(pM[0:8, tt * 128:(tt + 1) * 128],
                                          wgt[:, tt * 8:(tt + 1) * 8],
                                          self.ident_sb[:])
                    mm.inc(ins)

            with nc.Block() as blk:
                @blk.vector
                def _(e):
                    vv.inc(e.tensor_copy(wtT[:], pM[0:8, 0:TOK]))

                @blk.sync
                def _(e):
                    self.vv.wait(e)
                    so.inc(e.dma_start(out=self.wt_s, in_=wtT[:]), 16)

            self._phase_fence()

            for ex in range(E):
                with nc.Block() as blk:
                    @blk.sync
                    def _(e, ex=ex):
                        ld.inc(e.dma_start(out=werow[:],
                                           in_=self.wt_s[ex:ex + 1, :]), 16)

                    @blk.tensor
                    def _(e):
                        self.ld.wait(e)
                        for i in range(4):
                            ins = e.matmul(pM[:, i * TC:(i + 1) * TC],
                                           self.ones_sb[0:1, :],
                                           werow[0:1, i * TC:(i + 1) * TC],
                                           start=True, stop=True)
                        mm.inc(ins)

                    @blk.vector
                    def _(e):
                        self.mm.wait(e)
                        for i in range(4):
                            ins = e.tensor_copy(wcast[:, i * TC:(i + 1) * TC],
                                                pM[:, i * TC:(i + 1) * TC])
                        vv.inc(ins)

                if getattr(self, "gsnap", 0) and ex > 0:
                    with nc.Block() as blk:
                        @blk.sync
                        def _(e, ex=ex):
                            self.so.wait(e)
                            for dm in range(8):
                                ld.inc(e.dma_start(
                                    out=gio[:, 0:512],
                                    in_=self.g_s[dm * 128:(dm + 1) * 128, 0:512]), 16)
                                self.ld.wait(e)
                                so.inc(e.dma_start(
                                    out=self.gsnap_s[(ex - 1) * D + dm * 128:
                                                     (ex - 1) * D + (dm + 1) * 128,
                                                     0:512],
                                    in_=gio[:, 0:512]), 16)
                            self.so.wait(e)

                for p in range(2):
                    with nc.Block() as blk:
                        gd0, mm0, vv0, aa0 = (self.gd.n, self.mm.n,
                                              self.vv.n, self.aa.n)

                        wq_tgt = {}

                        @blk.gpsimd
                        def _(e, ex=ex, p=p, mm0=mm0):
                            for q in range(8):
                                if q >= 2:
                                    e.wait_ge(self.mm.h, mm0 + (q - 1) * 4)
                                s = self.wqs[q % 2]
                                s.inc(e.dma_start(
                                    out=w1q[q % 2][:].rearrange(
                                        "p (a c) -> p a c", c=512),
                                    in_=self.w1[ex, :, q * 512:(q + 1) * 512]
                                    .rearrange("(a p) c -> p a c", p=128)), 16)
                                s.inc(e.dma_start(
                                    out=w3q[q % 2][:].rearrange(
                                        "p (a c) -> p a c", c=512),
                                    in_=self.w3[ex, :, q * 512:(q + 1) * 512]
                                    .rearrange("(a p) c -> p a c", p=128)), 16)
                                wq_tgt[q] = s.n

                        @blk.vector
                        def _(e, ex=ex, p=p, mm0=mm0, aa0=aa0):
                            for k in range(8):
                                ins = e.tensor_mul(
                                    xse[:, k * 1024:(k + 1) * 1024],
                                    xnt_own[:, k * TOK + p * 1024:
                                            k * TOK + (p + 1) * 1024],
                                    wcast[:, p * 1024:(p + 1) * 1024])
                            vv.inc(ins)
                            for s in range(32):
                                e.wait_ge(self.aa.h, aa0 + s + 1)
                                e.wait_ge(self.mm.h, mm0 + s + 1)
                                base = (s % 2) * 4
                                slb = (s % 2) * 2
                                for tc in range(2):
                                    ins = e.tensor_mul(
                                        h1[:, (s * 2 + tc) * 512:
                                           (s * 2 + tc + 1) * 512],
                                        silb[:, (slb + tc) * 512:
                                             (slb + tc + 1) * 512],
                                        pM[:, (base + 2 + tc) * 512:
                                           (base + 3 + tc) * 512])
                                vv.inc(ins)

                        @blk.scalar
                        def _(e, mm0=mm0, vv0=vv0, aa0=aa0):
                            ser = getattr(self, "p1ser", 0)
                            for s in range(32):
                                e.wait_ge(self.mm.h, mm0 + s + 1)
                                if s >= 2 - ser:
                                    e.wait_ge(self.vv.h, vv0 + s + ser)
                                base = (s % 2) * 4
                                slb = (s % 2) * 2
                                for tc in range(2):
                                    ins = e.activation(
                                        silb[:, (slb + tc) * 512:
                                             (slb + tc + 1) * 512],
                                        pM[:, (base + tc) * 512:
                                           (base + tc + 1) * 512], AF.Silu)
                                aa.inc(ins)

                        @blk.tensor
                        def _(e, gd0=gd0, vv0=vv0):
                            for s in range(32):
                                q, hl = s // 4, s % 4
                                if s == 0:
                                    e.wait_ge(self.vv.h, vv0 + 1)
                                if hl == 0:
                                    e.wait_ge(self.wqs[q % 2].h, wq_tgt[q])
                                if s >= 2 and s % 2 == 0:
                                    e.wait_ge(self.vv.h, vv0 + s + 1)
                                base = (s % 2) * 4
                                for wi, wq in ((0, w1q[q % 2]), (1, w3q[q % 2])):
                                    for tc in range(2):
                                        for k in range(8):
                                            ins = e.matmul(
                                                pM[:, (base + wi * 2 + tc) * 512:
                                                   (base + wi * 2 + tc + 1) * 512],
                                                wq[:, k * 512 + hl * 128:
                                                   k * 512 + (hl + 1) * 128],
                                                xse[:, k * 1024 + tc * 512:
                                                    k * 1024 + (tc + 1) * 512],
                                                start=(k == 0), stop=(k == 7))
                                mm.inc(ins)

                    if ex == 0 and p == 0 and getattr(self, "moedbg", 0):
                        with nc.Block() as blk:
                            @blk.sync
                            def _(e):
                                self.gd.wait(e)
                                so.inc(e.dma_start(out=self.xse_s, in_=xse[:]), 16)
                                so.inc(e.dma_start(out=self.h1_s, in_=h1[:]), 16)
                                so.inc(e.dma_start(out=self.w1q_s, in_=w1q[1][:]), 16)
                                self.so.wait(e)

                    with nc.Block() as blk:
                        gd0, mm0, vv0, ld0, so0 = (self.gd.n, self.mm.n,
                                                   self.vv.n, self.ld.n,
                                                   self.so.n)

                        w2_tgt = {}

                        @blk.gpsimd
                        def _(e, ex=ex, mm0=mm0):
                            for tc in range(2):
                                for hp in range(16):
                                    li = tc * 16 + hp
                                    if li >= 3:
                                        e.wait_ge(self.mm.h, mm0 + (li - 2) * 2)
                                    s = self.w2s[li % 3]
                                    s.inc(e.dma_start(
                                        out=w2kt[li % 3][:].rearrange(
                                            "p (a c) -> p a c", c=1024),
                                        in_=self.w2[ex, hp * 256:(hp + 1) * 256, :]
                                        .rearrange("(a p) c -> p a c", p=128)), 16)
                                    w2_tgt[li] = s.n

                        gi_tgt = {}

                        @blk.sync
                        def _(e, p=p, ex=ex, vv0=vv0, so0=so0):
                            e.wait_ge(self.so.h, so0)
                            for tc in range(2):
                                for dm in range(8):
                                    va = tc * 8 + dm
                                    if va >= 2:
                                        e.wait_ge(self.vv.h, vv0 + va - 1)
                                    s = self.gis[va % 2]
                                    s.inc(e.dma_start(
                                        out=gio[:, (va % 2) * 512:
                                                (va % 2 + 1) * 512],
                                        in_=self.g_s[dm * 128:(dm + 1) * 128,
                                                     p * 1024 + tc * 512:
                                                     p * 1024 + (tc + 1) * 512]),
                                        16)
                                    gi_tgt[va] = s.n
                                e.wait_ge(self.vv.h, vv0 + (tc + 1) * 8)
                                for dm in range(8):
                                    dst = (self.y2snap_s[ex * D + dm * 128:
                                                         ex * D + (dm + 1) * 128,
                                                         p * 1024 + tc * 512:
                                                         p * 1024 + (tc + 1) * 512]
                                           if getattr(self, "y2dbg", 0) else
                                           self.g_s[dm * 128:(dm + 1) * 128,
                                                    p * 1024 + tc * 512:
                                                    p * 1024 + (tc + 1) * 512])
                                    so.inc(e.dma_start(
                                        out=dst,
                                        in_=gout[:, dm * 512:(dm + 1) * 512]), 16)

                        @blk.vector
                        def _(e, mm0=mm0, ld0=ld0, so0=so0):
                            for tc in range(2):
                                e.wait_ge(self.mm.h, mm0 + (tc + 1) * 32)
                                if getattr(self, "gw2", 1):
                                    e.wait_ge(self.so.h, so0 + tc * 8 * 16)
                                elif tc == 1:
                                    e.wait_ge(self.so.h, so0 + 8 * 16)
                                for dm in range(8):
                                    va = tc * 8 + dm
                                    e.wait_ge(self.gis[va % 2].h, gi_tgt[va])
                                    if getattr(self, "y2dbg", 0) == 1:
                                        vv.inc(e.tensor_copy(
                                            gout[:, dm * 512:(dm + 1) * 512],
                                            pM[:, dm * 512:(dm + 1) * 512]))
                                    else:
                                        vv.inc(e.tensor_add(
                                            gout[:, dm * 512:(dm + 1) * 512],
                                            gio[:, (va % 2) * 512:(va % 2 + 1) * 512],
                                            pM[:, dm * 512:(dm + 1) * 512]))

                        @blk.tensor
                        def _(e, gd0=gd0, vv0=vv0):
                            for tc in range(2):
                                if tc == 1:
                                    e.wait_ge(self.vv.h, vv0 + 8)
                                for ht in range(32):
                                    hp, hi = ht // 2, ht % 2
                                    li = tc * 16 + hp
                                    if hi == 0 and hp % 2 == 0:
                                        lj = min(li + 1, tc * 16 + 15)
                                        e.wait_ge(self.w2s[li % 3].h, w2_tgt[li])
                                        e.wait_ge(self.w2s[lj % 3].h, w2_tgt[lj])
                                    for dm in range(8):
                                        ins = e.matmul(
                                            pM[:, dm * 512:(dm + 1) * 512],
                                            w2kt[li % 3][:, hi * 1024 + dm * 128:
                                                         hi * 1024 + (dm + 1) * 128],
                                            h1[:, (ht * 2 + tc) * 512:
                                               (ht * 2 + tc + 1) * 512],
                                            start=(ht == 0), stop=(ht == 31))
                                    mm.inc(ins)

    def ph_f(self):
        nc, ld, mm, vv, aa, so = (self.nc, self.ld, self.mm, self.vv,
                                  self.aa, self.so)
        self._phase_fence()
        with ExitStack() as ctx:
            sb = lambda n, s, d=F32: ctx.enter_context(nc.sbuf_tensor(n, s, d))
            gbuf = sb("fg", [128, 8 * 512])
            sb0 = sb("fs0", [128, 8 * 512])
            sb1 = sb("fs1", [128, 8 * 512])
            xf = sb("fx", [128, 4 * 1024])
            gsb = sb("fgs", [128, 8 * 512])
            gtok = sb("fgt", [128, 4 * 1024])
            tmpb = sb("ftmp", [128, 4 * 1024])
            outb = sb("fout", [128, 4 * 1024])
            pF = ctx.enter_context(nc.psum_tensor("pF", [128, 4096], F32))

            for oc in range(4):
                with nc.Block() as blk:
                    @blk.sync
                    def _(e, oc=oc):
                        for dm in range(8):
                            ld.inc(e.dma_start(
                                out=gbuf[:, dm * 512:(dm + 1) * 512],
                                in_=self.g_s[dm * 128:(dm + 1) * 128,
                                             oc * 512:(oc + 1) * 512]), 16)
                            ld.inc(e.dma_start(
                                out=sb0[:, dm * 512:(dm + 1) * 512],
                                in_=self.sT_s[dm * 128:(dm + 1) * 128,
                                              oc * 512:(oc + 1) * 512]), 16)
                            ld.inc(e.dma_start(
                                out=sb1[:, dm * 512:(dm + 1) * 512],
                                in_=self.sT_s[dm * 128:(dm + 1) * 128,
                                              TOK + oc * 512:
                                              TOK + (oc + 1) * 512]), 16)
                        for tt in range(4):
                            ld.inc(e.dma_start(
                                out=xf[:, tt * 1024:(tt + 1) * 1024],
                                in_=self.x_own[oc * 512 + tt * 128:
                                               oc * 512 + (tt + 1) * 128, :]), 16)

                with nc.Block() as blk:
                    @blk.scalar
                    def _(e):
                        self.ld.wait(e)
                        aa.inc(e.activation(gsb[:], gbuf[:], AF.Silu))

                with nc.Block() as blk:
                    @blk.tensor
                    def _(e):
                        for dm in range(8):
                            for tt in range(4):
                                ins = e.transpose(
                                    pF[:, tt * 1024 + dm * 128:
                                       tt * 1024 + (dm + 1) * 128],
                                    gsb[:, dm * 512 + tt * 128:
                                        dm * 512 + (tt + 1) * 128],
                                    self.ident_sb[:])
                        mm.inc(ins)

                with nc.Block() as blk:
                    @blk.vector
                    def _(e):
                        for tt in range(4):
                            vv.inc(e.tensor_copy(
                                gtok[:, tt * 1024:(tt + 1) * 1024],
                                pF[:, tt * 1024:(tt + 1) * 1024]))

                for cand, sbc, outd in ((0, sb0, self.out_c0),
                                        (1, sb1, self.out_c1)):
                    with nc.Block() as blk:
                        @blk.tensor
                        def _(e, sbc=sbc):
                            for dm in range(8):
                                for tt in range(4):
                                    ins = e.transpose(
                                        pF[:, tt * 1024 + dm * 128:
                                           tt * 1024 + (dm + 1) * 128],
                                        sbc[:, dm * 512 + tt * 128:
                                            dm * 512 + (tt + 1) * 128],
                                        self.ident_sb[:])
                            mm.inc(ins)

                    with nc.Block() as blk:
                        so_t = self.so.n

                        @blk.vector
                        def _(e, so_t=so_t):
                            e.wait_ge(self.so.h, so_t)
                            for tt in range(4):
                                e.tensor_mul(tmpb[:, tt * 1024:(tt + 1) * 1024],
                                             pF[:, tt * 1024:(tt + 1) * 1024],
                                             gtok[:, tt * 1024:(tt + 1) * 1024])
                            e.drain()
                            for tt in range(4):
                                vv.inc(e.tensor_add(
                                    outb[:, tt * 1024:(tt + 1) * 1024],
                                    tmpb[:, tt * 1024:(tt + 1) * 1024],
                                    xf[:, tt * 1024:(tt + 1) * 1024]))

                        @blk.sync
                        def _(e, oc=oc, outd=outd):
                            self.vv.wait(e)
                            for tt in range(4):
                                so.inc(e.dma_start(
                                    out=outd[oc * 512 + tt * 128:
                                             oc * 512 + (tt + 1) * 128, :],
                                    in_=outb[:, tt * 1024:(tt + 1) * 1024]), 16)

    def ph_fini(self):
        nc = self.nc
        if self.dbg is not None:
            for di, (dap, name, rows, cols) in enumerate(self.dbg):
                src = getattr(self, name)
                with ExitStack() as ctx:
                    dbuf = ctx.enter_context(
                        nc.sbuf_tensor(f"dbgbuf{di}", [128, cols], src.dtype))
                    for rb in range((rows + 127) // 128):
                        r0 = rb * 128
                        r1 = min(rows, r0 + 128)
                        nr = r1 - r0
                        with nc.Block() as blk:
                            @blk.sync
                            def _(e, r0=r0, nr=nr):
                                self.so.wait(e)
                                self.ld.inc(e.dma_start(
                                    out=dbuf[0:nr, :],
                                    in_=src[r0:r0 + nr, 0:cols]), 16)
                                self.ld.wait(e)
                                self.so.inc(e.dma_start(
                                    out=dap[r0:r0 + nr, :],
                                    in_=dbuf[0:nr, :]), 16)
        with nc.Block() as blk:
            @blk.sync
            def _(e):
                self.so.wait(e)


_NC_CACHE = {}


def _get_nc(upto=99):
    if upto not in _NC_CACHE:
        _NC_CACHE[upto] = MK(upto=upto).build()
    return _NC_CACHE[upto]


def _pcols(v):
    n = v.shape[0] // 128
    return np.ascontiguousarray(v.reshape(n, 128).T)


def _pcols2(m):
    n = m.shape[0] // 128
    c = m.shape[1]
    return np.ascontiguousarray(m.reshape(n, 128, c).transpose(1, 0, 2).reshape(128, n * c))


def make_in_maps(inputs):
    x = np.ascontiguousarray(inputs["L0_attn_x"], dtype=np.float32)
    base = dict(
        ln_g=_pcols(inputs["L0_attn_ln_g"]),
        ln_b=_pcols(inputs["L0_attn_ln_b"]),
        w_in=np.ascontiguousarray(inputs["L0_attn_W_in"]),
        conv_w=_pcols2(inputs["L0_attn_conv_w"].reshape(DI, 4)),
        conv_b=_pcols(inputs["L0_attn_conv_b"]),
        w_xp=np.ascontiguousarray(inputs["L0_attn_W_xp"]),
        w_dt=np.ascontiguousarray(inputs["L0_attn_W_dt"]),
        dt_bias=_pcols(inputs["L0_attn_dt_bias"]),
        a_log=_pcols2(inputs["L0_attn_A_log"]),
        d_skip=_pcols(inputs["L0_attn_D_skip"]),
        w_out=np.ascontiguousarray(inputs["L0_attn_W_out"]),
        wg=np.ascontiguousarray(inputs["L0_ffn_Wg"]),
        bg_bc=np.ascontiguousarray(
            np.broadcast_to(inputs["L0_ffn_bg"].reshape(1, E), (128, E))),
        w1=np.ascontiguousarray(inputs["L0_ffn_W1"]),
        w2=np.ascontiguousarray(inputs["L0_ffn_W2"]),
        w3=np.ascontiguousarray(inputs["L0_ffn_W3"]),
        ident=np.eye(128, dtype=np.float32),
        ones_f=np.ones((1, 128), dtype=np.float32),
    )
    in_maps = []
    for b in range(B):
        for j in range(2):
            m = dict(base)
            m["x_full"] = np.ascontiguousarray(x[b])
            m["x_own"] = np.ascontiguousarray(x[b, j * TOK:(j + 1) * TOK])
            in_maps.append(m)
    return in_maps


def kernel(**inputs):
    nc = _get_nc(99)
    in_maps = make_in_maps(inputs)
    res = run_bass_kernel_spmd(nc, in_maps, core_ids=list(range(8)))
    out = np.empty((B, L, D), np.float32)
    for b in range(B):
        for j in range(2):
            r = res.results[b * 2 + j]
            out[b, j * TOK:(j + 1) * TOK] = r["out_c0" if j == 0 else "out_c1"]
    return out


# version 07
# speedup vs baseline: 1.0841x; 1.0841x over previous
import numpy as np
import ml_dtypes
from contextlib import ExitStack

import concourse.bass as bass
import concourse.mybir as mybir
from concourse.bass_utils import run_bass_kernel_spmd

F32 = mybir.dt.float32
BF16 = mybir.dt.bfloat16
AF = mybir.ActivationFunctionType
ALU = mybir.AluOpType
AX = mybir.AxisListType

B, L, D, DI, NST, RNK, HF, E = 4, 4096, 1024, 2048, 16, 64, 4096, 8
TOK = L // 2
NCH = 8
TC = L // NCH
EPS = 1e-5
BF16NP = ml_dtypes.bfloat16


class Sem:
    def __init__(self, nc, ctx, name):
        self.h = ctx.enter_context(nc.semaphore(name))
        self.n = 0

    def inc(self, instr, k=1):
        self.n += k
        return instr.then_inc(self.h, k)

    def wait(self, eng, at=None):
        eng.wait_ge(self.h, self.n if at is None else at)


class MK:
    def __init__(self, upto=99):
        self.upto = upto
        nc = bass.Bass("TRN2", target_bir_lowering=False)
        self.nc = nc
        dt = nc.dram_tensor
        ei, eo, it = "ExternalInput", "ExternalOutput", "Internal"
        self.x_full = dt("x_full", [L, D], F32, kind=ei).ap()
        self.x_own = dt("x_own", [TOK, D], F32, kind=ei).ap()
        self.ln_g = dt("ln_g", [128, 8], F32, kind=ei).ap()
        self.ln_b = dt("ln_b", [128, 8], F32, kind=ei).ap()
        self.w_in = dt("w_in", [D, 2 * DI], F32, kind=ei).ap()
        self.conv_w = dt("conv_w", [128, 64], F32, kind=ei).ap()
        self.conv_b = dt("conv_b", [128, 16], F32, kind=ei).ap()
        self.w_xp = dt("w_xp", [DI, 96], F32, kind=ei).ap()
        self.w_dt = dt("w_dt", [RNK, DI], F32, kind=ei).ap()
        self.dt_bias = dt("dt_bias", [128, 16], F32, kind=ei).ap()
        self.a_log = dt("a_log", [128, 256], F32, kind=ei).ap()
        self.d_skip = dt("d_skip", [128, 16], F32, kind=ei).ap()
        self.w_out = dt("w_out", [DI, D], F32, kind=ei).ap()
        self.wg = dt("wg", [D, E], F32, kind=ei).ap()
        self.bg_bc = dt("bg_bc", [128, E], F32, kind=ei).ap()
        self.w1 = dt("w1", [E, D, HF], F32, kind=ei).ap()
        self.w2 = dt("w2", [E, HF, D], F32, kind=ei).ap()
        self.w3 = dt("w3", [E, D, HF], F32, kind=ei).ap()
        self.ident = dt("ident", [128, 128], F32, kind=ei).ap()
        self.ones_f = dt("ones_f", [1, 128], F32, kind=ei).ap()
        self.out_c0 = dt("out_c0", [TOK, D], F32, kind=eo).ap()
        self.out_c1 = dt("out_c1", [TOK, D], F32, kind=eo).ap()
        self.xnT_s = dt("xnT_s", [D, L], F32, kind=it).ap()
        self.xnTo_s = dt("xnTo_s", [D, TOK], BF16, kind=it).ap()
        self.xiT_s = dt("xiT_s", [DI, L], F32, kind=it).ap()
        self.szT_s = dt("szT_s", [DI, L], BF16, kind=it).ap()
        self.xcT_s = dt("xcT_s", [DI, L], F32, kind=it).ap()
        self.xcdT_s = dt("xcdT_s", [DI, L], BF16, kind=it).ap()
        self.dltT_s = dt("dltT_s", [DI, L], BF16, kind=it).ap()
        self.bmcm_s = dt("bmcm_s", [2 * NST, L], F32, kind=it).ap()
        self.sT_s = dt("sT_s", [D, L], F32, kind=it).ap()
        self.g_s = dt("g_s", [D, TOK], F32, kind=it).ap()
        self.xse_s = dt("xse_s", [128, 8192], BF16, kind=it).ap()
        self.h1_s = dt("h1_s", [128, 32768], BF16, kind=it).ap()
        self.w1q_s = dt("w1q_s", [128, 4096], BF16, kind=it).ap()
        self.gsnap_s = dt("gsnap_s", [8 * D, TOK], F32, kind=it).ap()
        self.y2snap_s = dt("y2snap_s", [8 * D, TOK], F32, kind=it).ap()
        self.wt_s = dt("wt_s", [E, TOK], F32, kind=it).ap()
        self.dbg = None
        self.dbg_src = None

    def add_debug(self, *specs):
        self.dbg = []
        for i, (name, rows, cols) in enumerate(specs):
            ap = getattr(self, name)
            self.dbg.append((self.nc.dram_tensor(
                f"dbg{i}", [rows, cols], ap.dtype, kind="ExternalOutput").ap(),
                name, rows, cols))

    def build(self):
        nc = self.nc
        with ExitStack() as ctx:
            self.ld = Sem(nc, ctx, "ld")
            self.gd = Sem(nc, ctx, "gd")
            self.mm = Sem(nc, ctx, "mm")
            self.vv = Sem(nc, ctx, "vv")
            self.aa = Sem(nc, ctx, "aa")
            self.so = Sem(nc, ctx, "so")
            self.wos = [Sem(nc, ctx, f"wos{i}") for i in range(2)]
            self.wqs = [Sem(nc, ctx, f"wqs{i}") for i in range(2)]
            self.w2s = [Sem(nc, ctx, f"w2s{i}") for i in range(4)]
            self.gis = [Sem(nc, ctx, f"gis{i}") for i in range(2)]
            self.brs = [Sem(nc, ctx, f"brs{i}") for i in range(2)]
            self.pp = Sem(nc, ctx, "pp")
            sb = lambda name, shape, dtp=F32: ctx.enter_context(
                nc.sbuf_tensor(name, shape, dtp))
            self.ident_sb = sb("ident_sb", [128, 128])
            self.ones_sb = sb("ones_sb", [1, 128])
            self.stats = sb("stats", [128, 32 * 4])
            self.stats_o = sb("stats_o", [128, 16 * 2])
            self.lng_sb = sb("lng_sb", [128, 8])
            self.lnb_sb = sb("lnb_sb", [128, 8])
            self.cw_sb = sb("cw_sb", [128, 16 * 4])
            self.cb_sb = sb("cb_sb", [128, 16])
            self.dtb_sb = sb("dtb_sb", [128, 16])
            self.dsk_sb = sb("dsk_sb", [128, 16])
            self.a_sb = sb("a_sb", [128, 16 * NST])
            self.cst = sb("cst", [128, 8])

            self.ph_consts()
            if self.upto >= 1:
                self.ph_a0()
            if self.upto >= 2:
                self.ph_a()
            if self.upto >= 3:
                self.ph_a_own()
            if self.upto >= 4:
                self.ph_b()
            if self.upto >= 5:
                self.ph_s()
            if self.upto >= 6:
                self.ph_moe()
            if self.upto >= 7:
                self.ph_f()
            self.ph_fini()
        return nc

    def _phase_fence(self):
        nc, so = self.nc, self.so
        with nc.Block() as blk:
            @blk.sync
            def _(e):
                so.wait(e)

    def ph_consts(self):
        nc, ld, aa, vv = self.nc, self.ld, self.aa, self.vv
        with nc.Block() as blk:
            @blk.sync
            def _(e):
                ld.inc(e.dma_start(out=self.ident_sb[:], in_=self.ident), 16)
                ld.inc(e.dma_start(out=self.ones_sb[:], in_=self.ones_f), 16)
                ld.inc(e.dma_start(out=self.lng_sb[:], in_=self.ln_g), 16)
                ld.inc(e.dma_start(out=self.lnb_sb[:], in_=self.ln_b), 16)
                ld.inc(e.dma_start(out=self.cw_sb[:], in_=self.conv_w), 16)
                ld.inc(e.dma_start(out=self.cb_sb[:], in_=self.conv_b), 16)
                ld.inc(e.dma_start(out=self.dtb_sb[:], in_=self.dt_bias), 16)
                ld.inc(e.dma_start(out=self.dsk_sb[:], in_=self.d_skip), 16)
                ld.inc(e.dma_start(out=self.a_sb[:], in_=self.a_log), 16)

            @blk.scalar
            def _(e):
                ld.wait(e)
                aa.inc(e.activation(self.a_sb[:], self.a_sb[:], AF.Exp))

            @blk.vector
            def _(e):
                for i, v in enumerate([1.0 / D, -0.5, 1.0 + EPS, -EPS,
                                       -1.0, 1.0, 0.5]):
                    e.memset(self.cst[:, i:i + 1], v)
                aa.wait(e)
                vv.inc(e.tensor_scalar_mul(self.a_sb[:], self.a_sb[:], -1.0))

    def ph_a0(self):
        nc, ld, vv, aa = self.nc, self.ld, self.vv, self.aa
        with ExitStack() as ctx:
            xt = [ctx.enter_context(nc.sbuf_tensor(f"a0x{i}", [128, D], F32))
                  for i in range(8)]
            q = [ctx.enter_context(nc.sbuf_tensor(f"a0q{i}", [128, D], F32))
                 for i in range(4)]
            tmp = ctx.enter_context(nc.sbuf_tensor("a0t", [128, 16], F32))
            groups = [("full", g) for g in range(8)] + [("own", g) for g in range(4)]
            with nc.Block() as blk:
                @blk.sync
                def _(e):
                    for i in range(4):
                        ld.inc(e.dma_start(
                            out=xt[i][:],
                            in_=self.x_full[i * 128:(i + 1) * 128, :]), 16)
            for gi, (kind, g) in enumerate(groups):
                bufs = [xt[(gi % 2) * 4 + i] for i in range(4)]
                nbufs = [xt[((gi + 1) % 2) * 4 + i] for i in range(4)]
                ld_t = self.ld.n

                with nc.Block() as blk:
                    @blk.sync
                    def _(e, gi=gi, nbufs=nbufs):
                        if gi + 1 < len(groups):
                            nkind, ng = groups[gi + 1]
                            src = self.x_full if nkind == "full" else self.x_own
                            for i in range(4):
                                t = ng * 4 + i
                                ld.inc(e.dma_start(
                                    out=nbufs[i][:],
                                    in_=src[t * 128:(t + 1) * 128, :]), 16)

                    @blk.vector
                    def _(e, bufs=bufs, ld_t=ld_t):
                        e.wait_ge(self.ld.h, ld_t)
                        for i in range(4):
                            vv.inc(e.reduce_sum(tmp[:, i:i + 1], bufs[i][:],
                                                axis=AX.X))
                        e.drain()
                        for i in range(4):
                            vv.inc(e.tensor_scalar_mul(tmp[:, 4 + i:5 + i],
                                                       tmp[:, i:i + 1], 1.0 / D))
                        e.drain()
                        for i in range(4):
                            vv.inc(e.tensor_mul(tmp[:, 8 + i:9 + i],
                                                tmp[:, 4 + i:5 + i],
                                                tmp[:, 4 + i:5 + i]))
                        e.drain()
                        for i in range(4):
                            vv.inc(e.tensor_scalar(tmp[:, 8 + i:9 + i],
                                                   tmp[:, 8 + i:9 + i],
                                                   -1.0, EPS, ALU.mult, ALU.add))

                    @blk.scalar
                    def _(e, bufs=bufs, ld_t=ld_t):
                        e.wait_ge(self.ld.h, ld_t)
                        for i in range(4):
                            aa.inc(e.activation(q[i][:], bufs[i][:], AF.Square))

                with nc.Block() as blk:
                    @blk.vector
                    def _(e):
                        for i in range(4):
                            vv.inc(e.reduce_sum(tmp[:, 12 + i:13 + i], q[i][:],
                                                axis=AX.X))

                with nc.Block() as blk:
                    @blk.scalar
                    def _(e, kind=kind, g=g):
                        stats = self.stats if kind == "full" else self.stats_o
                        ncol = 4 if kind == "full" else 2
                        cc = [(g * 4 + i) * ncol for i in range(4)]
                        for i in range(4):
                            aa.inc(e.copy(stats[:, cc[i]:cc[i] + 1],
                                          tmp[:, 4 + i:5 + i]))
                            aa.inc(e.activation(tmp[:, 12 + i:13 + i],
                                                tmp[:, 12 + i:13 + i], AF.Ln,
                                                bias=tmp[:, 8 + i:9 + i],
                                                scale=self.cst[:, 0:1]))
                        e.drain()
                        for i in range(4):
                            aa.inc(e.activation(stats[:, cc[i] + 1:cc[i] + 2],
                                                tmp[:, 12 + i:13 + i], AF.Exp,
                                                scale=self.cst[:, 1:2]))
                        if kind == "full":
                            e.drain()
                            for i in range(4):
                                aa.inc(e.activation(tmp[:, 8 + i:9 + i],
                                                    stats[:, cc[i] + 1:cc[i] + 2],
                                                    AF.Square))
                            e.drain()
                            for i in range(4):
                                aa.inc(e.activation(tmp[:, 8 + i:9 + i],
                                                    tmp[:, 8 + i:9 + i], AF.Ln,
                                                    bias=self.cst[:, 2:3],
                                                    scale=self.cst[:, 3:4]))
                            e.drain()
                            for i in range(4):
                                aa.inc(e.activation(tmp[:, 8 + i:9 + i],
                                                    tmp[:, 8 + i:9 + i], AF.Exp,
                                                    scale=self.cst[:, 1:2]))

                with nc.Block() as blk:
                    @blk.vector
                    def _(e, kind=kind, g=g):
                        if kind != "full":
                            return
                        for i in range(4):
                            c0 = (g * 4 + i) * 4
                            vv.inc(e.tensor_mul(self.stats[:, c0 + 2:c0 + 3],
                                                self.stats[:, c0 + 1:c0 + 2],
                                                tmp[:, 8 + i:9 + i]))

    def ph_a(self):
        nc, ld, gd, mm, vv, aa, so = (self.nc, self.ld, self.gd, self.mm,
                                      self.vv, self.aa, self.so)
        with ExitStack() as ctx:
            sb = lambda n, s, d=F32: ctx.enter_context(nc.sbuf_tensor(n, s, d))
            w_in_bf = sb("w_in_bf", [128, 8 * 4096], BF16)
            xt = [sb(f"ax{i}", [128, D]) for i in range(8)]
            xn = [sb(f"axn{i}", [128, D]) for i in range(4)]
            z2 = [sb(f"az2{i}", [128, D]) for i in range(4)]
            xn2t = sb("axn2t", [128, 8 * TC], BF16)
            xnt = sb("axnt", [128, 8 * TC])
            ev = sb("aev", [128, 8 * TC])
            evz = sb("aevz", [128, 8 * TC], BF16)
            pA = ctx.enter_context(nc.psum_tensor("pA", [128, 8 * TC], F32))

            with nc.Block() as blk:
                @blk.gpsimd
                def _(e):
                    for k in range(8):
                        gd.inc(e.dma_start(
                            out=w_in_bf[:, k * 4096:(k + 1) * 4096],
                            in_=self.w_in[k * 128:(k + 1) * 128, :]), 16)

                @blk.sync
                def _(e):
                    for i in range(4):
                        ld.inc(e.dma_start(
                            out=xt[i][:],
                            in_=self.x_full[i * 128:(i + 1) * 128, :]), 16)

            for c in range(NCH):
                bufs = [xt[(c % 2) * 4 + i] for i in range(4)]
                nbufs = [xt[((c + 1) % 2) * 4 + i] for i in range(4)]
                ld_t = self.ld.n
                gd_t = self.gd.n

                with nc.Block() as blk:
                    @blk.sync
                    def _(e, c=c, nbufs=nbufs):
                        if c + 1 < NCH:
                            for i in range(4):
                                t = (c + 1) * 4 + i
                                ld.inc(e.dma_start(
                                    out=nbufs[i][:],
                                    in_=self.x_full[t * 128:(t + 1) * 128, :]), 16)

                    @blk.vector
                    def _(e, c=c, bufs=bufs, ld_t=ld_t):
                        e.wait_ge(self.ld.h, ld_t)
                        st = self.stats
                        for i in range(4):
                            t = c * 4 + i
                            vv.inc(e.tensor_scalar(
                                xn[i][:], bufs[i][:], st[:, 4 * t:4 * t + 1],
                                st[:, 4 * t + 1:4 * t + 2],
                                ALU.subtract, ALU.mult))
                            vv.inc(e.tensor_scalar(
                                z2[i][:], bufs[i][:], st[:, 4 * t:4 * t + 1],
                                st[:, 4 * t + 2:4 * t + 3],
                                ALU.subtract, ALU.mult))

                with nc.Block() as blk:
                    @blk.tensor
                    def _(e):
                        for dcol in range(8):
                            for tt in range(4):
                                ins = e.transpose(
                                    pA[:, dcol * TC + tt * 128:
                                       dcol * TC + (tt + 1) * 128],
                                    z2[tt][:, dcol * 128:(dcol + 1) * 128],
                                    self.ident_sb[:])
                        mm.inc(ins)

                with nc.Block() as blk:
                    @blk.vector
                    def _(e):
                        for dcol in range(8):
                            vv.inc(e.tensor_scalar(
                                xn2t[:, dcol * TC:(dcol + 1) * TC],
                                pA[:, dcol * TC:(dcol + 1) * TC],
                                self.lng_sb[:, dcol:dcol + 1],
                                self.lnb_sb[:, dcol:dcol + 1],
                                ALU.mult, ALU.add))

                with nc.Block() as blk:
                    @blk.tensor
                    def _(e):
                        for dcol in range(8):
                            for tt in range(4):
                                ins = e.transpose(
                                    pA[:, dcol * TC + tt * 128:
                                       dcol * TC + (tt + 1) * 128],
                                    xn[tt][:, dcol * 128:(dcol + 1) * 128],
                                    self.ident_sb[:])
                        mm.inc(ins)

                with nc.Block() as blk:
                    so_t = self.so.n

                    @blk.vector
                    def _(e, so_t=so_t):
                        e.wait_ge(self.so.h, so_t)
                        for dcol in range(8):
                            vv.inc(e.tensor_copy(
                                xnt[:, dcol * TC:(dcol + 1) * TC],
                                pA[:, dcol * TC:(dcol + 1) * TC]))

                    @blk.sync
                    def _(e, c=c):
                        self.vv.wait(e)
                        for dcol in range(8):
                            so.inc(e.dma_start(
                                out=self.xnT_s[dcol * 128:(dcol + 1) * 128,
                                               c * TC:(c + 1) * TC],
                                in_=xnt[:, dcol * TC:(dcol + 1) * TC]), 16)

                for grp in range(4):
                    o0 = grp * 8
                    with nc.Block() as blk:
                        @blk.tensor
                        def _(e, o0=o0, gd_t=gd_t, c=c, grp=grp):
                            if c == 0 and grp == 0:
                                e.wait_ge(self.gd.h, gd_t)
                            for oi in range(8):
                                o = o0 + oi
                                for k in range(8):
                                    ins = e.matmul(
                                        pA[:, oi * TC:(oi + 1) * TC],
                                        w_in_bf[:, k * 4096 + o * 128:
                                                k * 4096 + (o + 1) * 128],
                                        xn2t[:, k * TC:(k + 1) * TC],
                                        start=(k == 0), stop=(k == 7))
                            mm.inc(ins)

                    with nc.Block() as blk:
                        so_t = self.so.n
                        if o0 < 16:
                            @blk.vector
                            def _(e, so_t=so_t):
                                e.wait_ge(self.so.h, so_t)
                                for oi in range(8):
                                    vv.inc(e.tensor_copy(
                                        ev[:, oi * TC:(oi + 1) * TC],
                                        pA[:, oi * TC:(oi + 1) * TC]))

                            @blk.sync
                            def _(e, o0=o0, c=c):
                                self.vv.wait(e)
                                for oi in range(8):
                                    o = o0 + oi
                                    so.inc(e.dma_start(
                                        out=self.xiT_s[o * 128:(o + 1) * 128,
                                                       c * TC:(c + 1) * TC],
                                        in_=ev[:, oi * TC:(oi + 1) * TC]), 16)
                        else:
                            @blk.scalar
                            def _(e, so_t=so_t):
                                e.wait_ge(self.so.h, so_t)
                                for oi in range(8):
                                    aa.inc(e.activation(
                                        evz[:, oi * TC:(oi + 1) * TC],
                                        pA[:, oi * TC:(oi + 1) * TC], AF.Silu))

                            @blk.sync
                            def _(e, o0=o0, c=c):
                                self.aa.wait(e)
                                for oi in range(8):
                                    o = o0 + oi - 16
                                    so.inc(e.dma_start(
                                        out=self.szT_s[o * 128:(o + 1) * 128,
                                                       c * TC:(c + 1) * TC],
                                        in_=evz[:, oi * TC:(oi + 1) * TC]), 16)

    def ph_a_own(self):
        nc, ld, mm, vv, so = self.nc, self.ld, self.mm, self.vv, self.so
        with ExitStack() as ctx:
            sb = lambda n, s, d=F32: ctx.enter_context(nc.sbuf_tensor(n, s, d))
            xt = [sb(f"ox{i}", [128, D]) for i in range(8)]
            xn = [sb(f"oxn{i}", [128, D]) for i in range(4)]
            xnt_bf = sb("oxnt_bf", [128, 8 * TC], BF16)
            xnt_f = sb("oxnt_f", [128, 8 * TC])
            pA = ctx.enter_context(nc.psum_tensor("poA", [128, 8 * TC], F32))

            with nc.Block() as blk:
                @blk.sync
                def _(e):
                    for i in range(4):
                        ld.inc(e.dma_start(
                            out=xt[i][:],
                            in_=self.x_own[i * 128:(i + 1) * 128, :]), 16)

            for c in range(4):
                bufs = [xt[(c % 2) * 4 + i] for i in range(4)]
                nbufs = [xt[((c + 1) % 2) * 4 + i] for i in range(4)]
                ld_t = self.ld.n
                with nc.Block() as blk:
                    @blk.sync
                    def _(e, c=c, nbufs=nbufs):
                        if c + 1 < 4:
                            for i in range(4):
                                t = (c + 1) * 4 + i
                                ld.inc(e.dma_start(
                                    out=nbufs[i][:],
                                    in_=self.x_own[t * 128:(t + 1) * 128, :]), 16)

                    @blk.vector
                    def _(e, c=c, bufs=bufs, ld_t=ld_t):
                        e.wait_ge(self.ld.h, ld_t)
                        st = self.stats_o
                        for i in range(4):
                            t = c * 4 + i
                            vv.inc(e.tensor_scalar(
                                xn[i][:], bufs[i][:], st[:, 2 * t:2 * t + 1],
                                st[:, 2 * t + 1:2 * t + 2],
                                ALU.subtract, ALU.mult))

                with nc.Block() as blk:
                    @blk.tensor
                    def _(e):
                        for dcol in range(8):
                            for tt in range(4):
                                ins = e.transpose(
                                    pA[:, dcol * TC + tt * 128:
                                       dcol * TC + (tt + 1) * 128],
                                    xn[tt][:, dcol * 128:(dcol + 1) * 128],
                                    self.ident_sb[:])
                        mm.inc(ins)

                with nc.Block() as blk:
                    so_t = self.so.n

                    @blk.vector
                    def _(e, so_t=so_t):
                        e.wait_ge(self.so.h, so_t)
                        for dcol in range(8):
                            vv.inc(e.tensor_copy(
                                xnt_bf[:, dcol * TC:(dcol + 1) * TC],
                                pA[:, dcol * TC:(dcol + 1) * TC]))
                            vv.inc(e.tensor_copy(
                                xnt_f[:, dcol * TC:(dcol + 1) * TC],
                                pA[:, dcol * TC:(dcol + 1) * TC]))

                    @blk.sync
                    def _(e, c=c):
                        self.vv.wait(e)
                        for dcol in range(8):
                            so.inc(e.dma_start(
                                out=self.xnTo_s[dcol * 128:(dcol + 1) * 128,
                                                c * TC:(c + 1) * TC],
                                in_=xnt_bf[:, dcol * TC:(dcol + 1) * TC]), 16)
                            so.inc(e.dma_start(
                                out=self.g_s[dcol * 128:(dcol + 1) * 128,
                                             c * TC:(c + 1) * TC],
                                in_=xnt_f[:, dcol * TC:(dcol + 1) * TC]), 16)

    def ph_b(self):
        bparts = getattr(self, "bparts", {"conv", "xdbl", "delta", "delta_sp"})
        nc, ld, gd, mm, vv, aa, so = (self.nc, self.ld, self.gd, self.mm,
                                      self.vv, self.aa, self.so)
        self._phase_fence()
        with ExitStack() as ctx:
            sb = lambda n, s, d=F32: ctx.enter_context(nc.sbuf_tensor(n, s, d))
            xib = [sb(f"bxi{i}", [128, L + 4]) for i in range(2)]
            acc = [sb(f"bacc{i}", [128, L]) for i in range(2)]
            xc = sb("bxc", [128, L])
            xc_bf = sb("bxcbf", [128, L], BF16)
            xcd_bf = sb("bxcdbf", [128, L], BF16)
            w_xp_bf = sb("bwxp", [128, 16 * 96], BF16)
            dlo_bf = sb("bdlo", [64, L], BF16)
            bmcm = sb("bbmcm", [32, L])

            with ExitStack() as pctx:
                pX = pctx.enter_context(nc.psum_tensor("pX", [96, L], F32))

                with nc.Block() as blk:
                    @blk.gpsimd
                    def _(e):
                        gd.inc(e.dma_start(
                            out=w_xp_bf[:].rearrange("p (a c) -> p a c", c=96),
                            in_=self.w_xp.rearrange("(a p) c -> p a c", p=128)), 16)

                    @blk.sync
                    def _(e):
                        ld.inc(e.dma_start(out=xib[0][:, 4:4 + L],
                                           in_=self.xiT_s[0:128, :]), 16)

                for t in range(16):
                    buf = xib[t % 2]
                    nbuf = xib[(t + 1) % 2]
                    ld_t = self.ld.n
                    gd_t = self.gd.n
                    so_t = self.so.n
                    with nc.Block() as blk:
                        @blk.sync
                        def _(e, t=t, nbuf=nbuf):
                            if t + 1 < 16:
                                ld.inc(e.dma_start(
                                    out=nbuf[:, 4:4 + L],
                                    in_=self.xiT_s[(t + 1) * 128:(t + 2) * 128, :]),
                                    16)

                        @blk.vector
                        def _(e, t=t, buf=buf, ld_t=ld_t, so_t=so_t):
                            e.wait_ge(self.ld.h, ld_t)
                            e.memset(buf[:, 1:4], 0.0)
                            e.drain()
                            vv.inc(e.tensor_scalar_mul(
                                acc[0][:], buf[:, 1:1 + L],
                                self.cw_sb[:, 4 * t:4 * t + 1]))
                            e.drain()
                            vv.inc(e.scalar_tensor_tensor(
                                acc[1][:], buf[:, 2:2 + L],
                                self.cw_sb[:, 4 * t + 1:4 * t + 2], acc[0][:],
                                ALU.mult, ALU.add))
                            e.drain()
                            vv.inc(e.scalar_tensor_tensor(
                                acc[0][:], buf[:, 3:3 + L],
                                self.cw_sb[:, 4 * t + 2:4 * t + 3], acc[1][:],
                                ALU.mult, ALU.add))
                            e.drain()
                            vv.inc(e.scalar_tensor_tensor(
                                acc[1][:], buf[:, 4:4 + L],
                                self.cw_sb[:, 4 * t + 3:4 * t + 4], acc[0][:],
                                ALU.mult, ALU.add))
                            self._bconv_mark = self.vv.n
                            e.wait_ge(self.aa.h, self.aa.n + 1)
                            e.wait_ge(self.so.h, so_t)
                            vv.inc(e.tensor_copy(xc_bf[:], xc[:]))
                            vv.inc(e.tensor_scalar_mul(
                                xcd_bf[:], xc[:], self.dsk_sb[:, t:t + 1]))

                        @blk.scalar
                        def _(e, t=t, so_t=so_t):
                            e.wait_ge(self.vv.h, self._bconv_mark)
                            e.wait_ge(self.so.h, so_t)
                            aa.inc(e.activation(xc[:], acc[1][:], AF.Silu,
                                                bias=self.cb_sb[:, t:t + 1]))

                        @blk.sync
                        def _(e, t=t):
                            self.vv.wait(e)
                            so.inc(e.dma_start(
                                out=self.xcT_s[t * 128:(t + 1) * 128, :],
                                in_=xc[:]), 16)
                            so.inc(e.dma_start(
                                out=self.xcdT_s[t * 128:(t + 1) * 128, :],
                                in_=xcd_bf[:]), 16)

                        if "xdbl" in bparts:
                            @blk.tensor
                            def _(e, t=t, gd_t=gd_t):
                                if t == 0:
                                    e.wait_ge(self.gd.h, gd_t)
                                self.vv.wait(e)
                                for cc in range(8):
                                    ins = e.matmul(
                                        pX[:, cc * TC:(cc + 1) * TC],
                                        w_xp_bf[:, t * 96:(t + 1) * 96],
                                        xc_bf[:, cc * TC:(cc + 1) * TC],
                                        start=(t == 0), stop=(t == 15))
                                mm.inc(ins)

                if "xdbl" not in bparts:
                    return
                with nc.Block() as blk:
                    @blk.vector
                    def _(e):
                        vv.inc(e.tensor_copy(dlo_bf[:], pX[0:64, :]))
                        vv.inc(e.tensor_copy(bmcm[:], pX[64:96, :]))

                    @blk.sync
                    def _(e):
                        self.vv.wait(e)
                        so.inc(e.dma_start(out=self.bmcm_s, in_=bmcm[:]), 16)

            if "delta" not in bparts:
                return
            with ExitStack() as pctx:
                pD = pctx.enter_context(nc.psum_tensor("pD", [128, L], F32))
                w_dt_bf = ctx.enter_context(
                    nc.sbuf_tensor("bwdt", [64, DI], BF16))
                dl1 = ctx.enter_context(nc.sbuf_tensor("bdl1", [128, L], F32))
                dl2 = ctx.enter_context(nc.sbuf_tensor("bdl2", [128, L], F32))
                dl3 = ctx.enter_context(nc.sbuf_tensor("bdl3", [128, L], BF16))

                with nc.Block() as blk:
                    @blk.gpsimd
                    def _(e):
                        gd.inc(e.dma_start(out=w_dt_bf[:], in_=self.w_dt), 16)

                for t in range(16):
                    gd_t = self.gd.n
                    with nc.Block() as blk:
                        @blk.tensor
                        def _(e, t=t, gd_t=gd_t):
                            if t == 0:
                                e.wait_ge(self.gd.h, gd_t)
                            for cc in range(8):
                                ins = e.matmul(
                                    pD[:, cc * TC:(cc + 1) * TC],
                                    w_dt_bf[:, t * 128:(t + 1) * 128],
                                    dlo_bf[:, cc * TC:(cc + 1) * TC],
                                    start=True, stop=True)
                            mm.inc(ins)

                    if "delta_sp" not in bparts:
                        continue
                    with nc.Block() as blk:
                        so_t = self.so.n

                        @blk.scalar
                        def _(e, t=t):
                            for cc in range(8):
                                aa.inc(e.activation(
                                    dl1[:, cc * TC:(cc + 1) * TC],
                                    pD[:, cc * TC:(cc + 1) * TC], AF.Abs,
                                    bias=self.dtb_sb[:, t:t + 1]))
                            e.drain()
                            aa.inc(e.activation(dl1[:], dl1[:], AF.Exp,
                                                scale=self.cst[:, 4:5]))
                            e.drain()
                            aa.inc(e.activation(dl1[:], dl1[:], AF.Ln, bias=self.cst[:, 5:6]))

                        @blk.vector
                        def _(e, t=t, so_t=so_t):
                            for cc in range(8):
                                vv.inc(e.tensor_scalar(
                                    dl2[:, cc * TC:(cc + 1) * TC],
                                    pD[:, cc * TC:(cc + 1) * TC],
                                    self.dtb_sb[:, t:t + 1], 0.0,
                                    ALU.add, ALU.max))
                            self.aa.wait(e)
                            e.wait_ge(self.so.h, so_t)
                            e.drain()
                            vv.inc(e.tensor_add(dl3[:], dl2[:], dl1[:]))

                        @blk.sync
                        def _(e, t=t):
                            self.vv.wait(e)
                            so.inc(e.dma_start(
                                out=self.dltT_s[t * 128:(t + 1) * 128, :],
                                in_=dl3[:]), 16)

    def ph_s(self):
        nc, ld, gd, mm, vv, aa, so = (self.nc, self.ld, self.gd, self.mm,
                                      self.vv, self.aa, self.so)
        self._phase_fence()
        with ExitStack() as ctx:
            sb = lambda n, s, d=F32: ctx.enter_context(nc.sbuf_tensor(n, s, d))
            dlt = sb("sdlt", [128, 16 * TC], BF16)
            xch = sb("sxch", [128, 8 * TC])
            du = sb("sdu", [128, 16 * TC], BF16)
            Y = sb("sY", [128, 16 * TC])
            da = sb("sda", [128, 8 * TC])
            bca = sb("sbca", [128, 32 * TC], BF16)
            brow = sb("sbrow", [1, 2 * TC])
            ymbuf = sb("symbuf", [128, 4 * TC])
            stbf = sb("sstbf", [128, 8 * TC], BF16)
            hstate = sb("shstate", [128, 256])
            yg = sb("syg", [128, 16 * TC], BF16)
            wo = [sb(f"swo{i}", [128, 1024], BF16) for i in range(2)]
            xntb = sb("sxntb", [128, 8 * TC])
            sout = sb("ssout", [128, 8 * TC])
            pA = ctx.enter_context(nc.psum_tensor("pSA", [128, 8 * TC], F32))

            with nc.Block() as blk:
                @blk.vector
                def _(e):
                    vv.inc(e.memset(hstate[:], 0.0))

            for c in range(NCH):
                ld_t = self.ld.n
                with nc.Block() as blk:
                    @blk.sync
                    def _(e, c=c):
                        for h in range(2):
                            ld.inc(e.dma_start(
                                out=dlt[:, h * 8 * TC:(h + 1) * 8 * TC]
                                .rearrange("p (a t) -> p a t", t=TC),
                                in_=self.dltT_s[h * 1024:(h + 1) * 1024,
                                                c * TC:(c + 1) * TC]
                                .rearrange("(a p) t -> p a t", p=128)), 16)
                        ld.inc(e.dma_start(
                            out=xch[:].rearrange("p (a t) -> p a t", t=TC),
                            in_=self.xcT_s[0:1024, c * TC:(c + 1) * TC]
                            .rearrange("(a p) t -> p a t", p=128)), 16)
                        ld.inc(e.dma_start(
                            out=xntb[:].rearrange("p (a t) -> p a t", t=TC),
                            in_=self.xnT_s[:, c * TC:(c + 1) * TC]
                            .rearrange("(a p) t -> p a t", p=128)), 16)

                    @blk.vector
                    def _(e, c=c):
                        self.ld.wait(e)
                        vv.inc(e.memset(Y[:], 0.0))
                        vv.inc(e.tensor_mul(du[:, 0:8 * TC], dlt[:, 0:8 * TC],
                                            xch[:]))

                with nc.Block() as blk:
                    @blk.sync
                    def _(e, c=c):
                        ld.inc(e.dma_start(
                            out=xch[:].rearrange("p (a t) -> p a t", t=TC),
                            in_=self.xcT_s[1024:2048, c * TC:(c + 1) * TC]
                            .rearrange("(a p) t -> p a t", p=128)), 16)

                    @blk.vector
                    def _(e):
                        self.ld.wait(e)
                        vv.inc(e.tensor_mul(du[:, 8 * TC:16 * TC],
                                            dlt[:, 8 * TC:16 * TC], xch[:]))

                with nc.Block() as blk:
                    mm0, vv0 = self.mm.n, self.vv.n
                    br_tgt = {}

                    @blk.sync
                    def _(e, c=c, mm0=mm0):
                        for g in range(16):
                            if g >= 1:
                                e.wait_ge(self.mm.h, mm0 + g)
                            s = self.brs[g % 2]
                            for i in range(2):
                                r = (g % 8) * 2 + i + (16 if g >= 8 else 0)
                                s.inc(e.dma_start(
                                    out=brow[0:1, i * TC:(i + 1) * TC],
                                    in_=self.bmcm_s[r:r + 1,
                                                    c * TC:(c + 1) * TC]), 16)
                            br_tgt[g] = s.n

                    @blk.tensor
                    def _(e, mm0=mm0, vv0=vv0):
                        for g in range(16):
                            e.wait_ge(self.brs[g % 2].h, br_tgt[g])
                            if g >= 2:
                                e.wait_ge(self.vv.h, vv0 + g - 1)
                            half = (g % 2) * 2
                            for i in range(2):
                                ins = e.matmul(
                                    pA[:, (half + i) * TC:(half + i + 1) * TC],
                                    self.ones_sb[0:1, :],
                                    brow[0:1, i * TC:(i + 1) * TC],
                                    start=True, stop=True)
                            mm.inc(ins)

                    @blk.vector
                    def _(e, mm0=mm0):
                        for g in range(16):
                            e.wait_ge(self.mm.h, mm0 + g + 1)
                            half = (g % 2) * 2
                            for i in range(2):
                                slot = (g % 8) * 2 + i + (16 if g >= 8 else 0)
                                ins = e.tensor_copy(
                                    bca[:, slot * TC:(slot + 1) * TC],
                                    pA[:, (half + i) * TC:(half + i + 1) * TC])
                            vv.inc(ins)

                with nc.Block() as blk:
                    aa0, vv0 = self.aa.n, self.vv.n
                    nbatch = NST * 4

                    @blk.scalar
                    def _(e, aa0=aa0, vv0=vv0):
                        for b in range(nbatch):
                            n, q = b // 4, b % 4
                            if b >= 2:
                                e.wait_ge(self.vv.h, vv0 + b - 1)
                            sl = (b % 2) * 4
                            for i in range(4):
                                t = q * 4 + i
                                ins = e.activation(
                                    da[:, (sl + i) * TC:(sl + i + 1) * TC],
                                    dlt[:, t * TC:(t + 1) * TC], AF.Exp,
                                    scale=self.a_sb[:, t * NST + n:
                                                    t * NST + n + 1])
                            aa.inc(ins)

                    pp0 = self.pp.n

                    @blk.vector
                    def _(e, aa0=aa0, pp0=pp0):
                        for db in range(nbatch // 2):
                            bs = (2 * db, 2 * db + 1)
                            e.wait_ge(self.aa.h, aa0 + 2 * db + 2)
                            for h, b in enumerate(bs):
                                n, q = b // 4, b % 4
                                for i in range(4):
                                    t = q * 4 + i
                                    e.tensor_mul(
                                        stbf[:, (h * 4 + i) * TC:
                                             (h * 4 + i + 1) * TC],
                                        du[:, t * TC:(t + 1) * TC],
                                        bca[:, n * TC:(n + 1) * TC])
                            e.drain()
                            for h, b in enumerate(bs):
                                n, q = b // 4, b % 4
                                sl = (b % 2) * 4
                                for i in range(4):
                                    idx = n * 16 + q * 4 + i
                                    e.tensor_tensor_scan(
                                        xch[:, (h * 4 + i) * TC:
                                            (h * 4 + i + 1) * TC],
                                        da[:, (sl + i) * TC:(sl + i + 1) * TC],
                                        stbf[:, (h * 4 + i) * TC:
                                             (h * 4 + i + 1) * TC],
                                        hstate[:, idx:idx + 1],
                                        ALU.mult, ALU.add)
                            e.drain()
                            for h, b in enumerate(bs):
                                n, q = b // 4, b % 4
                                t0 = q * 4
                                e.tensor_copy(
                                    hstate[:, n * 16 + t0:n * 16 + t0 + 4],
                                    xch[:, (h * 4 + 1) * TC - 1:
                                        (h * 4 + 4) * TC:TC])
                                if b >= 1:
                                    e.wait_ge(self.pp.h, pp0 + b)
                                for i in range(4):
                                    ins = e.tensor_mul(
                                        ymbuf[:, i * TC:(i + 1) * TC],
                                        xch[:, (h * 4 + i) * TC:
                                            (h * 4 + i + 1) * TC],
                                        bca[:, (16 + n) * TC:(17 + n) * TC])
                                vv.inc(ins)

                    @blk.gpsimd
                    def _(e, pp0=pp0):
                        vvb = self.vv.n - nbatch
                        for b in range(nbatch):
                            n, q = b // 4, b % 4
                            t0 = q * 4
                            e.wait_ge(self.vv.h, vvb + b + 1)
                            for i in range(4):
                                t = t0 + i
                                ins = e.tensor_add(
                                    Y[:, t * TC:(t + 1) * TC],
                                    Y[:, t * TC:(t + 1) * TC],
                                    ymbuf[:, i * TC:(i + 1) * TC])
                            self.pp.inc(ins)

                with nc.Block() as blk:
                    @blk.sync
                    def _(e, c=c):
                        for h in range(2):
                            ld.inc(e.dma_start(
                                out=bca[:, h * 8 * TC:(h + 1) * 8 * TC]
                                .rearrange("p (a t) -> p a t", t=TC),
                                in_=self.xcdT_s[h * 1024:(h + 1) * 1024,
                                                c * TC:(c + 1) * TC]
                                .rearrange("(a p) t -> p a t", p=128)), 16)
                            ld.inc(e.dma_start(
                                out=bca[:, (16 + h * 8) * TC:(16 + (h + 1) * 8) * TC]
                                .rearrange("p (a t) -> p a t", t=TC),
                                in_=self.szT_s[h * 1024:(h + 1) * 1024,
                                               c * TC:(c + 1) * TC]
                                .rearrange("(a p) t -> p a t", p=128)), 16)

                with nc.Block() as blk:
                    @blk.vector
                    def _(e, c=c):
                        self.ld.wait(e)
                        self.pp.wait(e)
                        for grp in range(4):
                            if grp:
                                e.drain()
                            for i in range(4):
                                t = grp * 4 + i
                                e.tensor_add(ymbuf[:, i * TC:(i + 1) * TC],
                                             bca[:, t * TC:(t + 1) * TC],
                                             Y[:, t * TC:(t + 1) * TC])
                            e.drain()
                            for i in range(4):
                                t = grp * 4 + i
                                ins = e.tensor_mul(
                                    yg[:, t * TC:(t + 1) * TC],
                                    ymbuf[:, i * TC:(i + 1) * TC],
                                    bca[:, (16 + t) * TC:(17 + t) * TC])
                        vv.inc(ins)

                with nc.Block() as blk:
                    mm0 = self.mm.n
                    wo_tgt = {}

                    @blk.gpsimd
                    def _(e, mm0=mm0):
                        for t in range(16):
                            if t >= 2:
                                e.wait_ge(self.mm.h, mm0 + t - 1)
                            self.wos[t % 2].inc(e.dma_start(
                                out=wo[t % 2][:],
                                in_=self.w_out[t * 128:(t + 1) * 128, :]), 16)
                            wo_tgt[t] = self.wos[t % 2].n

                    @blk.tensor
                    def _(e):
                        for t in range(16):
                            e.wait_ge(self.wos[t % 2].h, wo_tgt[t])
                            for dm in range(8):
                                ins = e.matmul(
                                    pA[:, dm * TC:(dm + 1) * TC],
                                    wo[t % 2][:, dm * 128:(dm + 1) * 128],
                                    yg[:, t * TC:(t + 1) * TC],
                                    start=(t == 0), stop=(t == 15))
                            mm.inc(ins)

                with nc.Block() as blk:
                    so_t = self.so.n

                    @blk.vector
                    def _(e, so_t=so_t):
                        e.wait_ge(self.so.h, so_t)
                        for dm in range(8):
                            vv.inc(e.tensor_add(
                                sout[:, dm * TC:(dm + 1) * TC],
                                pA[:, dm * TC:(dm + 1) * TC],
                                xntb[:, dm * TC:(dm + 1) * TC]))

                    @blk.sync
                    def _(e, c=c):
                        self.vv.wait(e)
                        so.inc(e.dma_start(
                            out=self.sT_s[:, c * TC:(c + 1) * TC]
                            .rearrange("(a p) t -> p a t", p=128),
                            in_=sout[:].rearrange("p (a t) -> p a t", t=TC)), 16)

    def ph_moe(self):
        nc, ld, gd, mm, vv, aa, so = (self.nc, self.ld, self.gd, self.mm,
                                      self.vv, self.aa, self.so)
        self._phase_fence()
        with ExitStack() as ctx:
            sb = lambda n, s, d=F32: ctx.enter_context(nc.sbuf_tensor(n, s, d))
            xnt_own = sb("mxnt", [128, 8 * TOK], BF16)
            wgsb = sb("mwg", [128, 64], BF16)
            bgsb = sb("mbg", [128, 8])
            lg = sb("mlg", [128, 16 * 8])
            lm = sb("mlm", [128, 16 * 8])
            msk = sb("mmsk", [128, 16 * 8])
            wgt = sb("mwgt", [128, 16 * 8])
            gm = sb("mgm", [128, 16])
            gm2 = sb("mgm2", [128, 16])
            gw1 = sb("mgw1", [128, 16])
            gw2 = sb("mgw2", [128, 16])
            gtd = sb("mgtd", [128, 16])
            wtT = sb("mwtT", [8, TOK])
            werow = sb("mwerow", [1, TOK])
            wcast = sb("mwcast", [128, TOK], BF16)
            xse = sb("mxse", [128, 8 * 1024], BF16)
            w1q = [sb(f"mw1q{i}", [128, 8 * 512], BF16) for i in range(2)]
            w3q = [sb(f"mw3q{i}", [128, 8 * 512], BF16) for i in range(2)]
            h1 = sb("mh1", [128, 64 * 512], BF16)
            w2kt = [sb(f"mw2kt{i}", [128, 2 * 1024], BF16) for i in range(3)]
            silb = sb("msilb", [128, 4 * 512], BF16)
            gio = sb("mgio", [128, 2 * 512])
            gout = sb("mgout", [128, 8 * 512])
            pM = ctx.enter_context(nc.psum_tensor("pM", [128, 8 * TC], F32))

            with nc.Block() as blk:
                @blk.sync
                def _(e):
                    for dm in range(8):
                        ld.inc(e.dma_start(
                            out=xnt_own[:, dm * TOK:(dm + 1) * TOK],
                            in_=self.xnTo_s[dm * 128:(dm + 1) * 128, :]), 16)
                    ld.inc(e.dma_start(out=bgsb[:], in_=self.bg_bc), 16)

                @blk.gpsimd
                def _(e):
                    gd.inc(e.dma_start(
                        out=wgsb[:].rearrange("p (a c) -> p a c", c=8),
                        in_=self.wg.rearrange("(a p) c -> p a c", p=128)), 16)

            with nc.Block() as blk:
                @blk.tensor
                def _(e):
                    self.ld.wait(e)
                    self.gd.wait(e)
                    for tt in range(16):
                        for k in range(8):
                            ins = e.matmul(
                                pM[:, tt * 8:(tt + 1) * 8],
                                xnt_own[:, k * TOK + tt * 128:
                                        k * TOK + (tt + 1) * 128],
                                wgsb[:, k * 8:(k + 1) * 8],
                                start=(k == 0), stop=(k == 7))
                        mm.inc(ins)

            with nc.Block() as blk:
                @blk.vector
                def _(e):
                    for tt in range(16):
                        vv.inc(e.tensor_add(lg[:, tt * 8:(tt + 1) * 8],
                                            pM[:, tt * 8:(tt + 1) * 8], bgsb[:]))
                    e.drain()
                    for tt in range(16):
                        vv.inc(e.reduce_max(gm[:, tt:tt + 1],
                                            lg[:, tt * 8:(tt + 1) * 8], axis=AX.X))
                    e.drain()
                    for tt in range(16):
                        vv.inc(e.tensor_scalar(msk[:, tt * 8:(tt + 1) * 8],
                                               lg[:, tt * 8:(tt + 1) * 8],
                                               gm[:, tt:tt + 1], None,
                                               ALU.is_equal))
                    e.drain()
                    for tt in range(16):
                        vv.inc(e.scalar_tensor_tensor(
                            lm[:, tt * 8:(tt + 1) * 8],
                            msk[:, tt * 8:(tt + 1) * 8], -1e30,
                            lg[:, tt * 8:(tt + 1) * 8], ALU.mult, ALU.add))
                    e.drain()
                    for tt in range(16):
                        vv.inc(e.reduce_max(gm2[:, tt:tt + 1],
                                            lm[:, tt * 8:(tt + 1) * 8], axis=AX.X))
                    e.drain()
                    vv.inc(e.tensor_sub(gtd[:], gm[:], gm2[:]))

            with nc.Block() as blk:
                @blk.scalar
                def _(e):
                    aa.inc(e.activation(gtd[:], gtd[:], AF.Tanh, scale=self.cst[:, 6:7]))

            with nc.Block() as blk:
                @blk.vector
                def _(e):
                    vv.inc(e.tensor_scalar(gw1[:], gtd[:], 0.5, 0.5,
                                           ALU.mult, ALU.add))
                    for tt in range(16):
                        vv.inc(e.tensor_scalar(wgt[:, tt * 8:(tt + 1) * 8],
                                               lm[:, tt * 8:(tt + 1) * 8],
                                               gm2[:, tt:tt + 1], None,
                                               ALU.is_equal))
                    e.drain()
                    vv.inc(e.tensor_scalar(gw2[:], gw1[:], -1.0, 1.0,
                                           ALU.mult, ALU.add))
                    for tt in range(16):
                        vv.inc(e.tensor_scalar_mul(msk[:, tt * 8:(tt + 1) * 8],
                                                   msk[:, tt * 8:(tt + 1) * 8],
                                                   gw1[:, tt:tt + 1]))
                    e.drain()
                    for tt in range(16):
                        vv.inc(e.scalar_tensor_tensor(
                            wgt[:, tt * 8:(tt + 1) * 8],
                            wgt[:, tt * 8:(tt + 1) * 8],
                            gw2[:, tt:tt + 1],
                            msk[:, tt * 8:(tt + 1) * 8],
                            ALU.mult, ALU.add))

            with nc.Block() as blk:
                @blk.tensor
                def _(e):
                    for tt in range(16):
                        ins = e.transpose(pM[0:8, tt * 128:(tt + 1) * 128],
                                          wgt[:, tt * 8:(tt + 1) * 8],
                                          self.ident_sb[:])
                    mm.inc(ins)

            with nc.Block() as blk:
                @blk.vector
                def _(e):
                    vv.inc(e.tensor_copy(wtT[:], pM[0:8, 0:TOK]))

                @blk.sync
                def _(e):
                    self.vv.wait(e)
                    so.inc(e.dma_start(out=self.wt_s, in_=wtT[:]), 16)

            self._phase_fence()

            for ex in range(E):
                with nc.Block() as blk:
                    @blk.sync
                    def _(e, ex=ex):
                        ld.inc(e.dma_start(out=werow[:],
                                           in_=self.wt_s[ex:ex + 1, :]), 16)

                    @blk.tensor
                    def _(e):
                        self.ld.wait(e)
                        for i in range(4):
                            ins = e.matmul(pM[:, i * TC:(i + 1) * TC],
                                           self.ones_sb[0:1, :],
                                           werow[0:1, i * TC:(i + 1) * TC],
                                           start=True, stop=True)
                        mm.inc(ins)

                    @blk.vector
                    def _(e):
                        self.mm.wait(e)
                        for i in range(4):
                            ins = e.tensor_copy(wcast[:, i * TC:(i + 1) * TC],
                                                pM[:, i * TC:(i + 1) * TC])
                        vv.inc(ins)

                if getattr(self, "gsnap", 0) and ex > 0:
                    with nc.Block() as blk:
                        @blk.sync
                        def _(e, ex=ex):
                            self.so.wait(e)
                            for dm in range(8):
                                ld.inc(e.dma_start(
                                    out=gio[:, 0:512],
                                    in_=self.g_s[dm * 128:(dm + 1) * 128, 0:512]), 16)
                                self.ld.wait(e)
                                so.inc(e.dma_start(
                                    out=self.gsnap_s[(ex - 1) * D + dm * 128:
                                                     (ex - 1) * D + (dm + 1) * 128,
                                                     0:512],
                                    in_=gio[:, 0:512]), 16)
                            self.so.wait(e)

                for p in range(2):
                    with nc.Block() as blk:
                        gd0, mm0, vv0, aa0 = (self.gd.n, self.mm.n,
                                              self.vv.n, self.aa.n)

                        wq_tgt = {}

                        @blk.gpsimd
                        def _(e, ex=ex, p=p, mm0=mm0):
                            for q in range(8):
                                if q >= 2:
                                    e.wait_ge(self.mm.h, mm0 + (q - 1) * 4)
                                s = self.wqs[q % 2]
                                s.inc(e.dma_start(
                                    out=w1q[q % 2][:].rearrange(
                                        "p (a c) -> p a c", c=512),
                                    in_=self.w1[ex, :, q * 512:(q + 1) * 512]
                                    .rearrange("(a p) c -> p a c", p=128)), 16)
                                s.inc(e.dma_start(
                                    out=w3q[q % 2][:].rearrange(
                                        "p (a c) -> p a c", c=512),
                                    in_=self.w3[ex, :, q * 512:(q + 1) * 512]
                                    .rearrange("(a p) c -> p a c", p=128)), 16)
                                wq_tgt[q] = s.n

                        @blk.vector
                        def _(e, ex=ex, p=p, mm0=mm0, aa0=aa0):
                            for k in range(8):
                                ins = e.tensor_mul(
                                    xse[:, k * 1024:(k + 1) * 1024],
                                    xnt_own[:, k * TOK + p * 1024:
                                            k * TOK + (p + 1) * 1024],
                                    wcast[:, p * 1024:(p + 1) * 1024])
                            vv.inc(ins)
                            for s in range(32):
                                e.wait_ge(self.aa.h, aa0 + s + 1)
                                e.wait_ge(self.mm.h, mm0 + s + 1)
                                base = (s % 2) * 4
                                slb = (s % 2) * 2
                                for tc in range(2):
                                    ins = e.tensor_mul(
                                        h1[:, (s * 2 + tc) * 512:
                                           (s * 2 + tc + 1) * 512],
                                        silb[:, (slb + tc) * 512:
                                             (slb + tc + 1) * 512],
                                        pM[:, (base + 2 + tc) * 512:
                                           (base + 3 + tc) * 512])
                                vv.inc(ins)

                        @blk.scalar
                        def _(e, mm0=mm0, vv0=vv0, aa0=aa0):
                            ser = getattr(self, "p1ser", 0)
                            for s in range(32):
                                e.wait_ge(self.mm.h, mm0 + s + 1)
                                if s >= 2 - ser:
                                    e.wait_ge(self.vv.h, vv0 + s + ser)
                                base = (s % 2) * 4
                                slb = (s % 2) * 2
                                for tc in range(2):
                                    ins = e.activation(
                                        silb[:, (slb + tc) * 512:
                                             (slb + tc + 1) * 512],
                                        pM[:, (base + tc) * 512:
                                           (base + tc + 1) * 512], AF.Silu)
                                aa.inc(ins)

                        @blk.tensor
                        def _(e, gd0=gd0, vv0=vv0):
                            ser = getattr(self, "p1ser", 0)
                            for s in range(32):
                                q, hl = s // 4, s % 4
                                if s == 0:
                                    e.wait_ge(self.vv.h, vv0 + 1)
                                if hl == 0:
                                    e.wait_ge(self.wqs[q % 2].h, wq_tgt[q])
                                if s >= 2 - ser:
                                    e.wait_ge(self.vv.h, vv0 + s + ser)
                                base = (s % 2) * 4
                                for wi, wq in ((0, w1q[q % 2]), (1, w3q[q % 2])):
                                    for tc in range(2):
                                        for k in range(8):
                                            ins = e.matmul(
                                                pM[:, (base + wi * 2 + tc) * 512:
                                                   (base + wi * 2 + tc + 1) * 512],
                                                wq[:, k * 512 + hl * 128:
                                                   k * 512 + (hl + 1) * 128],
                                                xse[:, k * 1024 + tc * 512:
                                                    k * 1024 + (tc + 1) * 512],
                                                start=(k == 0), stop=(k == 7))
                                mm.inc(ins)

                    if ex == 0 and p == 0 and getattr(self, "moedbg", 0):
                        with nc.Block() as blk:
                            @blk.sync
                            def _(e):
                                self.gd.wait(e)
                                so.inc(e.dma_start(out=self.xse_s, in_=xse[:]), 16)
                                so.inc(e.dma_start(out=self.h1_s, in_=h1[:]), 16)
                                so.inc(e.dma_start(out=self.w1q_s, in_=w1q[1][:]), 16)
                                self.so.wait(e)

                    with nc.Block() as blk:
                        gd0, mm0, vv0, ld0, so0 = (self.gd.n, self.mm.n,
                                                   self.vv.n, self.ld.n,
                                                   self.so.n)

                        w2_tgt = {}

                        @blk.gpsimd
                        def _(e, ex=ex, mm0=mm0):
                            for tc in range(2):
                                for hp in range(16):
                                    li = tc * 16 + hp
                                    if li >= 3:
                                        e.wait_ge(self.mm.h, mm0 + (li - 2) * 2)
                                    s = self.w2s[li % 3]
                                    s.inc(e.dma_start(
                                        out=w2kt[li % 3][:].rearrange(
                                            "p (a c) -> p a c", c=1024),
                                        in_=self.w2[ex, hp * 256:(hp + 1) * 256, :]
                                        .rearrange("(a p) c -> p a c", p=128)), 16)
                                    w2_tgt[li] = s.n

                        gi_tgt = {}

                        @blk.sync
                        def _(e, p=p, ex=ex, vv0=vv0, so0=so0):
                            e.wait_ge(self.so.h, so0)
                            for tc in range(2):
                                for dm in range(8):
                                    va = tc * 8 + dm
                                    if va >= 2:
                                        e.wait_ge(self.vv.h, vv0 + va - 1)
                                    s = self.gis[va % 2]
                                    s.inc(e.dma_start(
                                        out=gio[:, (va % 2) * 512:
                                                (va % 2 + 1) * 512],
                                        in_=self.g_s[dm * 128:(dm + 1) * 128,
                                                     p * 1024 + tc * 512:
                                                     p * 1024 + (tc + 1) * 512]),
                                        16)
                                    gi_tgt[va] = s.n
                                e.wait_ge(self.vv.h, vv0 + (tc + 1) * 8)
                                for dm in range(8):
                                    dst = (self.y2snap_s[ex * D + dm * 128:
                                                         ex * D + (dm + 1) * 128,
                                                         p * 1024 + tc * 512:
                                                         p * 1024 + (tc + 1) * 512]
                                           if getattr(self, "y2dbg", 0) else
                                           self.g_s[dm * 128:(dm + 1) * 128,
                                                    p * 1024 + tc * 512:
                                                    p * 1024 + (tc + 1) * 512])
                                    so.inc(e.dma_start(
                                        out=dst,
                                        in_=gout[:, dm * 512:(dm + 1) * 512]), 16)

                        @blk.vector
                        def _(e, mm0=mm0, ld0=ld0, so0=so0):
                            for tc in range(2):
                                e.wait_ge(self.mm.h, mm0 + (tc + 1) * 32)
                                if getattr(self, "gw2", 1):
                                    e.wait_ge(self.so.h, so0 + tc * 8 * 16)
                                elif tc == 1:
                                    e.wait_ge(self.so.h, so0 + 8 * 16)
                                for dm in range(8):
                                    va = tc * 8 + dm
                                    e.wait_ge(self.gis[va % 2].h, gi_tgt[va])
                                    if getattr(self, "y2dbg", 0) == 1:
                                        vv.inc(e.tensor_copy(
                                            gout[:, dm * 512:(dm + 1) * 512],
                                            pM[:, dm * 512:(dm + 1) * 512]))
                                    else:
                                        vv.inc(e.tensor_add(
                                            gout[:, dm * 512:(dm + 1) * 512],
                                            gio[:, (va % 2) * 512:(va % 2 + 1) * 512],
                                            pM[:, dm * 512:(dm + 1) * 512]))

                        @blk.tensor
                        def _(e, gd0=gd0, vv0=vv0):
                            for tc in range(2):
                                if tc == 1:
                                    e.wait_ge(self.vv.h, vv0 + 8)
                                for ht in range(32):
                                    hp, hi = ht // 2, ht % 2
                                    li = tc * 16 + hp
                                    if hi == 0:
                                        e.wait_ge(self.w2s[li % 3].h, w2_tgt[li])
                                    for dm in range(8):
                                        ins = e.matmul(
                                            pM[:, dm * 512:(dm + 1) * 512],
                                            w2kt[li % 3][:, hi * 1024 + dm * 128:
                                                         hi * 1024 + (dm + 1) * 128],
                                            h1[:, (ht * 2 + tc) * 512:
                                               (ht * 2 + tc + 1) * 512],
                                            start=(ht == 0), stop=(ht == 31))
                                    mm.inc(ins)

    def ph_f(self):
        nc, ld, mm, vv, aa, so = (self.nc, self.ld, self.mm, self.vv,
                                  self.aa, self.so)
        self._phase_fence()
        with ExitStack() as ctx:
            sb = lambda n, s, d=F32: ctx.enter_context(nc.sbuf_tensor(n, s, d))
            gbuf = sb("fg", [128, 8 * 512])
            sb0 = sb("fs0", [128, 8 * 512])
            sb1 = sb("fs1", [128, 8 * 512])
            xf = sb("fx", [128, 4 * 1024])
            gsb = sb("fgs", [128, 8 * 512])
            gtok = sb("fgt", [128, 4 * 1024])
            tmpb = sb("ftmp", [128, 4 * 1024])
            outb = sb("fout", [128, 4 * 1024])
            pF = ctx.enter_context(nc.psum_tensor("pF", [128, 4096], F32))

            for oc in range(4):
                with nc.Block() as blk:
                    @blk.sync
                    def _(e, oc=oc):
                        ld.inc(e.dma_start(
                            out=gbuf[:].rearrange("p (a t) -> p a t", t=512),
                            in_=self.g_s[:, oc * 512:(oc + 1) * 512]
                            .rearrange("(a p) t -> p a t", p=128)), 16)
                        ld.inc(e.dma_start(
                            out=sb0[:].rearrange("p (a t) -> p a t", t=512),
                            in_=self.sT_s[:, oc * 512:(oc + 1) * 512]
                            .rearrange("(a p) t -> p a t", p=128)), 16)
                        ld.inc(e.dma_start(
                            out=sb1[:].rearrange("p (a t) -> p a t", t=512),
                            in_=self.sT_s[:, TOK + oc * 512:TOK + (oc + 1) * 512]
                            .rearrange("(a p) t -> p a t", p=128)), 16)
                        for tt in range(4):
                            ld.inc(e.dma_start(
                                out=xf[:, tt * 1024:(tt + 1) * 1024],
                                in_=self.x_own[oc * 512 + tt * 128:
                                               oc * 512 + (tt + 1) * 128, :]), 16)

                with nc.Block() as blk:
                    @blk.scalar
                    def _(e):
                        self.ld.wait(e)
                        aa.inc(e.activation(gsb[:], gbuf[:], AF.Silu))

                with nc.Block() as blk:
                    @blk.tensor
                    def _(e):
                        for dm in range(8):
                            for tt in range(4):
                                ins = e.transpose(
                                    pF[:, tt * 1024 + dm * 128:
                                       tt * 1024 + (dm + 1) * 128],
                                    gsb[:, dm * 512 + tt * 128:
                                        dm * 512 + (tt + 1) * 128],
                                    self.ident_sb[:])
                        mm.inc(ins)

                with nc.Block() as blk:
                    @blk.vector
                    def _(e):
                        for tt in range(4):
                            vv.inc(e.tensor_copy(
                                gtok[:, tt * 1024:(tt + 1) * 1024],
                                pF[:, tt * 1024:(tt + 1) * 1024]))

                for cand, sbc, outd in ((0, sb0, self.out_c0),
                                        (1, sb1, self.out_c1)):
                    with nc.Block() as blk:
                        @blk.tensor
                        def _(e, sbc=sbc):
                            for dm in range(8):
                                for tt in range(4):
                                    ins = e.transpose(
                                        pF[:, tt * 1024 + dm * 128:
                                           tt * 1024 + (dm + 1) * 128],
                                        sbc[:, dm * 512 + tt * 128:
                                            dm * 512 + (tt + 1) * 128],
                                        self.ident_sb[:])
                            mm.inc(ins)

                    with nc.Block() as blk:
                        so_t = self.so.n

                        @blk.vector
                        def _(e, so_t=so_t):
                            e.wait_ge(self.so.h, so_t)
                            for tt in range(4):
                                e.tensor_mul(tmpb[:, tt * 1024:(tt + 1) * 1024],
                                             pF[:, tt * 1024:(tt + 1) * 1024],
                                             gtok[:, tt * 1024:(tt + 1) * 1024])
                            e.drain()
                            for tt in range(4):
                                vv.inc(e.tensor_add(
                                    outb[:, tt * 1024:(tt + 1) * 1024],
                                    tmpb[:, tt * 1024:(tt + 1) * 1024],
                                    xf[:, tt * 1024:(tt + 1) * 1024]))

                        @blk.sync
                        def _(e, oc=oc, outd=outd):
                            self.vv.wait(e)
                            for tt in range(4):
                                so.inc(e.dma_start(
                                    out=outd[oc * 512 + tt * 128:
                                             oc * 512 + (tt + 1) * 128, :],
                                    in_=outb[:, tt * 1024:(tt + 1) * 1024]), 16)

    def ph_fini(self):
        nc = self.nc
        if self.dbg is not None:
            for di, (dap, name, rows, cols) in enumerate(self.dbg):
                src = getattr(self, name)
                with ExitStack() as ctx:
                    dbuf = ctx.enter_context(
                        nc.sbuf_tensor(f"dbgbuf{di}", [128, cols], src.dtype))
                    for rb in range((rows + 127) // 128):
                        r0 = rb * 128
                        r1 = min(rows, r0 + 128)
                        nr = r1 - r0
                        with nc.Block() as blk:
                            @blk.sync
                            def _(e, r0=r0, nr=nr):
                                self.so.wait(e)
                                self.ld.inc(e.dma_start(
                                    out=dbuf[0:nr, :],
                                    in_=src[r0:r0 + nr, 0:cols]), 16)
                                self.ld.wait(e)
                                self.so.inc(e.dma_start(
                                    out=dap[r0:r0 + nr, :],
                                    in_=dbuf[0:nr, :]), 16)
        with nc.Block() as blk:
            @blk.sync
            def _(e):
                self.so.wait(e)


_NC_CACHE = {}


def _get_nc(upto=99):
    if upto not in _NC_CACHE:
        _NC_CACHE[upto] = MK(upto=upto).build()
    return _NC_CACHE[upto]


def _pcols(v):
    n = v.shape[0] // 128
    return np.ascontiguousarray(v.reshape(n, 128).T)


def _pcols2(m):
    n = m.shape[0] // 128
    c = m.shape[1]
    return np.ascontiguousarray(m.reshape(n, 128, c).transpose(1, 0, 2).reshape(128, n * c))


def make_in_maps(inputs):
    x = np.ascontiguousarray(inputs["L0_attn_x"], dtype=np.float32)
    base = dict(
        ln_g=_pcols(inputs["L0_attn_ln_g"]),
        ln_b=_pcols(inputs["L0_attn_ln_b"]),
        w_in=np.ascontiguousarray(inputs["L0_attn_W_in"]),
        conv_w=_pcols2(inputs["L0_attn_conv_w"].reshape(DI, 4)),
        conv_b=_pcols(inputs["L0_attn_conv_b"]),
        w_xp=np.ascontiguousarray(inputs["L0_attn_W_xp"]),
        w_dt=np.ascontiguousarray(inputs["L0_attn_W_dt"]),
        dt_bias=_pcols(inputs["L0_attn_dt_bias"]),
        a_log=_pcols2(inputs["L0_attn_A_log"]),
        d_skip=_pcols(inputs["L0_attn_D_skip"]),
        w_out=np.ascontiguousarray(inputs["L0_attn_W_out"]),
        wg=np.ascontiguousarray(inputs["L0_ffn_Wg"]),
        bg_bc=np.ascontiguousarray(
            np.broadcast_to(inputs["L0_ffn_bg"].reshape(1, E), (128, E))),
        w1=np.ascontiguousarray(inputs["L0_ffn_W1"]),
        w2=np.ascontiguousarray(inputs["L0_ffn_W2"]),
        w3=np.ascontiguousarray(inputs["L0_ffn_W3"]),
        ident=np.eye(128, dtype=np.float32),
        ones_f=np.ones((1, 128), dtype=np.float32),
    )
    in_maps = []
    for b in range(B):
        for j in range(2):
            m = dict(base)
            m["x_full"] = np.ascontiguousarray(x[b])
            m["x_own"] = np.ascontiguousarray(x[b, j * TOK:(j + 1) * TOK])
            in_maps.append(m)
    return in_maps


def kernel(**inputs):
    nc = _get_nc(99)
    in_maps = make_in_maps(inputs)
    res = run_bass_kernel_spmd(nc, in_maps, core_ids=list(range(8)))
    out = np.empty((B, L, D), np.float32)
    for b in range(B):
        for j in range(2):
            r = res.results[b * 2 + j]
            out[b, j * TOK:(j + 1) * TOK] = r["out_c0" if j == 0 else "out_c1"]
    return out


# version 08 (final)
# speedup vs baseline: 1.1056x; 1.0198x over previous
import numpy as np
import ml_dtypes
from contextlib import ExitStack

import concourse.bass as bass
import concourse.mybir as mybir
from concourse.bass_utils import run_bass_kernel_spmd

F32 = mybir.dt.float32
BF16 = mybir.dt.bfloat16
AF = mybir.ActivationFunctionType
ALU = mybir.AluOpType
AX = mybir.AxisListType

B, L, D, DI, NST, RNK, HF, E = 4, 4096, 1024, 2048, 16, 64, 4096, 8
TOK = L // 2
NCH = 8
TC = L // NCH
EPS = 1e-5
BF16NP = ml_dtypes.bfloat16


class Sem:
    def __init__(self, nc, ctx, name):
        self.h = ctx.enter_context(nc.semaphore(name))
        self.n = 0

    def inc(self, instr, k=1):
        self.n += k
        return instr.then_inc(self.h, k)

    def wait(self, eng, at=None):
        eng.wait_ge(self.h, self.n if at is None else at)


class MK:
    def __init__(self, upto=99):
        self.upto = upto
        nc = bass.Bass("TRN2", target_bir_lowering=False)
        self.nc = nc
        dt = nc.dram_tensor
        ei, eo, it = "ExternalInput", "ExternalOutput", "Internal"
        self.x_full = dt("x_full", [L, D], F32, kind=ei).ap()
        self.x_own = dt("x_own", [TOK, D], F32, kind=ei).ap()
        self.ln_g = dt("ln_g", [128, 8], F32, kind=ei).ap()
        self.ln_b = dt("ln_b", [128, 8], F32, kind=ei).ap()
        self.w_in = dt("w_in", [D, 2 * DI], F32, kind=ei).ap()
        self.conv_w = dt("conv_w", [128, 64], F32, kind=ei).ap()
        self.conv_b = dt("conv_b", [128, 16], F32, kind=ei).ap()
        self.w_xp = dt("w_xp", [DI, 96], F32, kind=ei).ap()
        self.w_dt = dt("w_dt", [RNK, DI], F32, kind=ei).ap()
        self.dt_bias = dt("dt_bias", [128, 16], F32, kind=ei).ap()
        self.a_log = dt("a_log", [128, 256], F32, kind=ei).ap()
        self.d_skip = dt("d_skip", [128, 16], F32, kind=ei).ap()
        self.w_out = dt("w_out", [DI, D], F32, kind=ei).ap()
        self.wg = dt("wg", [D, E], F32, kind=ei).ap()
        self.bg_bc = dt("bg_bc", [128, E], F32, kind=ei).ap()
        self.w1 = dt("w1", [E, D, HF], F32, kind=ei).ap()
        self.w2 = dt("w2", [E, HF, D], F32, kind=ei).ap()
        self.w3 = dt("w3", [E, D, HF], F32, kind=ei).ap()
        self.ident = dt("ident", [128, 128], F32, kind=ei).ap()
        self.ones_f = dt("ones_f", [1, 128], F32, kind=ei).ap()
        self.out_c0 = dt("out_c0", [TOK, D], F32, kind=eo).ap()
        self.out_c1 = dt("out_c1", [TOK, D], F32, kind=eo).ap()
        self.xnT_s = dt("xnT_s", [D, L], F32, kind=it).ap()
        self.xnTo_s = dt("xnTo_s", [D, TOK], BF16, kind=it).ap()
        self.xiT_s = dt("xiT_s", [DI, L], F32, kind=it).ap()
        self.szT_s = dt("szT_s", [DI, L], BF16, kind=it).ap()
        self.xcT_s = dt("xcT_s", [DI, L], F32, kind=it).ap()
        self.xcdT_s = dt("xcdT_s", [DI, L], BF16, kind=it).ap()
        self.dltT_s = dt("dltT_s", [DI, L], BF16, kind=it).ap()
        self.bmcm_s = dt("bmcm_s", [2 * NST, L], F32, kind=it).ap()
        self.sT_s = dt("sT_s", [D, L], F32, kind=it).ap()
        self.g_s = dt("g_s", [D, TOK], F32, kind=it).ap()
        self.xse_s = dt("xse_s", [128, 8192], BF16, kind=it).ap()
        self.h1_s = dt("h1_s", [128, 32768], BF16, kind=it).ap()
        self.w1q_s = dt("w1q_s", [128, 4096], BF16, kind=it).ap()
        self.gsnap_s = dt("gsnap_s", [8 * D, TOK], F32, kind=it).ap()
        self.y2snap_s = dt("y2snap_s", [8 * D, TOK], F32, kind=it).ap()
        self.wt_s = dt("wt_s", [E, TOK], F32, kind=it).ap()
        self.dbg = None
        self.dbg_src = None

    def add_debug(self, *specs):
        self.dbg = []
        for i, (name, rows, cols) in enumerate(specs):
            ap = getattr(self, name)
            self.dbg.append((self.nc.dram_tensor(
                f"dbg{i}", [rows, cols], ap.dtype, kind="ExternalOutput").ap(),
                name, rows, cols))

    def build(self):
        nc = self.nc
        with ExitStack() as ctx:
            self.ld = Sem(nc, ctx, "ld")
            self.gd = Sem(nc, ctx, "gd")
            self.mm = Sem(nc, ctx, "mm")
            self.vv = Sem(nc, ctx, "vv")
            self.aa = Sem(nc, ctx, "aa")
            self.so = Sem(nc, ctx, "so")
            self.wos = [Sem(nc, ctx, f"wos{i}") for i in range(2)]
            self.wqs = [Sem(nc, ctx, f"wqs{i}") for i in range(2)]
            self.w2s = [Sem(nc, ctx, f"w2s{i}") for i in range(4)]
            self.gis = [Sem(nc, ctx, f"gis{i}") for i in range(2)]
            self.brs = [Sem(nc, ctx, f"brs{i}") for i in range(2)]
            self.pp = Sem(nc, ctx, "pp")
            sb = lambda name, shape, dtp=F32: ctx.enter_context(
                nc.sbuf_tensor(name, shape, dtp))
            self.ident_sb = sb("ident_sb", [128, 128])
            self.ones_sb = sb("ones_sb", [1, 128])
            self.stats = sb("stats", [128, 32 * 4])
            self.stats_o = sb("stats_o", [128, 16 * 2])
            self.lng_sb = sb("lng_sb", [128, 8])
            self.lnb_sb = sb("lnb_sb", [128, 8])
            self.cw_sb = sb("cw_sb", [128, 16 * 4])
            self.cb_sb = sb("cb_sb", [128, 16])
            self.dtb_sb = sb("dtb_sb", [128, 16])
            self.dsk_sb = sb("dsk_sb", [128, 16])
            self.a_sb = sb("a_sb", [128, 16 * NST])
            self.cst = sb("cst", [128, 8])

            self.ph_consts()
            if self.upto >= 1:
                self.ph_a0()
            if self.upto >= 2:
                self.ph_a()
            if self.upto >= 3:
                self.ph_a_own()
            if self.upto >= 4:
                self.ph_b()
            if self.upto >= 5:
                self.ph_s()
            if self.upto >= 6:
                self.ph_moe()
            if self.upto >= 7:
                self.ph_f()
            self.ph_fini()
        return nc

    def _phase_fence(self):
        nc, so = self.nc, self.so
        with nc.Block() as blk:
            @blk.sync
            def _(e):
                so.wait(e)

    def ph_consts(self):
        nc, ld, aa, vv = self.nc, self.ld, self.aa, self.vv
        with nc.Block() as blk:
            @blk.sync
            def _(e):
                ld.inc(e.dma_start(out=self.ident_sb[:], in_=self.ident), 16)
                ld.inc(e.dma_start(out=self.ones_sb[:], in_=self.ones_f), 16)
                ld.inc(e.dma_start(out=self.lng_sb[:], in_=self.ln_g), 16)
                ld.inc(e.dma_start(out=self.lnb_sb[:], in_=self.ln_b), 16)
                ld.inc(e.dma_start(out=self.cw_sb[:], in_=self.conv_w), 16)
                ld.inc(e.dma_start(out=self.cb_sb[:], in_=self.conv_b), 16)
                ld.inc(e.dma_start(out=self.dtb_sb[:], in_=self.dt_bias), 16)
                ld.inc(e.dma_start(out=self.dsk_sb[:], in_=self.d_skip), 16)
                ld.inc(e.dma_start(out=self.a_sb[:], in_=self.a_log), 16)

            @blk.scalar
            def _(e):
                ld.wait(e)
                aa.inc(e.activation(self.a_sb[:], self.a_sb[:], AF.Exp))

            @blk.vector
            def _(e):
                for i, v in enumerate([1.0 / D, -0.5, 1.0 + EPS, -EPS,
                                       -1.0, 1.0, 0.5]):
                    e.memset(self.cst[:, i:i + 1], v)
                aa.wait(e)
                vv.inc(e.tensor_scalar_mul(self.a_sb[:], self.a_sb[:], -1.0))

    def ph_a0(self):
        nc, ld, vv, aa = self.nc, self.ld, self.vv, self.aa
        with ExitStack() as ctx:
            xt = [ctx.enter_context(nc.sbuf_tensor(f"a0x{i}", [128, D], F32))
                  for i in range(8)]
            q = [ctx.enter_context(nc.sbuf_tensor(f"a0q{i}", [128, D], F32))
                 for i in range(4)]
            tmp = ctx.enter_context(nc.sbuf_tensor("a0t", [128, 16], F32))
            groups = [("full", g) for g in range(8)] + [("own", g) for g in range(4)]
            with nc.Block() as blk:
                @blk.sync
                def _(e):
                    for i in range(4):
                        ld.inc(e.dma_start(
                            out=xt[i][:],
                            in_=self.x_full[i * 128:(i + 1) * 128, :]), 16)
            for gi, (kind, g) in enumerate(groups):
                bufs = [xt[(gi % 2) * 4 + i] for i in range(4)]
                nbufs = [xt[((gi + 1) % 2) * 4 + i] for i in range(4)]
                ld_t = self.ld.n

                with nc.Block() as blk:
                    @blk.sync
                    def _(e, gi=gi, nbufs=nbufs):
                        if gi + 1 < len(groups):
                            nkind, ng = groups[gi + 1]
                            src = self.x_full if nkind == "full" else self.x_own
                            for i in range(4):
                                t = ng * 4 + i
                                ld.inc(e.dma_start(
                                    out=nbufs[i][:],
                                    in_=src[t * 128:(t + 1) * 128, :]), 16)

                    @blk.vector
                    def _(e, bufs=bufs, ld_t=ld_t):
                        e.wait_ge(self.ld.h, ld_t)
                        for i in range(4):
                            vv.inc(e.reduce_sum(tmp[:, i:i + 1], bufs[i][:],
                                                axis=AX.X))
                        e.drain()
                        for i in range(4):
                            vv.inc(e.tensor_scalar_mul(tmp[:, 4 + i:5 + i],
                                                       tmp[:, i:i + 1], 1.0 / D))
                        e.drain()
                        for i in range(4):
                            vv.inc(e.tensor_mul(tmp[:, 8 + i:9 + i],
                                                tmp[:, 4 + i:5 + i],
                                                tmp[:, 4 + i:5 + i]))
                        e.drain()
                        for i in range(4):
                            vv.inc(e.tensor_scalar(tmp[:, 8 + i:9 + i],
                                                   tmp[:, 8 + i:9 + i],
                                                   -1.0, EPS, ALU.mult, ALU.add))

                    @blk.scalar
                    def _(e, bufs=bufs, ld_t=ld_t):
                        e.wait_ge(self.ld.h, ld_t)
                        for i in range(4):
                            aa.inc(e.activation(q[i][:], bufs[i][:], AF.Square))

                with nc.Block() as blk:
                    @blk.vector
                    def _(e):
                        for i in range(4):
                            vv.inc(e.reduce_sum(tmp[:, 12 + i:13 + i], q[i][:],
                                                axis=AX.X))

                with nc.Block() as blk:
                    @blk.scalar
                    def _(e, kind=kind, g=g):
                        stats = self.stats if kind == "full" else self.stats_o
                        ncol = 4 if kind == "full" else 2
                        cc = [(g * 4 + i) * ncol for i in range(4)]
                        for i in range(4):
                            aa.inc(e.copy(stats[:, cc[i]:cc[i] + 1],
                                          tmp[:, 4 + i:5 + i]))
                            aa.inc(e.activation(tmp[:, 12 + i:13 + i],
                                                tmp[:, 12 + i:13 + i], AF.Ln,
                                                bias=tmp[:, 8 + i:9 + i],
                                                scale=self.cst[:, 0:1]))
                        e.drain()
                        for i in range(4):
                            aa.inc(e.activation(stats[:, cc[i] + 1:cc[i] + 2],
                                                tmp[:, 12 + i:13 + i], AF.Exp,
                                                scale=self.cst[:, 1:2]))
                        if kind == "full":
                            e.drain()
                            for i in range(4):
                                aa.inc(e.activation(tmp[:, 8 + i:9 + i],
                                                    stats[:, cc[i] + 1:cc[i] + 2],
                                                    AF.Square))
                            e.drain()
                            for i in range(4):
                                aa.inc(e.activation(tmp[:, 8 + i:9 + i],
                                                    tmp[:, 8 + i:9 + i], AF.Ln,
                                                    bias=self.cst[:, 2:3],
                                                    scale=self.cst[:, 3:4]))
                            e.drain()
                            for i in range(4):
                                aa.inc(e.activation(tmp[:, 8 + i:9 + i],
                                                    tmp[:, 8 + i:9 + i], AF.Exp,
                                                    scale=self.cst[:, 1:2]))

                with nc.Block() as blk:
                    @blk.vector
                    def _(e, kind=kind, g=g):
                        if kind != "full":
                            return
                        for i in range(4):
                            c0 = (g * 4 + i) * 4
                            vv.inc(e.tensor_mul(self.stats[:, c0 + 2:c0 + 3],
                                                self.stats[:, c0 + 1:c0 + 2],
                                                tmp[:, 8 + i:9 + i]))

    def ph_a(self):
        nc, ld, gd, mm, vv, aa, so = (self.nc, self.ld, self.gd, self.mm,
                                      self.vv, self.aa, self.so)
        with ExitStack() as ctx:
            sb = lambda n, s, d=F32: ctx.enter_context(nc.sbuf_tensor(n, s, d))
            w_in_bf = sb("w_in_bf", [128, 8 * 4096], BF16)
            xt = [sb(f"ax{i}", [128, D]) for i in range(8)]
            xn = [sb(f"axn{i}", [128, D]) for i in range(4)]
            z2 = [sb(f"az2{i}", [128, D]) for i in range(4)]
            xn2t = sb("axn2t", [128, 8 * TC], BF16)
            xnt = sb("axnt", [128, 8 * TC])
            ev = sb("aev", [128, 8 * TC])
            evz = sb("aevz", [128, 8 * TC], BF16)
            pA = ctx.enter_context(nc.psum_tensor("pA", [128, 8 * TC], F32))

            with nc.Block() as blk:
                @blk.gpsimd
                def _(e):
                    for k in range(8):
                        gd.inc(e.dma_start(
                            out=w_in_bf[:, k * 4096:(k + 1) * 4096],
                            in_=self.w_in[k * 128:(k + 1) * 128, :]), 16)

                @blk.sync
                def _(e):
                    for i in range(4):
                        ld.inc(e.dma_start(
                            out=xt[i][:],
                            in_=self.x_full[i * 128:(i + 1) * 128, :]), 16)

            for c in range(NCH):
                bufs = [xt[(c % 2) * 4 + i] for i in range(4)]
                nbufs = [xt[((c + 1) % 2) * 4 + i] for i in range(4)]
                ld_t = self.ld.n
                gd_t = self.gd.n

                with nc.Block() as blk:
                    @blk.sync
                    def _(e, c=c, nbufs=nbufs):
                        if c + 1 < NCH:
                            for i in range(4):
                                t = (c + 1) * 4 + i
                                ld.inc(e.dma_start(
                                    out=nbufs[i][:],
                                    in_=self.x_full[t * 128:(t + 1) * 128, :]), 16)

                    @blk.vector
                    def _(e, c=c, bufs=bufs, ld_t=ld_t):
                        e.wait_ge(self.ld.h, ld_t)
                        st = self.stats
                        for i in range(4):
                            t = c * 4 + i
                            vv.inc(e.tensor_scalar(
                                xn[i][:], bufs[i][:], st[:, 4 * t:4 * t + 1],
                                st[:, 4 * t + 1:4 * t + 2],
                                ALU.subtract, ALU.mult))
                            vv.inc(e.tensor_scalar(
                                z2[i][:], bufs[i][:], st[:, 4 * t:4 * t + 1],
                                st[:, 4 * t + 2:4 * t + 3],
                                ALU.subtract, ALU.mult))

                with nc.Block() as blk:
                    @blk.tensor
                    def _(e):
                        for dcol in range(8):
                            for tt in range(4):
                                ins = e.transpose(
                                    pA[:, dcol * TC + tt * 128:
                                       dcol * TC + (tt + 1) * 128],
                                    z2[tt][:, dcol * 128:(dcol + 1) * 128],
                                    self.ident_sb[:])
                        mm.inc(ins)

                with nc.Block() as blk:
                    @blk.vector
                    def _(e):
                        for dcol in range(8):
                            vv.inc(e.tensor_scalar(
                                xn2t[:, dcol * TC:(dcol + 1) * TC],
                                pA[:, dcol * TC:(dcol + 1) * TC],
                                self.lng_sb[:, dcol:dcol + 1],
                                self.lnb_sb[:, dcol:dcol + 1],
                                ALU.mult, ALU.add))

                with nc.Block() as blk:
                    @blk.tensor
                    def _(e):
                        for dcol in range(8):
                            for tt in range(4):
                                ins = e.transpose(
                                    pA[:, dcol * TC + tt * 128:
                                       dcol * TC + (tt + 1) * 128],
                                    xn[tt][:, dcol * 128:(dcol + 1) * 128],
                                    self.ident_sb[:])
                        mm.inc(ins)

                with nc.Block() as blk:
                    so_t = self.so.n

                    @blk.vector
                    def _(e, so_t=so_t):
                        e.wait_ge(self.so.h, so_t)
                        for dcol in range(8):
                            vv.inc(e.tensor_copy(
                                xnt[:, dcol * TC:(dcol + 1) * TC],
                                pA[:, dcol * TC:(dcol + 1) * TC]))

                    @blk.sync
                    def _(e, c=c):
                        self.vv.wait(e)
                        for dcol in range(8):
                            so.inc(e.dma_start(
                                out=self.xnT_s[dcol * 128:(dcol + 1) * 128,
                                               c * TC:(c + 1) * TC],
                                in_=xnt[:, dcol * TC:(dcol + 1) * TC]), 16)

                for grp in range(4):
                    o0 = grp * 8
                    with nc.Block() as blk:
                        @blk.tensor
                        def _(e, o0=o0, gd_t=gd_t, c=c, grp=grp):
                            if c == 0 and grp == 0:
                                e.wait_ge(self.gd.h, gd_t)
                            for oi in range(8):
                                o = o0 + oi
                                for k in range(8):
                                    ins = e.matmul(
                                        pA[:, oi * TC:(oi + 1) * TC],
                                        w_in_bf[:, k * 4096 + o * 128:
                                                k * 4096 + (o + 1) * 128],
                                        xn2t[:, k * TC:(k + 1) * TC],
                                        start=(k == 0), stop=(k == 7))
                            mm.inc(ins)

                    with nc.Block() as blk:
                        so_t = self.so.n
                        if o0 < 16:
                            @blk.vector
                            def _(e, so_t=so_t):
                                e.wait_ge(self.so.h, so_t)
                                for oi in range(8):
                                    vv.inc(e.tensor_copy(
                                        ev[:, oi * TC:(oi + 1) * TC],
                                        pA[:, oi * TC:(oi + 1) * TC]))

                            @blk.sync
                            def _(e, o0=o0, c=c):
                                self.vv.wait(e)
                                for oi in range(8):
                                    o = o0 + oi
                                    so.inc(e.dma_start(
                                        out=self.xiT_s[o * 128:(o + 1) * 128,
                                                       c * TC:(c + 1) * TC],
                                        in_=ev[:, oi * TC:(oi + 1) * TC]), 16)
                        else:
                            @blk.scalar
                            def _(e, so_t=so_t):
                                e.wait_ge(self.so.h, so_t)
                                for oi in range(8):
                                    aa.inc(e.activation(
                                        evz[:, oi * TC:(oi + 1) * TC],
                                        pA[:, oi * TC:(oi + 1) * TC], AF.Silu))

                            @blk.sync
                            def _(e, o0=o0, c=c):
                                self.aa.wait(e)
                                for oi in range(8):
                                    o = o0 + oi - 16
                                    so.inc(e.dma_start(
                                        out=self.szT_s[o * 128:(o + 1) * 128,
                                                       c * TC:(c + 1) * TC],
                                        in_=evz[:, oi * TC:(oi + 1) * TC]), 16)

    def ph_a_own(self):
        nc, ld, mm, vv, so = self.nc, self.ld, self.mm, self.vv, self.so
        with ExitStack() as ctx:
            sb = lambda n, s, d=F32: ctx.enter_context(nc.sbuf_tensor(n, s, d))
            xt = [sb(f"ox{i}", [128, D]) for i in range(8)]
            xn = [sb(f"oxn{i}", [128, D]) for i in range(4)]
            xnt_bf = sb("oxnt_bf", [128, 8 * TC], BF16)
            xnt_f = sb("oxnt_f", [128, 8 * TC])
            pA = ctx.enter_context(nc.psum_tensor("poA", [128, 8 * TC], F32))

            with nc.Block() as blk:
                @blk.sync
                def _(e):
                    for i in range(4):
                        ld.inc(e.dma_start(
                            out=xt[i][:],
                            in_=self.x_own[i * 128:(i + 1) * 128, :]), 16)

            for c in range(4):
                bufs = [xt[(c % 2) * 4 + i] for i in range(4)]
                nbufs = [xt[((c + 1) % 2) * 4 + i] for i in range(4)]
                ld_t = self.ld.n
                with nc.Block() as blk:
                    @blk.sync
                    def _(e, c=c, nbufs=nbufs):
                        if c + 1 < 4:
                            for i in range(4):
                                t = (c + 1) * 4 + i
                                ld.inc(e.dma_start(
                                    out=nbufs[i][:],
                                    in_=self.x_own[t * 128:(t + 1) * 128, :]), 16)

                    @blk.vector
                    def _(e, c=c, bufs=bufs, ld_t=ld_t):
                        e.wait_ge(self.ld.h, ld_t)
                        st = self.stats_o
                        for i in range(4):
                            t = c * 4 + i
                            vv.inc(e.tensor_scalar(
                                xn[i][:], bufs[i][:], st[:, 2 * t:2 * t + 1],
                                st[:, 2 * t + 1:2 * t + 2],
                                ALU.subtract, ALU.mult))

                with nc.Block() as blk:
                    @blk.tensor
                    def _(e):
                        for dcol in range(8):
                            for tt in range(4):
                                ins = e.transpose(
                                    pA[:, dcol * TC + tt * 128:
                                       dcol * TC + (tt + 1) * 128],
                                    xn[tt][:, dcol * 128:(dcol + 1) * 128],
                                    self.ident_sb[:])
                        mm.inc(ins)

                with nc.Block() as blk:
                    so_t = self.so.n

                    @blk.vector
                    def _(e, so_t=so_t):
                        e.wait_ge(self.so.h, so_t)
                        for dcol in range(8):
                            vv.inc(e.tensor_copy(
                                xnt_bf[:, dcol * TC:(dcol + 1) * TC],
                                pA[:, dcol * TC:(dcol + 1) * TC]))
                            vv.inc(e.tensor_copy(
                                xnt_f[:, dcol * TC:(dcol + 1) * TC],
                                pA[:, dcol * TC:(dcol + 1) * TC]))

                    @blk.sync
                    def _(e, c=c):
                        self.vv.wait(e)
                        for dcol in range(8):
                            so.inc(e.dma_start(
                                out=self.xnTo_s[dcol * 128:(dcol + 1) * 128,
                                                c * TC:(c + 1) * TC],
                                in_=xnt_bf[:, dcol * TC:(dcol + 1) * TC]), 16)
                            so.inc(e.dma_start(
                                out=self.g_s[dcol * 128:(dcol + 1) * 128,
                                             c * TC:(c + 1) * TC],
                                in_=xnt_f[:, dcol * TC:(dcol + 1) * TC]), 16)

    def ph_b(self):
        bparts = getattr(self, "bparts", {"conv", "xdbl", "delta", "delta_sp"})
        nc, ld, gd, mm, vv, aa, so = (self.nc, self.ld, self.gd, self.mm,
                                      self.vv, self.aa, self.so)
        self._phase_fence()
        with ExitStack() as ctx:
            sb = lambda n, s, d=F32: ctx.enter_context(nc.sbuf_tensor(n, s, d))
            xib = [sb(f"bxi{i}", [128, L + 4]) for i in range(2)]
            acc = [sb(f"bacc{i}", [128, L]) for i in range(2)]
            xc = sb("bxc", [128, L])
            xc_bf = sb("bxcbf", [128, L], BF16)
            xcd_bf = sb("bxcdbf", [128, L], BF16)
            w_xp_bf = sb("bwxp", [128, 16 * 96], BF16)
            dlo_bf = sb("bdlo", [64, L], BF16)
            bmcm = sb("bbmcm", [32, L])

            with ExitStack() as pctx:
                pX = pctx.enter_context(nc.psum_tensor("pX", [96, L], F32))

                with nc.Block() as blk:
                    @blk.gpsimd
                    def _(e):
                        gd.inc(e.dma_start(
                            out=w_xp_bf[:].rearrange("p (a c) -> p a c", c=96),
                            in_=self.w_xp.rearrange("(a p) c -> p a c", p=128)), 16)

                    @blk.sync
                    def _(e):
                        ld.inc(e.dma_start(out=xib[0][:, 4:4 + L],
                                           in_=self.xiT_s[0:128, :]), 16)

                for t in range(16):
                    buf = xib[t % 2]
                    nbuf = xib[(t + 1) % 2]
                    ld_t = self.ld.n
                    gd_t = self.gd.n
                    so_t = self.so.n
                    with nc.Block() as blk:
                        @blk.sync
                        def _(e, t=t, nbuf=nbuf):
                            if t + 1 < 16:
                                ld.inc(e.dma_start(
                                    out=nbuf[:, 4:4 + L],
                                    in_=self.xiT_s[(t + 1) * 128:(t + 2) * 128, :]),
                                    16)

                        @blk.vector
                        def _(e, t=t, buf=buf, ld_t=ld_t, so_t=so_t):
                            e.wait_ge(self.ld.h, ld_t)
                            e.memset(buf[:, 1:4], 0.0)
                            e.drain()
                            vv.inc(e.tensor_scalar_mul(
                                acc[0][:], buf[:, 1:1 + L],
                                self.cw_sb[:, 4 * t:4 * t + 1]))
                            e.drain()
                            vv.inc(e.scalar_tensor_tensor(
                                acc[1][:], buf[:, 2:2 + L],
                                self.cw_sb[:, 4 * t + 1:4 * t + 2], acc[0][:],
                                ALU.mult, ALU.add))
                            e.drain()
                            vv.inc(e.scalar_tensor_tensor(
                                acc[0][:], buf[:, 3:3 + L],
                                self.cw_sb[:, 4 * t + 2:4 * t + 3], acc[1][:],
                                ALU.mult, ALU.add))
                            e.drain()
                            vv.inc(e.scalar_tensor_tensor(
                                acc[1][:], buf[:, 4:4 + L],
                                self.cw_sb[:, 4 * t + 3:4 * t + 4], acc[0][:],
                                ALU.mult, ALU.add))
                            self._bconv_mark = self.vv.n
                            e.wait_ge(self.aa.h, self.aa.n + 1)
                            e.wait_ge(self.so.h, so_t)
                            vv.inc(e.tensor_copy(xc_bf[:], xc[:]))
                            vv.inc(e.tensor_scalar_mul(
                                xcd_bf[:], xc[:], self.dsk_sb[:, t:t + 1]))

                        @blk.scalar
                        def _(e, t=t, so_t=so_t):
                            e.wait_ge(self.vv.h, self._bconv_mark)
                            e.wait_ge(self.so.h, so_t)
                            aa.inc(e.activation(xc[:], acc[1][:], AF.Silu,
                                                bias=self.cb_sb[:, t:t + 1]))

                        @blk.sync
                        def _(e, t=t):
                            self.vv.wait(e)
                            so.inc(e.dma_start(
                                out=self.xcT_s[t * 128:(t + 1) * 128, :],
                                in_=xc[:]), 16)
                            so.inc(e.dma_start(
                                out=self.xcdT_s[t * 128:(t + 1) * 128, :],
                                in_=xcd_bf[:]), 16)

                        if "xdbl" in bparts:
                            @blk.tensor
                            def _(e, t=t, gd_t=gd_t):
                                if t == 0:
                                    e.wait_ge(self.gd.h, gd_t)
                                self.vv.wait(e)
                                for cc in range(8):
                                    ins = e.matmul(
                                        pX[:, cc * TC:(cc + 1) * TC],
                                        w_xp_bf[:, t * 96:(t + 1) * 96],
                                        xc_bf[:, cc * TC:(cc + 1) * TC],
                                        start=(t == 0), stop=(t == 15))
                                mm.inc(ins)

                if "xdbl" not in bparts:
                    return
                with nc.Block() as blk:
                    @blk.vector
                    def _(e):
                        vv.inc(e.tensor_copy(dlo_bf[:], pX[0:64, :]))
                        vv.inc(e.tensor_copy(bmcm[:], pX[64:96, :]))

                    @blk.sync
                    def _(e):
                        self.vv.wait(e)
                        so.inc(e.dma_start(out=self.bmcm_s, in_=bmcm[:]), 16)

            if "delta" not in bparts:
                return
            with ExitStack() as pctx:
                pD = pctx.enter_context(nc.psum_tensor("pD", [128, L], F32))
                w_dt_bf = ctx.enter_context(
                    nc.sbuf_tensor("bwdt", [64, DI], BF16))
                dl1 = ctx.enter_context(nc.sbuf_tensor("bdl1", [128, L], F32))
                dl2 = ctx.enter_context(nc.sbuf_tensor("bdl2", [128, L], F32))
                dl3 = ctx.enter_context(nc.sbuf_tensor("bdl3", [128, L], BF16))

                with nc.Block() as blk:
                    @blk.gpsimd
                    def _(e):
                        gd.inc(e.dma_start(out=w_dt_bf[:], in_=self.w_dt), 16)

                for t in range(16):
                    gd_t = self.gd.n
                    with nc.Block() as blk:
                        @blk.tensor
                        def _(e, t=t, gd_t=gd_t):
                            if t == 0:
                                e.wait_ge(self.gd.h, gd_t)
                            for cc in range(8):
                                ins = e.matmul(
                                    pD[:, cc * TC:(cc + 1) * TC],
                                    w_dt_bf[:, t * 128:(t + 1) * 128],
                                    dlo_bf[:, cc * TC:(cc + 1) * TC],
                                    start=True, stop=True)
                            mm.inc(ins)

                    if "delta_sp" not in bparts:
                        continue
                    with nc.Block() as blk:
                        so_t = self.so.n

                        @blk.scalar
                        def _(e, t=t):
                            for cc in range(8):
                                aa.inc(e.activation(
                                    dl1[:, cc * TC:(cc + 1) * TC],
                                    pD[:, cc * TC:(cc + 1) * TC], AF.Abs,
                                    bias=self.dtb_sb[:, t:t + 1]))
                            e.drain()
                            aa.inc(e.activation(dl1[:], dl1[:], AF.Exp,
                                                scale=self.cst[:, 4:5]))
                            e.drain()
                            aa.inc(e.activation(dl1[:], dl1[:], AF.Ln, bias=self.cst[:, 5:6]))

                        @blk.vector
                        def _(e, t=t, so_t=so_t):
                            for cc in range(8):
                                vv.inc(e.tensor_scalar(
                                    dl2[:, cc * TC:(cc + 1) * TC],
                                    pD[:, cc * TC:(cc + 1) * TC],
                                    self.dtb_sb[:, t:t + 1], 0.0,
                                    ALU.add, ALU.max))
                            self.aa.wait(e)
                            e.wait_ge(self.so.h, so_t)
                            e.drain()
                            vv.inc(e.tensor_add(dl3[:], dl2[:], dl1[:]))

                        @blk.sync
                        def _(e, t=t):
                            self.vv.wait(e)
                            so.inc(e.dma_start(
                                out=self.dltT_s[t * 128:(t + 1) * 128, :],
                                in_=dl3[:]), 16)

    def ph_s(self):
        nc, ld, gd, mm, vv, aa, so = (self.nc, self.ld, self.gd, self.mm,
                                      self.vv, self.aa, self.so)
        self._phase_fence()
        with ExitStack() as ctx:
            sb = lambda n, s, d=F32: ctx.enter_context(nc.sbuf_tensor(n, s, d))
            dlt = sb("sdlt", [128, 16 * TC], BF16)
            xch = sb("sxch", [128, 8 * TC])
            du = sb("sdu", [128, 16 * TC], BF16)
            Y = sb("sY", [128, 16 * TC])
            da = sb("sda", [128, 8 * TC])
            bca = sb("sbca", [128, 32 * TC], BF16)
            brow = sb("sbrow", [1, 2 * TC])
            ymbuf = sb("symbuf", [128, 4 * TC])
            stbf = sb("sstbf", [128, 8 * TC], BF16)
            hstate = sb("shstate", [128, 256])
            yg = sb("syg", [128, 16 * TC], BF16)
            wo = [sb(f"swo{i}", [128, 1024], BF16) for i in range(2)]
            xntb = sb("sxntb", [128, 8 * TC])
            sout = sb("ssout", [128, 8 * TC])
            pA = ctx.enter_context(nc.psum_tensor("pSA", [128, 8 * TC], F32))

            with nc.Block() as blk:
                @blk.vector
                def _(e):
                    vv.inc(e.memset(hstate[:], 0.0))

            for c in range(NCH):
                ld_t = self.ld.n
                with nc.Block() as blk:
                    @blk.sync
                    def _(e, c=c):
                        for h in range(2):
                            ld.inc(e.dma_start(
                                out=dlt[:, h * 8 * TC:(h + 1) * 8 * TC]
                                .rearrange("p (a t) -> p a t", t=TC),
                                in_=self.dltT_s[h * 1024:(h + 1) * 1024,
                                                c * TC:(c + 1) * TC]
                                .rearrange("(a p) t -> p a t", p=128)), 16)
                        ld.inc(e.dma_start(
                            out=xch[:].rearrange("p (a t) -> p a t", t=TC),
                            in_=self.xcT_s[0:1024, c * TC:(c + 1) * TC]
                            .rearrange("(a p) t -> p a t", p=128)), 16)
                        ld.inc(e.dma_start(
                            out=xntb[:].rearrange("p (a t) -> p a t", t=TC),
                            in_=self.xnT_s[:, c * TC:(c + 1) * TC]
                            .rearrange("(a p) t -> p a t", p=128)), 16)

                    @blk.vector
                    def _(e, c=c):
                        self.ld.wait(e)
                        vv.inc(e.memset(Y[:], 0.0))
                        vv.inc(e.tensor_mul(du[:, 0:8 * TC], dlt[:, 0:8 * TC],
                                            xch[:]))

                with nc.Block() as blk:
                    @blk.sync
                    def _(e, c=c):
                        ld.inc(e.dma_start(
                            out=xch[:].rearrange("p (a t) -> p a t", t=TC),
                            in_=self.xcT_s[1024:2048, c * TC:(c + 1) * TC]
                            .rearrange("(a p) t -> p a t", p=128)), 16)

                    @blk.vector
                    def _(e):
                        self.ld.wait(e)
                        vv.inc(e.tensor_mul(du[:, 8 * TC:16 * TC],
                                            dlt[:, 8 * TC:16 * TC], xch[:]))

                with nc.Block() as blk:
                    mm0, vv0 = self.mm.n, self.vv.n
                    br_tgt = {}

                    @blk.sync
                    def _(e, c=c, mm0=mm0):
                        for g in range(16):
                            if g >= 1:
                                e.wait_ge(self.mm.h, mm0 + g)
                            s = self.brs[g % 2]
                            for i in range(2):
                                r = (g % 8) * 2 + i + (16 if g >= 8 else 0)
                                s.inc(e.dma_start(
                                    out=brow[0:1, i * TC:(i + 1) * TC],
                                    in_=self.bmcm_s[r:r + 1,
                                                    c * TC:(c + 1) * TC]), 16)
                            br_tgt[g] = s.n

                    @blk.tensor
                    def _(e, mm0=mm0, vv0=vv0):
                        for g in range(16):
                            e.wait_ge(self.brs[g % 2].h, br_tgt[g])
                            if g >= 2:
                                e.wait_ge(self.vv.h, vv0 + g - 1)
                            half = (g % 2) * 2
                            for i in range(2):
                                ins = e.matmul(
                                    pA[:, (half + i) * TC:(half + i + 1) * TC],
                                    self.ones_sb[0:1, :],
                                    brow[0:1, i * TC:(i + 1) * TC],
                                    start=True, stop=True)
                            mm.inc(ins)

                    @blk.vector
                    def _(e, mm0=mm0):
                        for g in range(16):
                            e.wait_ge(self.mm.h, mm0 + g + 1)
                            half = (g % 2) * 2
                            for i in range(2):
                                slot = (g % 8) * 2 + i + (16 if g >= 8 else 0)
                                ins = e.tensor_copy(
                                    bca[:, slot * TC:(slot + 1) * TC],
                                    pA[:, (half + i) * TC:(half + i + 1) * TC])
                            vv.inc(ins)

                with nc.Block() as blk:
                    aa0, vv0 = self.aa.n, self.vv.n
                    nbatch = NST * 4

                    @blk.scalar
                    def _(e, aa0=aa0, vv0=vv0):
                        for b in range(nbatch):
                            n, q = b // 4, b % 4
                            if b >= 2:
                                e.wait_ge(self.vv.h, vv0 + b - 1)
                            sl = (b % 2) * 4
                            for i in range(4):
                                t = q * 4 + i
                                ins = e.activation(
                                    da[:, (sl + i) * TC:(sl + i + 1) * TC],
                                    dlt[:, t * TC:(t + 1) * TC], AF.Exp,
                                    scale=self.a_sb[:, t * NST + n:
                                                    t * NST + n + 1])
                            aa.inc(ins)

                    pp0 = self.pp.n

                    @blk.vector
                    def _(e, aa0=aa0, pp0=pp0):
                        for db in range(nbatch // 2):
                            bs = (2 * db, 2 * db + 1)
                            e.wait_ge(self.aa.h, aa0 + 2 * db + 2)
                            for h, b in enumerate(bs):
                                n, q = b // 4, b % 4
                                for i in range(4):
                                    t = q * 4 + i
                                    e.tensor_mul(
                                        stbf[:, (h * 4 + i) * TC:
                                             (h * 4 + i + 1) * TC],
                                        du[:, t * TC:(t + 1) * TC],
                                        bca[:, n * TC:(n + 1) * TC])
                            e.drain()
                            for h, b in enumerate(bs):
                                n, q = b // 4, b % 4
                                sl = (b % 2) * 4
                                for i in range(4):
                                    idx = n * 16 + q * 4 + i
                                    e.tensor_tensor_scan(
                                        xch[:, (h * 4 + i) * TC:
                                            (h * 4 + i + 1) * TC],
                                        da[:, (sl + i) * TC:(sl + i + 1) * TC],
                                        stbf[:, (h * 4 + i) * TC:
                                             (h * 4 + i + 1) * TC],
                                        hstate[:, idx:idx + 1],
                                        ALU.mult, ALU.add)
                            e.drain()
                            for h, b in enumerate(bs):
                                n, q = b // 4, b % 4
                                t0 = q * 4
                                e.tensor_copy(
                                    hstate[:, n * 16 + t0:n * 16 + t0 + 4],
                                    xch[:, (h * 4 + 1) * TC - 1:
                                        (h * 4 + 4) * TC:TC])
                                if b >= 1:
                                    e.wait_ge(self.pp.h, pp0 + b)
                                for i in range(4):
                                    ins = e.tensor_mul(
                                        ymbuf[:, i * TC:(i + 1) * TC],
                                        xch[:, (h * 4 + i) * TC:
                                            (h * 4 + i + 1) * TC],
                                        bca[:, (16 + n) * TC:(17 + n) * TC])
                                vv.inc(ins)

                    @blk.gpsimd
                    def _(e, pp0=pp0):
                        vvb = self.vv.n - nbatch
                        for b in range(nbatch):
                            n, q = b // 4, b % 4
                            t0 = q * 4
                            e.wait_ge(self.vv.h, vvb + b + 1)
                            for i in range(4):
                                t = t0 + i
                                ins = e.tensor_add(
                                    Y[:, t * TC:(t + 1) * TC],
                                    Y[:, t * TC:(t + 1) * TC],
                                    ymbuf[:, i * TC:(i + 1) * TC])
                            self.pp.inc(ins)

                with nc.Block() as blk:
                    @blk.sync
                    def _(e, c=c):
                        for h in range(2):
                            ld.inc(e.dma_start(
                                out=bca[:, h * 8 * TC:(h + 1) * 8 * TC]
                                .rearrange("p (a t) -> p a t", t=TC),
                                in_=self.xcdT_s[h * 1024:(h + 1) * 1024,
                                                c * TC:(c + 1) * TC]
                                .rearrange("(a p) t -> p a t", p=128)), 16)
                            ld.inc(e.dma_start(
                                out=bca[:, (16 + h * 8) * TC:(16 + (h + 1) * 8) * TC]
                                .rearrange("p (a t) -> p a t", t=TC),
                                in_=self.szT_s[h * 1024:(h + 1) * 1024,
                                               c * TC:(c + 1) * TC]
                                .rearrange("(a p) t -> p a t", p=128)), 16)

                with nc.Block() as blk:
                    @blk.vector
                    def _(e, c=c):
                        self.ld.wait(e)
                        self.pp.wait(e)
                        for grp in range(4):
                            if grp:
                                e.drain()
                            for i in range(4):
                                t = grp * 4 + i
                                e.tensor_add(ymbuf[:, i * TC:(i + 1) * TC],
                                             bca[:, t * TC:(t + 1) * TC],
                                             Y[:, t * TC:(t + 1) * TC])
                            e.drain()
                            for i in range(4):
                                t = grp * 4 + i
                                ins = e.tensor_mul(
                                    yg[:, t * TC:(t + 1) * TC],
                                    ymbuf[:, i * TC:(i + 1) * TC],
                                    bca[:, (16 + t) * TC:(17 + t) * TC])
                        vv.inc(ins)

                with nc.Block() as blk:
                    mm0 = self.mm.n
                    wo_tgt = {}

                    @blk.gpsimd
                    def _(e, mm0=mm0):
                        for t in range(16):
                            if t >= 2:
                                e.wait_ge(self.mm.h, mm0 + t - 1)
                            self.wos[t % 2].inc(e.dma_start(
                                out=wo[t % 2][:],
                                in_=self.w_out[t * 128:(t + 1) * 128, :]), 16)
                            wo_tgt[t] = self.wos[t % 2].n

                    @blk.tensor
                    def _(e):
                        for t in range(16):
                            e.wait_ge(self.wos[t % 2].h, wo_tgt[t])
                            for dm in range(8):
                                ins = e.matmul(
                                    pA[:, dm * TC:(dm + 1) * TC],
                                    wo[t % 2][:, dm * 128:(dm + 1) * 128],
                                    yg[:, t * TC:(t + 1) * TC],
                                    start=(t == 0), stop=(t == 15))
                            mm.inc(ins)

                with nc.Block() as blk:
                    so_t = self.so.n

                    @blk.vector
                    def _(e, so_t=so_t):
                        e.wait_ge(self.so.h, so_t)
                        for dm in range(8):
                            vv.inc(e.tensor_add(
                                sout[:, dm * TC:(dm + 1) * TC],
                                pA[:, dm * TC:(dm + 1) * TC],
                                xntb[:, dm * TC:(dm + 1) * TC]))

                    @blk.sync
                    def _(e, c=c):
                        self.vv.wait(e)
                        so.inc(e.dma_start(
                            out=self.sT_s[:, c * TC:(c + 1) * TC]
                            .rearrange("(a p) t -> p a t", p=128),
                            in_=sout[:].rearrange("p (a t) -> p a t", t=TC)), 16)

    def ph_moe(self):
        nc, ld, gd, mm, vv, aa, so = (self.nc, self.ld, self.gd, self.mm,
                                      self.vv, self.aa, self.so)
        self._phase_fence()
        with ExitStack() as ctx:
            sb = lambda n, s, d=F32: ctx.enter_context(nc.sbuf_tensor(n, s, d))
            xnt_own = sb("mxnt", [128, 8 * TOK], BF16)
            wgsb = sb("mwg", [128, 64], BF16)
            bgsb = sb("mbg", [128, 8])
            lg = sb("mlg", [128, 16 * 8])
            lm = sb("mlm", [128, 16 * 8])
            msk = sb("mmsk", [128, 16 * 8])
            wgt = sb("mwgt", [128, 16 * 8])
            gm = sb("mgm", [128, 16])
            gm2 = sb("mgm2", [128, 16])
            gw1 = sb("mgw1", [128, 16])
            gw2 = sb("mgw2", [128, 16])
            gtd = sb("mgtd", [128, 16])
            wtT = sb("mwtT", [8, TOK])
            werow = sb("mwerow", [1, TOK])
            wcast = sb("mwcast", [128, TOK], BF16)
            xse = sb("mxse", [128, 8 * 1024], BF16)
            w1q = [sb(f"mw1q{i}", [128, 8 * 512], BF16) for i in range(2)]
            w3q = [sb(f"mw3q{i}", [128, 8 * 512], BF16) for i in range(2)]
            h1 = sb("mh1", [128, 64 * 512], BF16)
            w2kt = [sb(f"mw2kt{i}", [128, 2 * 512], BF16) for i in range(3)]
            silb = sb("msilb", [128, 4 * 512], BF16)
            gio = sb("mgio", [128, 2 * 512])
            gout = sb("mgout", [128, 8 * 512])
            pM = ctx.enter_context(nc.psum_tensor("pM", [128, 8 * TC], F32))

            with nc.Block() as blk:
                @blk.sync
                def _(e):
                    for dm in range(8):
                        ld.inc(e.dma_start(
                            out=xnt_own[:, dm * TOK:(dm + 1) * TOK],
                            in_=self.xnTo_s[dm * 128:(dm + 1) * 128, :]), 16)
                    ld.inc(e.dma_start(out=bgsb[:], in_=self.bg_bc), 16)

                @blk.gpsimd
                def _(e):
                    gd.inc(e.dma_start(
                        out=wgsb[:].rearrange("p (a c) -> p a c", c=8),
                        in_=self.wg.rearrange("(a p) c -> p a c", p=128)), 16)

            with nc.Block() as blk:
                @blk.tensor
                def _(e):
                    self.ld.wait(e)
                    self.gd.wait(e)
                    for tt in range(16):
                        for k in range(8):
                            ins = e.matmul(
                                pM[:, tt * 8:(tt + 1) * 8],
                                xnt_own[:, k * TOK + tt * 128:
                                        k * TOK + (tt + 1) * 128],
                                wgsb[:, k * 8:(k + 1) * 8],
                                start=(k == 0), stop=(k == 7))
                        mm.inc(ins)

            with nc.Block() as blk:
                @blk.vector
                def _(e):
                    for tt in range(16):
                        vv.inc(e.tensor_add(lg[:, tt * 8:(tt + 1) * 8],
                                            pM[:, tt * 8:(tt + 1) * 8], bgsb[:]))
                    e.drain()
                    for tt in range(16):
                        vv.inc(e.reduce_max(gm[:, tt:tt + 1],
                                            lg[:, tt * 8:(tt + 1) * 8], axis=AX.X))
                    e.drain()
                    for tt in range(16):
                        vv.inc(e.tensor_scalar(msk[:, tt * 8:(tt + 1) * 8],
                                               lg[:, tt * 8:(tt + 1) * 8],
                                               gm[:, tt:tt + 1], None,
                                               ALU.is_equal))
                    e.drain()
                    for tt in range(16):
                        vv.inc(e.scalar_tensor_tensor(
                            lm[:, tt * 8:(tt + 1) * 8],
                            msk[:, tt * 8:(tt + 1) * 8], -1e30,
                            lg[:, tt * 8:(tt + 1) * 8], ALU.mult, ALU.add))
                    e.drain()
                    for tt in range(16):
                        vv.inc(e.reduce_max(gm2[:, tt:tt + 1],
                                            lm[:, tt * 8:(tt + 1) * 8], axis=AX.X))
                    e.drain()
                    vv.inc(e.tensor_sub(gtd[:], gm[:], gm2[:]))

            with nc.Block() as blk:
                @blk.scalar
                def _(e):
                    aa.inc(e.activation(gtd[:], gtd[:], AF.Tanh, scale=self.cst[:, 6:7]))

            with nc.Block() as blk:
                @blk.vector
                def _(e):
                    vv.inc(e.tensor_scalar(gw1[:], gtd[:], 0.5, 0.5,
                                           ALU.mult, ALU.add))
                    for tt in range(16):
                        vv.inc(e.tensor_scalar(wgt[:, tt * 8:(tt + 1) * 8],
                                               lm[:, tt * 8:(tt + 1) * 8],
                                               gm2[:, tt:tt + 1], None,
                                               ALU.is_equal))
                    e.drain()
                    vv.inc(e.tensor_scalar(gw2[:], gw1[:], -1.0, 1.0,
                                           ALU.mult, ALU.add))
                    for tt in range(16):
                        vv.inc(e.tensor_scalar_mul(msk[:, tt * 8:(tt + 1) * 8],
                                                   msk[:, tt * 8:(tt + 1) * 8],
                                                   gw1[:, tt:tt + 1]))
                    e.drain()
                    for tt in range(16):
                        vv.inc(e.scalar_tensor_tensor(
                            wgt[:, tt * 8:(tt + 1) * 8],
                            wgt[:, tt * 8:(tt + 1) * 8],
                            gw2[:, tt:tt + 1],
                            msk[:, tt * 8:(tt + 1) * 8],
                            ALU.mult, ALU.add))

            with nc.Block() as blk:
                @blk.tensor
                def _(e):
                    for tt in range(16):
                        ins = e.transpose(pM[0:8, tt * 128:(tt + 1) * 128],
                                          wgt[:, tt * 8:(tt + 1) * 8],
                                          self.ident_sb[:])
                    mm.inc(ins)

            with nc.Block() as blk:
                @blk.vector
                def _(e):
                    vv.inc(e.tensor_copy(wtT[:], pM[0:8, 0:TOK]))

                @blk.sync
                def _(e):
                    self.vv.wait(e)
                    so.inc(e.dma_start(out=self.wt_s, in_=wtT[:]), 16)

            self._phase_fence()

            for ex in range(E):
                with nc.Block() as blk:
                    @blk.sync
                    def _(e, ex=ex):
                        ld.inc(e.dma_start(out=werow[:],
                                           in_=self.wt_s[ex:ex + 1, :]), 16)

                    @blk.tensor
                    def _(e):
                        self.ld.wait(e)
                        for i in range(4):
                            ins = e.matmul(pM[:, i * TC:(i + 1) * TC],
                                           self.ones_sb[0:1, :],
                                           werow[0:1, i * TC:(i + 1) * TC],
                                           start=True, stop=True)
                        mm.inc(ins)

                    @blk.vector
                    def _(e):
                        self.mm.wait(e)
                        for i in range(4):
                            ins = e.tensor_copy(wcast[:, i * TC:(i + 1) * TC],
                                                pM[:, i * TC:(i + 1) * TC])
                        vv.inc(ins)

                if getattr(self, "gsnap", 0) and ex > 0:
                    with nc.Block() as blk:
                        @blk.sync
                        def _(e, ex=ex):
                            self.so.wait(e)
                            for dm in range(8):
                                ld.inc(e.dma_start(
                                    out=gio[:, 0:512],
                                    in_=self.g_s[dm * 128:(dm + 1) * 128, 0:512]), 16)
                                self.ld.wait(e)
                                so.inc(e.dma_start(
                                    out=self.gsnap_s[(ex - 1) * D + dm * 128:
                                                     (ex - 1) * D + (dm + 1) * 128,
                                                     0:512],
                                    in_=gio[:, 0:512]), 16)
                            self.so.wait(e)

                for p in range(2):
                    with nc.Block() as blk:
                        gd0, mm0, vv0, aa0 = (self.gd.n, self.mm.n,
                                              self.vv.n, self.aa.n)

                        wq_tgt = {}

                        @blk.gpsimd
                        def _(e, ex=ex, p=p, mm0=mm0):
                            for q in range(8):
                                if q >= 2:
                                    e.wait_ge(self.mm.h, mm0 + (q - 1) * 4)
                                s = self.wqs[q % 2]
                                s.inc(e.dma_start(
                                    out=w1q[q % 2][:].rearrange(
                                        "p (a c) -> p a c", c=512),
                                    in_=self.w1[ex, :, q * 512:(q + 1) * 512]
                                    .rearrange("(a p) c -> p a c", p=128)), 16)
                                s.inc(e.dma_start(
                                    out=w3q[q % 2][:].rearrange(
                                        "p (a c) -> p a c", c=512),
                                    in_=self.w3[ex, :, q * 512:(q + 1) * 512]
                                    .rearrange("(a p) c -> p a c", p=128)), 16)
                                wq_tgt[q] = s.n

                        @blk.vector
                        def _(e, ex=ex, p=p, mm0=mm0, aa0=aa0):
                            for k in range(8):
                                ins = e.tensor_mul(
                                    xse[:, k * 1024:(k + 1) * 1024],
                                    xnt_own[:, k * TOK + p * 1024:
                                            k * TOK + (p + 1) * 1024],
                                    wcast[:, p * 1024:(p + 1) * 1024])
                            vv.inc(ins)
                            for s in range(32):
                                e.wait_ge(self.aa.h, aa0 + s + 1)
                                e.wait_ge(self.mm.h, mm0 + s + 1)
                                base = (s % 2) * 4
                                slb = (s % 2) * 2
                                for tc in range(2):
                                    ins = e.tensor_mul(
                                        h1[:, (s * 2 + tc) * 512:
                                           (s * 2 + tc + 1) * 512],
                                        silb[:, (slb + tc) * 512:
                                             (slb + tc + 1) * 512],
                                        pM[:, (base + 2 + tc) * 512:
                                           (base + 3 + tc) * 512])
                                vv.inc(ins)

                        @blk.scalar
                        def _(e, mm0=mm0, vv0=vv0, aa0=aa0):
                            ser = getattr(self, "p1ser", 0)
                            for s in range(32):
                                e.wait_ge(self.mm.h, mm0 + s + 1)
                                if s >= 2 - ser:
                                    e.wait_ge(self.vv.h, vv0 + s + ser)
                                base = (s % 2) * 4
                                slb = (s % 2) * 2
                                for tc in range(2):
                                    ins = e.activation(
                                        silb[:, (slb + tc) * 512:
                                             (slb + tc + 1) * 512],
                                        pM[:, (base + tc) * 512:
                                           (base + tc + 1) * 512], AF.Silu)
                                aa.inc(ins)

                        @blk.tensor
                        def _(e, gd0=gd0, vv0=vv0):
                            ser = getattr(self, "p1ser", 0)
                            for s in range(32):
                                q, hl = s // 4, s % 4
                                if s == 0:
                                    e.wait_ge(self.vv.h, vv0 + 1)
                                if hl == 0:
                                    e.wait_ge(self.wqs[q % 2].h, wq_tgt[q])
                                if s >= 2 - ser:
                                    e.wait_ge(self.vv.h, vv0 + s + ser)
                                base = (s % 2) * 4
                                for wi, wq in ((0, w1q[q % 2]), (1, w3q[q % 2])):
                                    for tc in range(2):
                                        for k in range(8):
                                            ins = e.matmul(
                                                pM[:, (base + wi * 2 + tc) * 512:
                                                   (base + wi * 2 + tc + 1) * 512],
                                                wq[:, k * 512 + hl * 128:
                                                   k * 512 + (hl + 1) * 128],
                                                xse[:, k * 1024 + tc * 512:
                                                    k * 1024 + (tc + 1) * 512],
                                                start=(k == 0), stop=(k == 7))
                                mm.inc(ins)

                    if ex == 0 and p == 0 and getattr(self, "moedbg", 0):
                        with nc.Block() as blk:
                            @blk.sync
                            def _(e):
                                self.gd.wait(e)
                                so.inc(e.dma_start(out=self.xse_s, in_=xse[:]), 16)
                                so.inc(e.dma_start(out=self.h1_s, in_=h1[:]), 16)
                                so.inc(e.dma_start(out=self.w1q_s, in_=w1q[1][:]), 16)
                                self.so.wait(e)

                    with nc.Block() as blk:
                        gd0, mm0, vv0, ld0, so0 = (self.gd.n, self.mm.n,
                                                   self.vv.n, self.ld.n,
                                                   self.so.n)

                        w2_tgt = {}

                        @blk.gpsimd
                        def _(e, ex=ex, mm0=mm0):
                            for dmh in range(2):
                                for hp in range(16):
                                    li = dmh * 16 + hp
                                    if li >= 3:
                                        e.wait_ge(self.mm.h, mm0 + (li - 2) * 2)
                                    s = self.w2s[li % 3]
                                    s.inc(e.dma_start(
                                        out=w2kt[li % 3][:].rearrange(
                                            "p (a c) -> p a c", c=512),
                                        in_=self.w2[ex, hp * 256:(hp + 1) * 256,
                                                    dmh * 512:(dmh + 1) * 512]
                                        .rearrange("(a p) c -> p a c", p=128)), 16)
                                    w2_tgt[li] = s.n

                        gi_tgt = {}

                        @blk.sync
                        def _(e, p=p, ex=ex, vv0=vv0, so0=so0):
                            e.wait_ge(self.so.h, so0)
                            for dmh in range(2):
                                for tc in range(2):
                                    for dm4 in range(4):
                                        va = dmh * 8 + tc * 4 + dm4
                                        if va >= 2:
                                            e.wait_ge(self.vv.h, vv0 + va - 1)
                                        s = self.gis[va % 2]
                                        s.inc(e.dma_start(
                                            out=gio[:, (va % 2) * 512:
                                                    (va % 2 + 1) * 512],
                                            in_=self.g_s[
                                                (dmh * 4 + dm4) * 128:
                                                (dmh * 4 + dm4 + 1) * 128,
                                                p * 1024 + tc * 512:
                                                p * 1024 + (tc + 1) * 512]), 16)
                                        gi_tgt[va] = s.n
                                e.wait_ge(self.vv.h, vv0 + (dmh + 1) * 8)
                                for tc in range(2):
                                    for dm4 in range(4):
                                        so.inc(e.dma_start(
                                            out=self.g_s[
                                                (dmh * 4 + dm4) * 128:
                                                (dmh * 4 + dm4 + 1) * 128,
                                                p * 1024 + tc * 512:
                                                p * 1024 + (tc + 1) * 512],
                                            in_=gout[:, (tc * 4 + dm4) * 512:
                                                     (tc * 4 + dm4 + 1) * 512]),
                                            16)

                        @blk.vector
                        def _(e, mm0=mm0, so0=so0):
                            for dmh in range(2):
                                e.wait_ge(self.mm.h, mm0 + (dmh + 1) * 32)
                                if dmh == 1:
                                    e.wait_ge(self.so.h, so0 + 8 * 16)
                                for tc in range(2):
                                    for dm4 in range(4):
                                        va = dmh * 8 + tc * 4 + dm4
                                        e.wait_ge(self.gis[va % 2].h, gi_tgt[va])
                                        sl = (tc * 4 + dm4) * 512
                                        vv.inc(e.tensor_add(
                                            gout[:, sl:sl + 512],
                                            gio[:, (va % 2) * 512:
                                                (va % 2 + 1) * 512],
                                            pM[:, sl:sl + 512]))

                        @blk.tensor
                        def _(e, vv0=vv0):
                            for dmh in range(2):
                                if dmh == 1:
                                    e.wait_ge(self.vv.h, vv0 + 8)
                                for ht in range(32):
                                    hp, hi = ht // 2, ht % 2
                                    li = dmh * 16 + hp
                                    if hi == 0:
                                        e.wait_ge(self.w2s[li % 3].h, w2_tgt[li])
                                    for tc in range(2):
                                        for dm4 in range(4):
                                            sl = (tc * 4 + dm4) * 512
                                            ins = e.matmul(
                                                pM[:, sl:sl + 512],
                                                w2kt[li % 3][:, hi * 512 +
                                                             dm4 * 128:
                                                             hi * 512 +
                                                             (dm4 + 1) * 128],
                                                h1[:, (ht * 2 + tc) * 512:
                                                   (ht * 2 + tc + 1) * 512],
                                                start=(ht == 0), stop=(ht == 31))
                                    mm.inc(ins)

    def ph_f(self):
        nc, ld, mm, vv, aa, so = (self.nc, self.ld, self.mm, self.vv,
                                  self.aa, self.so)
        self._phase_fence()
        with ExitStack() as ctx:
            sb = lambda n, s, d=F32: ctx.enter_context(nc.sbuf_tensor(n, s, d))
            gbuf = sb("fg", [128, 8 * 512])
            sb0 = sb("fs0", [128, 8 * 512])
            sb1 = sb("fs1", [128, 8 * 512])
            xf = sb("fx", [128, 4 * 1024])
            gsb = sb("fgs", [128, 8 * 512])
            gtok = sb("fgt", [128, 4 * 1024])
            tmpb = sb("ftmp", [128, 4 * 1024])
            outb = sb("fout", [128, 4 * 1024])
            pF = ctx.enter_context(nc.psum_tensor("pF", [128, 4096], F32))

            for oc in range(4):
                with nc.Block() as blk:
                    @blk.sync
                    def _(e, oc=oc):
                        ld.inc(e.dma_start(
                            out=gbuf[:].rearrange("p (a t) -> p a t", t=512),
                            in_=self.g_s[:, oc * 512:(oc + 1) * 512]
                            .rearrange("(a p) t -> p a t", p=128)), 16)
                        ld.inc(e.dma_start(
                            out=sb0[:].rearrange("p (a t) -> p a t", t=512),
                            in_=self.sT_s[:, oc * 512:(oc + 1) * 512]
                            .rearrange("(a p) t -> p a t", p=128)), 16)
                        ld.inc(e.dma_start(
                            out=sb1[:].rearrange("p (a t) -> p a t", t=512),
                            in_=self.sT_s[:, TOK + oc * 512:TOK + (oc + 1) * 512]
                            .rearrange("(a p) t -> p a t", p=128)), 16)
                        for tt in range(4):
                            ld.inc(e.dma_start(
                                out=xf[:, tt * 1024:(tt + 1) * 1024],
                                in_=self.x_own[oc * 512 + tt * 128:
                                               oc * 512 + (tt + 1) * 128, :]), 16)

                with nc.Block() as blk:
                    @blk.scalar
                    def _(e):
                        self.ld.wait(e)
                        aa.inc(e.activation(gsb[:], gbuf[:], AF.Silu))

                with nc.Block() as blk:
                    @blk.tensor
                    def _(e):
                        for dm in range(8):
                            for tt in range(4):
                                ins = e.transpose(
                                    pF[:, tt * 1024 + dm * 128:
                                       tt * 1024 + (dm + 1) * 128],
                                    gsb[:, dm * 512 + tt * 128:
                                        dm * 512 + (tt + 1) * 128],
                                    self.ident_sb[:])
                        mm.inc(ins)

                with nc.Block() as blk:
                    @blk.vector
                    def _(e):
                        for tt in range(4):
                            vv.inc(e.tensor_copy(
                                gtok[:, tt * 1024:(tt + 1) * 1024],
                                pF[:, tt * 1024:(tt + 1) * 1024]))

                for cand, sbc, outd in ((0, sb0, self.out_c0),
                                        (1, sb1, self.out_c1)):
                    with nc.Block() as blk:
                        @blk.tensor
                        def _(e, sbc=sbc):
                            for dm in range(8):
                                for tt in range(4):
                                    ins = e.transpose(
                                        pF[:, tt * 1024 + dm * 128:
                                           tt * 1024 + (dm + 1) * 128],
                                        sbc[:, dm * 512 + tt * 128:
                                            dm * 512 + (tt + 1) * 128],
                                        self.ident_sb[:])
                            mm.inc(ins)

                    with nc.Block() as blk:
                        so_t = self.so.n

                        @blk.vector
                        def _(e, so_t=so_t):
                            e.wait_ge(self.so.h, so_t)
                            for tt in range(4):
                                e.tensor_mul(tmpb[:, tt * 1024:(tt + 1) * 1024],
                                             pF[:, tt * 1024:(tt + 1) * 1024],
                                             gtok[:, tt * 1024:(tt + 1) * 1024])
                            e.drain()
                            for tt in range(4):
                                vv.inc(e.tensor_add(
                                    outb[:, tt * 1024:(tt + 1) * 1024],
                                    tmpb[:, tt * 1024:(tt + 1) * 1024],
                                    xf[:, tt * 1024:(tt + 1) * 1024]))

                        @blk.sync
                        def _(e, oc=oc, outd=outd):
                            self.vv.wait(e)
                            for tt in range(4):
                                so.inc(e.dma_start(
                                    out=outd[oc * 512 + tt * 128:
                                             oc * 512 + (tt + 1) * 128, :],
                                    in_=outb[:, tt * 1024:(tt + 1) * 1024]), 16)

    def ph_fini(self):
        nc = self.nc
        if self.dbg is not None:
            for di, (dap, name, rows, cols) in enumerate(self.dbg):
                src = getattr(self, name)
                with ExitStack() as ctx:
                    dbuf = ctx.enter_context(
                        nc.sbuf_tensor(f"dbgbuf{di}", [128, cols], src.dtype))
                    for rb in range((rows + 127) // 128):
                        r0 = rb * 128
                        r1 = min(rows, r0 + 128)
                        nr = r1 - r0
                        with nc.Block() as blk:
                            @blk.sync
                            def _(e, r0=r0, nr=nr):
                                self.so.wait(e)
                                self.ld.inc(e.dma_start(
                                    out=dbuf[0:nr, :],
                                    in_=src[r0:r0 + nr, 0:cols]), 16)
                                self.ld.wait(e)
                                self.so.inc(e.dma_start(
                                    out=dap[r0:r0 + nr, :],
                                    in_=dbuf[0:nr, :]), 16)
        with nc.Block() as blk:
            @blk.sync
            def _(e):
                self.so.wait(e)


_NC_CACHE = {}


def _get_nc(upto=99):
    if upto not in _NC_CACHE:
        _NC_CACHE[upto] = MK(upto=upto).build()
    return _NC_CACHE[upto]


def _pcols(v):
    n = v.shape[0] // 128
    return np.ascontiguousarray(v.reshape(n, 128).T)


def _pcols2(m):
    n = m.shape[0] // 128
    c = m.shape[1]
    return np.ascontiguousarray(m.reshape(n, 128, c).transpose(1, 0, 2).reshape(128, n * c))


def make_in_maps(inputs):
    x = np.ascontiguousarray(inputs["L0_attn_x"], dtype=np.float32)
    base = dict(
        ln_g=_pcols(inputs["L0_attn_ln_g"]),
        ln_b=_pcols(inputs["L0_attn_ln_b"]),
        w_in=np.ascontiguousarray(inputs["L0_attn_W_in"]),
        conv_w=_pcols2(inputs["L0_attn_conv_w"].reshape(DI, 4)),
        conv_b=_pcols(inputs["L0_attn_conv_b"]),
        w_xp=np.ascontiguousarray(inputs["L0_attn_W_xp"]),
        w_dt=np.ascontiguousarray(inputs["L0_attn_W_dt"]),
        dt_bias=_pcols(inputs["L0_attn_dt_bias"]),
        a_log=_pcols2(inputs["L0_attn_A_log"]),
        d_skip=_pcols(inputs["L0_attn_D_skip"]),
        w_out=np.ascontiguousarray(inputs["L0_attn_W_out"]),
        wg=np.ascontiguousarray(inputs["L0_ffn_Wg"]),
        bg_bc=np.ascontiguousarray(
            np.broadcast_to(inputs["L0_ffn_bg"].reshape(1, E), (128, E))),
        w1=np.ascontiguousarray(inputs["L0_ffn_W1"]),
        w2=np.ascontiguousarray(inputs["L0_ffn_W2"]),
        w3=np.ascontiguousarray(inputs["L0_ffn_W3"]),
        ident=np.eye(128, dtype=np.float32),
        ones_f=np.ones((1, 128), dtype=np.float32),
    )
    in_maps = []
    for b in range(B):
        for j in range(2):
            m = dict(base)
            m["x_full"] = np.ascontiguousarray(x[b])
            m["x_own"] = np.ascontiguousarray(x[b, j * TOK:(j + 1) * TOK])
            in_maps.append(m)
    return in_maps


def kernel(**inputs):
    nc = _get_nc(99)
    in_maps = make_in_maps(inputs)
    res = run_bass_kernel_spmd(nc, in_maps, core_ids=list(range(8)))
    out = np.empty((B, L, D), np.float32)
    for b in range(B):
        for j in range(2):
            r = res.results[b * 2 + j]
            out[b, j * TOK:(j + 1) * TOK] = r["out_c0" if j == 0 else "out_c1"]
    return out
